# version 14
# speedup vs baseline: 1.1495x; 1.0806x over previous
import math
import contextlib
import numpy as np
import concourse.bass as bass
import concourse.mybir as mybir
from concourse.bass_utils import run_bass_kernel_spmd

F32 = mybir.dt.float32
F16 = mybir.dt.float16
I16 = mybir.dt.int16
AF = mybir.ActivationFunctionType
ALU = mybir.AluOpType
AX = mybir.AxisListType.X

C = 2048
H = 16
DH = 128
NF = 8192
TCORE = 1024
NCH = C // 128
NNCH = NF // 128
ISQD = 1.0 / math.sqrt(128.0)
LN127 = float(np.log(np.float32(127.0)))
EPS = 1e-5
ENGINES = ("tensor", "vector", "scalar", "gpsimd", "sync")

DBG = False
STAGE = 3


class KB:
    def __init__(self, nc):
        self.nc = nc
        self.cnt = {}
        self.sems = {}
        self.streams = {e: [] for e in ENGINES}
        self.lastwait = {e: {} for e in ENGINES}
        self._sem_stack = contextlib.ExitStack()

    def sem(self, name):
        if name not in self.sems:
            self.sems[name] = self._sem_stack.enter_context(self.nc.semaphore(name))
            self.cnt[name] = 0
        return self.sems[name]

    def emit(self, eng, fn, waits=(), inc=None, incn=1):
        ws = []
        for (s, c) in waits:
            if c is None or c <= 0:
                continue
            self.sem(s)
            if c > self.lastwait[eng].get(s, -1):
                ws.append((s, c))
                self.lastwait[eng][s] = c
        out = None
        if inc is not None:
            self.sem(inc)
            self.cnt[inc] += incn
            out = self.cnt[inc]
        self.streams[eng].append((fn, ws, inc, incn))
        return out

    def c(self, name):
        self.sem(name)
        return self.cnt[name]

    def barrier(self):
        snaps = {}
        for e in ENGINES:
            snaps[e] = self.emit(e, lambda eng: eng.nop(), inc="bar_" + e)
        for e in ENGINES:
            self.emit(e, lambda eng: eng.nop(),
                      waits=[("bar_" + o, snaps[o]) for o in ENGINES if o != e])

    def flush(self, block):
        streams = {e: list(self.streams[e]) for e in ENGINES}
        for e in ENGINES:
            self.streams[e] = []

        def mk(e):
            def body(eng):
                for (fn, ws, inc, incn) in streams[e]:
                    for (s, cv) in ws:
                        eng.wait_ge(self.sems[s], cv)
                    instr = fn(eng)
                    if inc is not None:
                        instr.then_inc(self.sems[inc], incn)
            return body

        block.tensor(mk("tensor"))
        block.vector(mk("vector"))
        block.scalar(mk("scalar"))
        block.gpsimd(mk("gpsimd"))
        block.sync(mk("sync"))


sA, sV, sG, sT, sL, sW, sCC, sO = "a", "v", "g", "t", "l", "w", "cc", "o"


def build_nc(stage=STAGE):
    nc = bass.Bass("TRN2", target_bir_lowering=False)
    K = KB(nc)

    xt = nc.dram_tensor("xt", [C, TCORE], F32, kind="ExternalInput").ap()
    w0t = nc.dram_tensor("w0t", [C, NF], F32, kind="ExternalInput").ap()
    w1t = nc.dram_tensor("w1t", [NF, C], F32, kind="ExternalInput").ap()
    b0m = nc.dram_tensor("b0m", [128, NNCH], F32, kind="ExternalInput").ap()
    b1m = nc.dram_tensor("b1m", [128, NCH], F32, kind="ExternalInput").ap()
    wqt = nc.dram_tensor("wqt", [C, C], F32, kind="ExternalInput").ap()
    wkt = nc.dram_tensor("wkt", [C, C], F32, kind="ExternalInput").ap()
    wvt = nc.dram_tensor("wvt", [C, C], F32, kind="ExternalInput").ap()
    wpt = nc.dram_tensor("wpt", [C, C], F32, kind="ExternalInput").ap()
    id32 = nc.dram_tensor("id32", [128, 128], F32, kind="ExternalInput").ap()
    id16 = nc.dram_tensor("id16", [128, 128], F16, kind="ExternalInput").ap()
    ones2 = nc.dram_tensor("ones2", [2, 128], F16, kind="ExternalInput").ap()
    ones1r = nc.dram_tensor("ones1r", [1, 128], F16, kind="ExternalInput").ap()
    onescol = nc.dram_tensor("onescol", [128, 1], F16, kind="ExternalInput").ap()
    ones32r = nc.dram_tensor("ones32r", [1, 128], F32, kind="ExternalInput").ap()
    ones32c = nc.dram_tensor("ones32c", [128, 1], F32, kind="ExternalInput").ap()
    invccr = nc.dram_tensor("invccr", [1, 128], F32, kind="ExternalInput").ap()
    yt = nc.dram_tensor("yt", [C, TCORE], F32, kind="ExternalOutput").ap()
    dbg_xn = nc.dram_tensor("dbg_xn", [C, TCORE], F16, kind="ExternalOutput").ap()
    dbg_d = nc.dram_tensor("dbg_d", [NF, 512], F16, kind="ExternalOutput").ap()
    dbg_q = nc.dram_tensor("dbg_q", [128, H * TCORE], F16, kind="ExternalOutput").ap()
    dbg_y = nc.dram_tensor("dbg_y", [128, H * TCORE], F16, kind="ExternalOutput").ap()
    dbg_sw = nc.dram_tensor("dbg_sw", [128, 8], F32, kind="ExternalOutput").ap()
    dbg_z = nc.dram_tensor("dbg_z", [128, 128], F32, kind="ExternalOutput").ap()
    dbg_kg = nc.dram_tensor("dbg_kg", [4 * C, TCORE], F16, kind="ExternalOutput").ap()
    dbg_sc = nc.dram_tensor("dbg_sc", [128, 4096], F32, kind="ExternalOutput").ap()
    dbg_ysc = nc.dram_tensor("dbg_ysc", [128, 512], F32, kind="ExternalOutput").ap()
    dbg_mrow = nc.dram_tensor("dbg_mrow", [2, 512], F16, kind="ExternalOutput").ap()
    dbg_w16 = nc.dram_tensor("dbg_w16", [128, 1024], F16, kind="ExternalOutput").ap()
    dbg_mw = nc.dram_tensor("dbg_mw", [128, 16], F32, kind="ExternalOutput").ap()
    w0f16 = nc.dram_tensor("w0f16", [C, NF], F16).ap()
    w1f16 = nc.dram_tensor("w1f16", [NF, C], F16).ap()
    wqq = nc.dram_tensor("wqq", [C, C], F16).ap()
    wpq = nc.dram_tensor("wpq", [C, C], F16).ap()
    cck_in = nc.dram_tensor("cck_in", [C, TCORE], F16)
    cck_out = nc.dram_tensor("cck_out", [4 * C, TCORE], F16)
    ccv_in = nc.dram_tensor("ccv_in", [TCORE, C], F16)
    ccv_out = nc.dram_tensor("ccv_out", [4 * TCORE, C], F16)

    st = contextlib.ExitStack()
    SB = lambda *a: st.enter_context(nc.sbuf_tensor(*a))
    _nmc = [0]

    def NM(b):
        _nmc[0] += 1
        return "%s_%d" % (b, _nmc[0])

    xres = SB("xres", [128, NCH * TCORE], F32)
    xn16 = SB("xn16", [128, NCH * TCORE], F16)
    bm16 = SB("bm16", [128, TCORE], F16)
    br16 = SB("br16", [128, TCORE], F16)
    b0sb = SB("b0sb", [128, NNCH], F32)
    b1sb = SB("b1sb", [128, NCH], F32)
    b1h = SB("b1h", [128, NCH], F32)
    id32sb = SB("id32sb", [128, 128], F32)
    id16sb = SB("id16sb", [128, 128], F16)
    ones2sb = SB("ones2sb", [2, 128], F16)
    ones1sb = SB("ones1sb", [1, 128], F16)
    onescolsb = SB("onescolsb", [128, 1], F16)
    ones32rsb = SB("ones32rsb", [1, 128], F32)
    ones32csb = SB("ones32csb", [128, 1], F32)
    invccrsb = SB("invccrsb", [1, 128], F32)
    asum = SB("asum", [128, 24], F32)
    swsc = SB("swsc", [1, 16], F32)
    swb = SB("swb", [128, 8], F32)
    axinv = SB("axinv", [128, 8], F32)
    zbuf = SB("zbuf", [128, 128], F32)
    z127i = SB("z127i", [128, 128], F32)
    mwork = SB("mwork", [128, 16], F32)
    mtile = SB("mtile", [128, 2], F16)
    mrow = SB("mrow", [2, 512], F16)
    uq = SB("uq", [128, 4], F32)
    mxq = SB("mxq", [128, 4], F32)
    pwork = SB("pwork", [128, 128], F32)
    ywork = SB("ywork", [128, 32], F32)
    ysc = SB("ysc", [128, 512], F32)
    ln127sb = SB("ln127sb", [128, 1], F32)

    def TS(stream, out, in0, s1, op0, s2=None, op1=None, accum=None, waits=(), inc=None):
        def fn(eng, o=out, i=in0, a=s1, b=s2, p0=op0, p1=op1, ac=accum):
            if p1 is None and ac is None:
                return eng.tensor_scalar(o, i, a, None, p0)
            return eng.tensor_scalar(o, i, a, b, p0, p1 if p1 is not None else ALU.bypass,
                                     accum_out=ac)
        return K.emit(stream, fn, waits=waits, inc=inc)

    def VW():
        return [(sV, K.c(sV))]

    def DMA(q, out, in_, waits=(), sem=None):
        if sem is None:
            sem = sW if q == "gpsimd" else sL
        return K.emit(q, lambda eng, o=out, i=in_: eng.dma_start(out=o, in_=i),
                      waits=waits, inc=sem, incn=16)

    def p0_loads():
        DMA("sync", xres[:, :].rearrange("p (cc t) -> p cc t", t=TCORE),
            xt.rearrange("(cc p) t -> p cc t", p=128))
        for (dst, src) in [(b0sb, b0m), (b1sb, b1m), (id32sb, id32), (id16sb, id16),
                           (ones2sb, ones2), (ones1sb, ones1r), (onescolsb, onescol),
                           (ones32rsb, ones32r), (ones32csb, ones32c), (invccrsb, invccr)]:
            DMA("sync", dst[:, :], src)
        ld = K.c(sL)
        TS("vector", b1h[:, :], b1sb[:, :], 0.5, ALU.mult, waits=[(sL, ld)], inc=sV)
        K.emit("vector", lambda eng: eng.memset(ln127sb[:, :], LN127), inc=sV)
        return ld

    def ln_stats(pstat, pbc, xsq, stat, statrow, ld):
        xsq_free = {}
        for cc in range(NCH):
            TS("vector", xn16[:, cc * TCORE:(cc + 1) * TCORE],
               xres[:, cc * TCORE:(cc + 1) * TCORE], 1.0, ALU.mult,
               waits=[(sL, ld), (sT, xsq_free.get(cc - 2, 0))], inc=sV)
            K.emit("vector", lambda eng, c=cc: eng.tensor_mul(
                xsq[:, (c % 2) * TCORE:(c % 2 + 1) * TCORE],
                xn16[:, c * TCORE:(c + 1) * TCORE],
                xn16[:, c * TCORE:(c + 1) * TCORE]), inc=sV)
            vv = K.c(sV)
            for hf in range(2):
                K.emit("tensor", lambda eng, c=cc, h=hf: eng.matmul(
                    pstat[0:1, h * 512:(h + 1) * 512], onescolsb[:, :],
                    xn16[:, c * TCORE + h * 512: c * TCORE + (h + 1) * 512],
                    start=(c == 0), stop=(c == NCH - 1)),
                    waits=[(sV, vv)], inc=sT)
                K.emit("tensor", lambda eng, c=cc, h=hf: eng.matmul(
                    pstat[0:1, 1024 + h * 512: 1024 + (h + 1) * 512], onescolsb[:, :],
                    xsq[:, (c % 2) * TCORE + h * 512:(c % 2) * TCORE + (h + 1) * 512],
                    start=(c == 0), stop=(c == NCH - 1)), inc=sT)
            xsq_free[cc] = K.c(sT)
        tmm = K.c(sT)
        K.emit("scalar", lambda eng: eng.copy(stat[0:1, 0:2048], pstat[0:1, 0:2048]),
               waits=[(sT, tmm)], inc=sA)
        aev = K.c(sA)
        MEAN, EX2, MSQ, VAR, VEP, RVA = 2048, 3072, 4096, 5120, 6144, 7168
        TS("vector", stat[0:1, MEAN:MEAN + 1024], stat[0:1, 0:1024], 1.0 / C, ALU.mult, waits=[(sA, aev)], inc=sV)
        TS("vector", stat[0:1, EX2:EX2 + 1024], stat[0:1, 1024:2048], 1.0 / C, ALU.mult, inc=sV)
        K.emit("vector", lambda eng: eng.tensor_mul(stat[0:1, MSQ:MSQ + 1024], stat[0:1, MEAN:MEAN + 1024], stat[0:1, MEAN:MEAN + 1024]), inc=sV)
        K.emit("vector", lambda eng: eng.tensor_sub(stat[0:1, VAR:VAR + 1024], stat[0:1, EX2:EX2 + 1024], stat[0:1, MSQ:MSQ + 1024]), inc=sV)
        TS("vector", stat[0:1, VEP:VEP + 1024], stat[0:1, VAR:VAR + 1024], 1e-5, ALU.add, inc=sV)
        K.emit("vector", lambda eng: eng.reciprocal(stat[0:1, RVA:RVA + 1024], stat[0:1, VEP:VEP + 1024]), inc=sV)
        vv = K.c(sV)
        K.emit("scalar", lambda eng: eng.activation(stat[0:1, MSQ:MSQ + 1024], stat[0:1, RVA:RVA + 1024], AF.Sqrt),
               waits=[(sV, vv)], inc=sA)
        asq = K.c(sA)
        TS("vector", statrow[0:1, 0:1024], stat[0:1, MEAN:MEAN + 1024], -1.0, ALU.mult, waits=[(sA, asq)], inc=sV)
        K.emit("vector", lambda eng: eng.tensor_copy(statrow[0:1, 1024:2048], stat[0:1, MSQ:MSQ + 1024]), inc=sV)
        vv = K.c(sV)
        for hf in range(2):
            K.emit("tensor", lambda eng, h=hf: eng.matmul(
                pbc[:, h * 512:(h + 1) * 512], ones1sb[:, :],
                statrow[0:1, h * 512:(h + 1) * 512], start=True, stop=True),
                waits=[(sV, vv)], inc=sT)
            K.emit("tensor", lambda eng, h=hf: eng.matmul(
                pbc[:, 1024 + h * 512: 1024 + (h + 1) * 512], ones1sb[:, :],
                statrow[0:1, 1024 + h * 512: 1024 + (h + 1) * 512], start=True, stop=True), inc=sT)
        tb = K.c(sT)
        K.emit("scalar", lambda eng: eng.copy(bm16[:, :], pbc[:, 0:1024]), waits=[(sT, tb)], inc=sA)
        K.emit("scalar", lambda eng: eng.copy(br16[:, :], pbc[:, 1024:2048]), inc=sA)
        return K.c(sA)

    def ln_apply(bc):
        for cc in range(NCH):
            e, sem = (("gpsimd", sG) if cc % 2 == 0 else ("vector", sV))
            K.emit(e, lambda eng, c=cc: eng.tensor_add(
                xn16[:, c * TCORE:(c + 1) * TCORE], xres[:, c * TCORE:(c + 1) * TCORE],
                bm16[:, :]), waits=[(sA, bc)], inc=sem)
            K.emit(e, lambda eng, c=cc: eng.tensor_mul(
                xn16[:, c * TCORE:(c + 1) * TCORE], xn16[:, c * TCORE:(c + 1) * TCORE],
                br16[:, :]), inc=sem)
        return K.c(sG), K.c(sV)

    def make_absum_pump(pbc, ab32):
        jobs = []
        for (mi, wsrc) in enumerate((wqt, wkt, wvt, wpt)):
            for j in range(16):
                jobs.append((mi, j, wsrc))
        state = {"i": 0, "red": {}, "pa": 0}

        def pump(n=1):
            for _ in range(n):
                i = state["i"]
                if i >= len(jobs):
                    return
                mi, j, wsrc = jobs[i]
                ldn = DMA("sync", ab32[:, 0:C],
                          wsrc[j * 128:(j + 1) * 128, :],
                          waits=[(sV, state["red"].get(i - 1, 0))], sem="ab")
                K.emit("vector", lambda eng, jj=j: eng.reduce_sum(
                    asum[:, jj:jj + 1], ab32[:, 0:C], axis=AX,
                    apply_absolute_value=True), waits=[("ab", ldn)], inc=sV)
                state["red"][i] = K.c(sV)
                state["i"] = i + 1
                if (i + 1) % 16 == 0:
                    m = i // 16
                    K.emit("vector", lambda eng: eng.reduce_sum(
                        asum[:, 16:17], asum[:, 0:16], axis=AX), waits=VW(), inc=sV)
                    vv = K.c(sV)
                    K.emit("tensor", lambda eng: eng.matmul(
                        pbc[0:1, 512:513], asum[:, 16:17], ones32csb[:, :],
                        start=True, stop=True), waits=[(sV, vv), (sA, state["pa"])], inc=sT)
                    tv = K.c(sT)
                    K.emit("scalar", lambda eng, mm=m: eng.copy(swsc[0:1, mm:mm + 1], pbc[0:1, 512:513]),
                           waits=[(sT, tv)], inc=sA)
                    av = K.c(sA)
                    K.emit("tensor", lambda eng, mm=m: eng.matmul(
                        pbc[:, 0:1], invccrsb[:, :], swsc[0:1, mm:mm + 1],
                        start=True, stop=True), waits=[(sA, av)], inc=sT)
                    tv2 = K.c(sT)
                    K.emit("scalar", lambda eng: eng.copy(asum[:, 20:21], pbc[:, 0:1]),
                           waits=[(sT, tv2)], inc=sA)
                    av2 = K.c(sA)
                    state["pa"] = av2
                    TS("vector", swb[:, 2 * m:2 * m + 1], asum[:, 20:21], EPS, ALU.max,
                       waits=[(sA, av2)] + VW(), inc=sV)
                    K.emit("vector", lambda eng, mm=m: eng.reciprocal(
                        swb[:, 2 * mm + 1:2 * mm + 2], swb[:, 2 * mm:2 * mm + 1]),
                        waits=VW(), inc=sV)
        return pump

    def ffn(dtile, wreg, ehalf, pd, pe, pump=None, first=False):
        w0in = w0t.rearrange("(cc p) n -> p cc n", p=128)
        w1in = w1t.rearrange("(nc p) c -> p nc c", p=128)
        w0in16 = w0f16.rearrange("(cc p) n -> p cc n", p=128)
        w1in16 = w1f16.rearrange("(nc p) c -> p nc c", p=128)
        wreg_free = [0, 0]
        wb_n = [0, 0]
        pd_free = [0, 0]
        pe_free = [0, 0]
        eh_free = [0, 0]
        for th in range(2):
            fp32src = first and th == 0
            for sl in range(32):
                buf = sl % 2
                wv = wreg[:, buf * 4096:(buf + 1) * 4096].rearrange("p (cc n) -> p cc n", n=256)
                if fp32src:
                    wn = DMA("gpsimd", wv, w0in[:, :, sl * 256:(sl + 1) * 256],
                             waits=[(sT, wreg_free[buf]), ("wb%d" % buf, wb_n[buf])],
                             sem="w%d" % buf)
                    wb_n[buf] = DMA("sync", w0in16[:, :, sl * 256:(sl + 1) * 256], wv,
                                    waits=[("w%d" % buf, wn)], sem="wb%d" % buf)
                else:
                    wn = DMA("sync", wv, w0in16[:, :, sl * 256:(sl + 1) * 256],
                             waits=[(sT, wreg_free[buf]),
                                    ("wb0", K.c("wb0")), ("wb1", K.c("wb1"))],
                             sem="w%d" % buf)
                if pump:
                    pump(2)
                for nb in range(2):
                    n = sl * 2 + nb
                    ps = n % 2
                    for ccc in range(NCH):
                        K.emit("tensor", lambda eng, b=buf, q=nb, c=ccc, p=ps, t=th: eng.matmul(
                            pd[:, p * 512:(p + 1) * 512],
                            wreg[:, b * 4096 + c * 256 + q * 128: b * 4096 + c * 256 + q * 128 + 128],
                            xn16[:, c * TCORE + t * 512: c * TCORE + (t + 1) * 512],
                            start=(c == 0), stop=(c == NCH - 1)),
                            waits=([("w%d" % buf, wn), (sA, pd_free[ps])] if ccc == 0 else []),
                            inc=sT)
                    tn = K.c(sT)
                    K.emit("scalar", lambda eng, nn=n, p=ps: eng.activation(
                        dtile[:, nn * 512:(nn + 1) * 512], pd[:, p * 512:(p + 1) * 512],
                        AF.Gelu, bias=b0sb[:, nn:nn + 1], scale=1.0),
                        waits=[(sT, tn)], inc=sA)
                    pd_free[ps] = K.c(sA)
                wreg_free[buf] = K.c(sT)
            dready = K.c(sA)
            if DBG and th == 0:
                DMA("sync", dbg_d.rearrange("(nn p) t -> p nn t", p=128),
                    dtile[:, :].rearrange("p (nn t) -> p nn t", t=512),
                    waits=[(sA, dready)])
            for cc in range(NCH):
                es = cc % 2
                for hv in range(2):
                    buf = (cc * 2 + hv) % 2
                    wv = wreg[:, buf * 4096:(buf + 1) * 4096].rearrange("p (nc q) -> p nc q", q=128)
                    if fp32src:
                        wn = DMA("gpsimd", wv,
                                 w1in[:, hv * 32:(hv + 1) * 32, cc * 128:(cc + 1) * 128],
                                 waits=[(sT, wreg_free[buf]), ("wb%d" % buf, wb_n[buf])],
                                 sem="w%d" % buf)
                        wb_n[buf] = DMA("sync",
                                        w1in16[:, hv * 32:(hv + 1) * 32, cc * 128:(cc + 1) * 128],
                                        wv, waits=[("w%d" % buf, wn)], sem="wb%d" % buf)
                    else:
                        wn = DMA("sync", wv,
                                 w1in16[:, hv * 32:(hv + 1) * 32, cc * 128:(cc + 1) * 128],
                                 waits=[(sT, wreg_free[buf]),
                                        ("wb0", K.c("wb0")), ("wb1", K.c("wb1"))],
                                 sem="w%d" % buf)
                    if pump:
                        pump(2)
                    for nb in range(32):
                        n = hv * 32 + nb
                        K.emit("tensor", lambda eng, b=buf, q=nb, nn=n, e2=es, t=th: eng.matmul(
                            pe[:, e2 * 512:(e2 + 1) * 512],
                            wreg[:, b * 4096 + q * 128: b * 4096 + q * 128 + 128],
                            dtile[:, nn * 512:(nn + 1) * 512],
                            start=(nn == 0), stop=(nn == NNCH - 1)),
                            waits=([("w%d" % buf, wn), (sA, max(dready, pe_free[es])),
                                    (sV, eh_free[es])] if nb == 0 else []),
                            inc=sT)
                    wreg_free[buf] = K.c(sT)
                tn = K.c(sT)
                K.emit("scalar", lambda eng, c=cc, e2=es: eng.activation(
                    ehalf[:, e2 * 512:(e2 + 1) * 512], pe[:, e2 * 512:(e2 + 1) * 512],
                    AF.Identity, bias=b1h[:, c:c + 1], scale=0.5),
                    waits=[(sT, tn)], inc=sA)
                pe_free[es] = K.c(sA)
                an = pe_free[es]
                K.emit("vector", lambda eng, c=cc, e2=es, t=th: eng.tensor_add(
                    xres[:, c * TCORE + t * 512: c * TCORE + (t + 1) * 512],
                    xres[:, c * TCORE + t * 512: c * TCORE + (t + 1) * 512],
                    ehalf[:, e2 * 512:(e2 + 1) * 512]),
                    waits=[(sA, an)], inc=sV)
                eh_free[es] = K.c(sV)
        if pump:
            pump(64)

    with nc.Block() as blk0:
        ld0 = p0_loads()
        K.flush(blk0)

    def ln_phase(ld):
        with (nc.psum_tensor(NM("pstat"), [2, 2048], F32) as pstat,
              nc.psum_tensor(NM("pbcp"), [128, 2048], F32) as pbcp,
              nc.sbuf_tensor(NM("xsq"), [128, 2 * TCORE], F16) as xsq,
              nc.sbuf_tensor(NM("stat"), [1, 8192], F32) as stat,
              nc.sbuf_tensor(NM("statrow"), [1, 2048], F16) as statrow,
              nc.Block() as blk):
            bc = ln_stats(pstat, pbcp, xsq, stat, statrow, ld)
            ln_apply(bc)
            K.barrier()
            K.flush(blk)

    def ffn_phase(with_absums, first=False):
        with (nc.sbuf_tensor(NM("dtile"), [128, NNCH * 512], F16) as dtile,
              nc.sbuf_tensor(NM("wreg"), [128, 8192], F16) as wreg,
              nc.sbuf_tensor(NM("ehalf"), [128, 1024], F16) as ehalf,
              nc.psum_tensor(NM("pd"), [128, 1024], F32) as pd,
              nc.psum_tensor(NM("pe"), [128, 1024], F32) as pe,
              nc.Block() as blk):
            pump = None
            if with_absums:
                with (nc.sbuf_tensor(NM("ab32"), [128, C], F32) as ab32,
                      nc.psum_tensor(NM("pbcf"), [128, 1024], F32) as pbcf):
                    pump = make_absum_pump(pbcf, ab32)
                    ffn(dtile, wreg, ehalf, pd, pe, pump, first=first)
                    K.barrier()
                    K.flush(blk)
            else:
                ffn(dtile, wreg, ehalf, pd, pe, None, first=first)
                K.barrier()
                K.flush(blk)

    _FIRST_FFN = [True]
    ln_phase(ld0)
    if DBG:
        with nc.Block() as blkd:
            DMA("sync", dbg_xn.rearrange("(cc p) t -> p cc t", p=128),
                xn16[:, :].rearrange("p (cc t) -> p cc t", t=TCORE),
                waits=[(sV, K.c(sV)), (sG, K.c(sG))])
            K.flush(blkd)
    ffn_phase(with_absums=(stage >= 2), first=True)

    if stage >= 2:
        def p3_ln2_quant():
            ld = 0
            with (nc.psum_tensor(NM("pstat"), [2, 2048], F32) as pstat,
                  nc.psum_tensor(NM("pbcp"), [128, 2048], F32) as pbcp,
                  nc.sbuf_tensor(NM("xsq"), [128, 2 * TCORE], F16) as xsq,
                  nc.sbuf_tensor(NM("stat"), [1, 8192], F32) as stat,
                  nc.sbuf_tensor(NM("statrow"), [1, 2048], F16) as statrow,
                  nc.Block() as blk):
                bc = ln_stats(pstat, pbcp, xsq, stat, statrow, ld)
                K.barrier()
                K.flush(blk)
            with (nc.psum_tensor(NM("pt32"), [128, 2048], F32) as pt32,
                  nc.sbuf_tensor(NM("xn32t"), [128, 2048], F32) as xn32t,
                  nc.sbuf_tensor(NM("xqi16t"), [128, 2048], I16) as xqi16t,
                  nc.sbuf_tensor(NM("xq32t"), [128, 2048], F32) as xq32t,
                  nc.Block() as blk):
                pt_free = 0
                xn32_free = 0
                for tc in range(8):
                    for cc in range(NCH):
                        K.emit("vector", lambda eng, c=cc, t=tc: eng.tensor_add(
                            xn32t[:, c * 128:(c + 1) * 128],
                            xres[:, c * TCORE + t * 128: c * TCORE + t * 128 + 128],
                            bm16[:, t * 128:(t + 1) * 128]),
                            waits=[(sT, xn32_free)], inc=sV)
                        K.emit("vector", lambda eng, c=cc, t=tc: eng.tensor_mul(
                            xn32t[:, c * 128:(c + 1) * 128], xn32t[:, c * 128:(c + 1) * 128],
                            br16[:, t * 128:(t + 1) * 128]), waits=VW(), inc=sV)
                    vn = K.c(sV)
                    for cc in range(NCH):
                        K.emit("tensor", lambda eng, c=cc: eng.transpose(
                            pt32[:, c * 128:(c + 1) * 128], xn32t[:, c * 128:(c + 1) * 128],
                            id32sb[:, :]), waits=[(sV, max(vn, pt_free))], inc=sT)
                    xn32_free = K.c(sT)
                    tn = K.c(sT)
                    K.emit("vector", lambda eng: eng.reduce_max(
                        pwork[:, 0:1], pt32[:, :], axis=AX, apply_absolute_value=True),
                        waits=[(sT, tn)], inc=sV)
                    TS("vector", pwork[:, 1:2], pwork[:, 0:1], EPS, ALU.max, waits=VW(), inc=sV)
                    K.emit("vector", lambda eng: eng.reciprocal(pwork[:, 2:3], pwork[:, 1:2]),
                           waits=VW(), inc=sV)
                    TS("vector", pwork[:, 3:4], pwork[:, 2:3], 127.0, ALU.mult, waits=VW(), inc=sV)
                    TS("vector", axinv[:, tc:tc + 1], pwork[:, 1:2], 1.0 / 127.0, ALU.mult,
                       waits=VW(), inc=sV)
                    vq = K.c(sV)
                    K.emit("scalar", lambda eng: eng.activation(
                        xqi16t[:, :], pt32[:, :], AF.Copy, bias=0.0, scale=pwork[:, 3:4]),
                        waits=[(sV, vq)], inc=sA)
                    an = K.c(sA)
                    K.emit("vector", lambda eng: eng.tensor_copy(xq32t[:, :], xqi16t[:, :]),
                           waits=[(sA, an)], inc=sV)
                    vc = K.c(sV)
                    for cc in range(NCH):
                        K.emit("tensor", lambda eng, c=cc: eng.transpose(
                            pt32[:, c * 128:(c + 1) * 128], xq32t[:, c * 128:(c + 1) * 128],
                            id32sb[:, :]), waits=[(sV, vc)], inc=sT)
                    tb = K.c(sT)
                    for cc in range(NCH):
                        K.emit("vector", lambda eng, c=cc, t=tc: eng.tensor_copy(
                            xn16[:, c * TCORE + t * 128: c * TCORE + t * 128 + 128],
                            pt32[:, c * 128:(c + 1) * 128]),
                            waits=[(sT, tb)], inc=sV)
                    pt_free = K.c(sV)
                K.barrier()
                K.flush(blk)

        RG = [[0, 1, 2, 3], [4, 5, 6, 7]]

        def tern_resident(mi, wsrc, wdst, f32buf, pqi16):
            acts = {}
            dves = {}
            for j in range(NCH):
                b = j % 2
                ldn = DMA("gpsimd", f32buf[:, 0:2048],
                          wsrc[j * 128:(j + 1) * 128, :],
                          waits=[(sA, acts.get(j - 1, 0))], sem="tn%d" % b)
                K.emit("scalar", lambda eng, m=mi, bb=b: eng.activation(
                    pqi16[:, bb * 2048:(bb + 1) * 2048], f32buf[:, 0:2048],
                    AF.Copy, bias=0.0, scale=swb[:, 2 * m + 1:2 * m + 2]),
                    waits=[("tn%d" % b, ldn), (sV, dves.get(j - 2, 0))], inc=sA)
                acts[j] = K.c(sA)
                TS("vector", wdst[:, j * 2048:(j + 1) * 2048], pqi16[:, b * 2048:(b + 1) * 2048],
                   -1.0, ALU.max, 1.0, ALU.min, waits=[(sA, acts[j])], inc=sV)
                dves[j] = K.c(sV)
            return dves

        def tern_jobs(mi, wsrc, hdst, f32buf, pqi16hi, f16buf, w0=None):
            st = {"acts": {}, "dves": {}, "outs": {}}

            def mk(j):
                def go():
                    b = j % 2
                    ldn = DMA("gpsimd", f32buf[:, 0:2048],
                              wsrc[j * 128:(j + 1) * 128, :],
                              waits=[(sA, st["acts"].get(j - 1, 0))] + ([w0] if w0 and j < 1 else []),
                              sem="tn%d" % b)
                    K.emit("scalar", lambda eng, m=mi, bb=b: eng.activation(
                        pqi16hi[:, :], f32buf[:, 0:2048],
                        AF.Copy, bias=0.0, scale=swb[:, 2 * m + 1:2 * m + 2]),
                        waits=[("tn%d" % b, ldn), (sV, st["dves"].get(j - 1, 0))], inc=sA)
                    st["acts"][j] = K.c(sA)
                    TS("vector", f16buf[:, 0:2048], pqi16hi[:, :],
                       -1.0, ALU.max, 1.0, ALU.min,
                       waits=[(sA, st["acts"][j]), ("th0", st["outs"].get(j - 1, 0))], inc=sV)
                    st["dves"][j] = K.c(sV)
                    st["outs"][j] = DMA("sync", hdst[j * 128:(j + 1) * 128, :],
                                        f16buf[:, 0:2048],
                                        waits=[(sV, st["dves"][j])], sem="th0")
                return go
            return [mk(j) for j in range(NCH)]

        def proj_quant_evac(tc, pgv, pqi16, mi, dq_extra, out_cb, tn=None):
            if tn is None:
                tn = K.c(sT)
            for seg in range(H):
                K.emit("vector", lambda eng, s=seg: eng.reduce_max(
                    pwork[:, s:s + 1], pgv[:, s * 128:(s + 1) * 128], axis=AX,
                    apply_absolute_value=True), waits=[(sT, tn)], inc=sV)
            K.emit("vector", lambda eng, m=mi, t=tc: eng.tensor_mul(
                pwork[:, 16:17], swb[:, 2 * m:2 * m + 1], axinv[:, t:t + 1]), waits=VW(), inc=sV)
            TS("vector", pwork[:, 17:33], pwork[:, 0:16], pwork[:, 16:17], ALU.mult,
               waits=VW(), inc=sV)
            TS("vector", pwork[:, 33:49], pwork[:, 17:33], EPS, ALU.max, waits=VW(), inc=sV)
            K.emit("vector", lambda eng: eng.reciprocal(pwork[:, 49:65], pwork[:, 33:49]),
                   waits=VW(), inc=sV)
            TS("vector", pwork[:, 65:81], pwork[:, 49:65], pwork[:, 16:17], ALU.mult,
               s2=127.0, op1=ALU.mult, waits=VW(), inc=sV)
            TS("vector", pwork[:, 81:97], pwork[:, 33:49], dq_extra / 127.0, ALU.mult,
               waits=VW(), inc=sV)
            vq = K.c(sV)
            aprev = 0
            for seg in range(H):
                K.emit("scalar", lambda eng, s=seg: eng.activation(
                    pqi16[:, s * 128:(s + 1) * 128], pgv[:, s * 128:(s + 1) * 128],
                    AF.Copy, bias=0.0, scale=pwork[:, 65 + s:66 + s]),
                    waits=[(sV, vq)], inc=sA)
            an = K.c(sA)
            for seg in range(H):
                out_cb(seg, pqi16[:, seg * 128:(seg + 1) * 128], pwork[:, 81 + seg:82 + seg], an, pgv)
            return an

        def proj_phase(mi, rhs_tiles, pg, pqi16, dq_extra, out_cb, side_jobs=None,
                       evac_fn=None):
            if evac_fn is None:
                evac_fn = proj_quant_evac
            """TM projection: G[tc] = xq_blocks.T @ rhs; quant; out_cb per seg.
            rhs_tiles: fn(j) -> (ap [128, 2048], wait_tuple or None)."""
            pg_free = [0, 0]
            prev = None
            for tc in range(8):
                par = tc % 2
                for j in range(NCH):
                    rap, rw = rhs_tiles(j)
                    w = []
                    if j == 0:
                        w.append((sV, pg_free[par]))
                    if rw is not None:
                        w.append(rw)
                    for isl in range(4):
                        K.emit("tensor", lambda eng, jj=j, t=tc, p=par, q=isl, r=rap: eng.matmul(
                            pg[:, p * 2048 + q * 512: p * 2048 + (q + 1) * 512],
                            xn16[:, jj * TCORE + t * 128: jj * TCORE + t * 128 + 128],
                            r[:, q * 512:(q + 1) * 512],
                            start=(jj == 0), stop=(jj == NCH - 1)),
                            waits=(w if isl == 0 else []), inc=sT)
                tn_now = K.c(sT)
                if prev is not None:
                    evac_fn(prev, pg[:, (prev % 2) * 2048:(prev % 2 + 1) * 2048],
                            pqi16, mi, dq_extra, out_cb, tn=tn_prev)
                    pg_free[prev % 2] = K.c(sV)
                    if side_jobs:
                        for _ in range(2):
                            if side_jobs:
                                side_jobs.pop(0)()
                prev = tc
                tn_prev = tn_now
            evac_fn(prev, pg[:, (prev % 2) * 2048:(prev % 2 + 1) * 2048],
                    pqi16, mi, dq_extra, out_cb, tn=tn_prev)
            pg_free[prev % 2] = K.c(sV)
            while side_jobs:
                side_jobs.pop(0)()

        def p4_projections():
            kasm = SB("kasm", [128, 2048], F16)
            with (nc.sbuf_tensor(NM("wres"), [128, NCH * 2048], F16) as wres,
                  nc.sbuf_tensor(NM("f32buf"), [128, 2048], F32) as f32buf,
                  nc.sbuf_tensor(NM("pqi16"), [128, 4096], I16) as pqi16,
                  nc.sbuf_tensor(NM("sc32"), [128, 2048], F32) as sc32,
                  nc.sbuf_tensor(NM("f16b"), [128, 2048], F16) as f16b,
                  nc.psum_tensor(NM("pg"), [128, 4096], F32) as pg,
                  nc.Block() as blk):
                tden = tern_resident(1, wkt, wres, f32buf, pqi16)
                tjobs = tern_jobs(0, wqt, wqq, f32buf, pqi16[:, 2048:4096], f16b,
                                  w0=(sV, tden[NCH - 1]))
                ck = {"n": 0}

                def k_evac(tc, pgv, pqi16_, mi, dq_extra, out_cb, tn=None):
                    if tn is None:
                        tn = K.c(sT)
                    for i in range(16):
                        K.emit("vector", lambda eng, ii=i, pp=pgv: eng.reduce_max(
                            pwork[:, ii:ii + 1], pp[:, ii * 128:(ii + 1) * 128],
                            axis=AX, apply_absolute_value=True),
                            waits=([(sT, tn)] if i == 0 else []), inc=sV)
                    K.emit("vector", lambda eng, m=mi, t=tc: eng.tensor_mul(
                        pwork[:, 16:17], swb[:, 2 * m:2 * m + 1], axinv[:, t:t + 1]), inc=sV)
                    TS("vector", pwork[:, 17:33], pwork[:, 0:16], pwork[:, 16:17],
                       ALU.mult, waits=VW(), inc=sV)
                    TS("vector", pwork[:, 33:49], pwork[:, 17:33], EPS, ALU.max, waits=VW(), inc=sV)
                    K.emit("vector", lambda eng: eng.reciprocal(pwork[:, 49:65], pwork[:, 33:49]),
                           waits=VW(), inc=sV)
                    TS("vector", pwork[:, 65:81], pwork[:, 49:65], pwork[:, 16:17],
                       ALU.mult, s2=127.0, op1=ALU.mult, waits=VW(), inc=sV)
                    vq = K.c(sV)
                    TS("vector", pwork[:, 81:97], pwork[:, 33:49], dq_extra / 127.0, ALU.mult,
                       waits=VW(), inc=sV)
                    for i in range(16):
                        K.emit("scalar", lambda eng, ii=i, pp=pgv: eng.activation(
                            pqi16_[:, ii * 128:(ii + 1) * 128], pp[:, ii * 128:(ii + 1) * 128],
                            AF.Copy, bias=0.0, scale=pwork[:, 65 + ii:66 + ii]),
                            waits=([(sV, vq)] if i == 0 else []), inc=sA)
                    an = K.c(sA)
                    abase = an - 16
                    for i in range(16):
                        K.emit("vector", lambda eng, ii=i: eng.tensor_scalar(
                            sc32[:, ii * 128:(ii + 1) * 128], pqi16_[:, ii * 128:(ii + 1) * 128],
                            pwork[:, 81 + ii:82 + ii], None, ALU.mult),
                            waits=[(sA, abase + i + 1)] + (VW() if i == 0 else []), inc=sV)
                    vbase = K.c(sV) - 16
                    tbase = K.c(sT)
                    for b4 in range(4):
                        for g in range(4):
                            i = b4 * 4 + g
                            K.emit("tensor", lambda eng, ii=i, pp=pgv: eng.transpose(
                                pp[:, ii * 128:(ii + 1) * 128],
                                sc32[:, ii * 128:(ii + 1) * 128], id32sb[:, :]),
                                waits=[(sV, vbase + i + 1), (sA, abase + b4 * 4 + 4)], inc=sT)
                        for g in range(4):
                            i = b4 * 4 + g
                            K.emit("vector", lambda eng, ii=i, pp=pgv: eng.tensor_copy(
                                kasm[:, ii * 128:(ii + 1) * 128], pp[:, ii * 128:(ii + 1) * 128]),
                                waits=[(sT, tbase + b4 * 4 + 4)]
                                + ([("ck", ck["n"])] if i == 0 else []), inc=sV)
                    vn = K.c(sV)
                    ck["n"] = DMA("sync",
                                  bass.AP(cck_in, tc * 128, [[1024, 128], [1024 * 128, 16], [1, 128]]),
                                  kasm[:, :].rearrange("p (ic d) -> p ic d", d=128),
                                  waits=[(sV, vn)], sem="ck")

                proj_phase(1, lambda j: (wres[:, j * 2048:(j + 1) * 2048],
                                         (sV, tden[j])), pg, pqi16, 1.0, None,
                           side_jobs=tjobs, evac_fn=k_evac)
                K.emit("gpsimd", lambda eng: eng.collective_compute(
                    "AllGather", ALU.bypass, replica_groups=RG,
                    ins=[cck_in.ap().opt()], outs=[cck_out.ap().opt()]),
                    waits=[("ck", ck["n"])], inc=sCC)
                K.barrier()
                K.flush(blk)
            with (nc.sbuf_tensor(NM("wres"), [128, NCH * 2048], F16) as wres,
                  nc.sbuf_tensor(NM("f32buf"), [128, 2048], F32) as f32buf,
                  nc.sbuf_tensor(NM("pqi16"), [128, 4096], I16) as pqi16,
                  nc.sbuf_tensor(NM("vst16"), [128, 2048], F16) as vst16,
                  nc.sbuf_tensor(NM("f16b"), [128, 2048], F16) as f16b,
                  nc.psum_tensor(NM("pg"), [128, 4096], F32) as pg,
                  nc.Block() as blk):
                tden = tern_resident(2, wvt, wres, f32buf, pqi16)
                tjobs = tern_jobs(3, wpt, wpq, f32buf, pqi16[:, 2048:4096], f16b,
                                  w0=(sV, tden[NCH - 1]))
                cv = {"n": 0}
                tc_idx = {"tc": 0}

                def v_out(seg, i16sl, dqcol, an, pgv):
                    K.emit("vector", lambda eng, s=i16sl, d=dqcol, g=seg: eng.tensor_scalar(
                        vst16[:, g * 128:(g + 1) * 128], s, d, None, ALU.mult),
                        waits=[(sA, an)] + (VW() + [("cv", cv["n"])] if seg == 0 else []),
                        inc=sV)
                    if seg == H - 1:
                        tc = tc_idx["tc"]
                        vn = K.c(sV)
                        cv["n"] = DMA("sync", ccv_in.ap()[tc * 128:(tc + 1) * 128, :],
                                      vst16[:, :], waits=[(sV, vn)], sem="cv")
                        tc_idx["tc"] = tc + 1

                proj_phase(2, lambda j: (wres[:, j * 2048:(j + 1) * 2048],
                                         (sV, tden[j])), pg, pqi16, 1.0, v_out,
                           side_jobs=tjobs)
                K.emit("gpsimd", lambda eng: eng.collective_compute(
                    "AllGather", ALU.bypass, replica_groups=RG,
                    ins=[ccv_in.ap().opt()], outs=[ccv_out.ap().opt()]),
                    waits=[("cv", cv["n"])], inc=sCC)
                K.barrier()
                K.flush(blk)

        def p4_qproj(qst):
            with (nc.sbuf_tensor(NM("wqbuf"), [128, 2 * 512], F16) as wqbuf,
                  nc.sbuf_tensor(NM("pqi16"), [128, 2048], I16) as pqi16,
                  nc.sbuf_tensor(NM("sc32"), [128, 2048], F32) as sc32,
                  nc.psum_tensor(NM("pg"), [128, 4096], F32) as pg,
                  nc.Block() as blk):
                wq_free = [0, 0]
                wq_n = [0, 0]
                pg_free = [0, 0]

                def qevac(q, hf, tn):
                    for i in range(16):
                        tci, sgi = i // 4, i % 4
                        K.emit("vector", lambda eng, ii=i, b=hf * 4 + i // 4, g=i % 4: eng.reduce_max(
                            pwork[:, ii:ii + 1], pg[:, b * 512 + g * 128: b * 512 + (g + 1) * 128],
                            axis=AX, apply_absolute_value=True),
                            waits=([(sT, tn)] if i == 0 else []), inc=sV)
                    for tci in range(4):
                        K.emit("vector", lambda eng, t=hf * 4 + tci, c=tci: eng.tensor_mul(
                            pwork[:, 16 + c:17 + c], swb[:, 0:1], axinv[:, t:t + 1]), inc=sV)
                    for tci in range(4):
                        TS("vector", pwork[:, 20 + tci * 4:24 + tci * 4],
                           pwork[:, tci * 4:tci * 4 + 4], pwork[:, 16 + tci:17 + tci],
                           ALU.mult, waits=(VW() if tci == 0 else []), inc=sV)
                    TS("vector", pwork[:, 36:52], pwork[:, 20:36], EPS, ALU.max, waits=VW(), inc=sV)
                    K.emit("vector", lambda eng: eng.reciprocal(pwork[:, 52:68], pwork[:, 36:52]),
                           waits=VW(), inc=sV)
                    for tci in range(4):
                        TS("vector", pwork[:, 68 + tci * 4:72 + tci * 4],
                           pwork[:, 52 + tci * 4:56 + tci * 4], pwork[:, 16 + tci:17 + tci],
                           ALU.mult, s2=127.0, op1=ALU.mult,
                           waits=(VW() if tci == 0 else []), inc=sV)
                    vq = K.c(sV)
                    TS("vector", pwork[:, 84:100], pwork[:, 36:52], ISQD / 127.0, ALU.mult,
                       waits=VW(), inc=sV)
                    for i in range(16):
                        K.emit("scalar", lambda eng, ii=i, b=hf * 4 + i // 4, g=i % 4: eng.activation(
                            pqi16[:, ii * 128:(ii + 1) * 128],
                            pg[:, b * 512 + g * 128: b * 512 + (g + 1) * 128],
                            AF.Copy, bias=0.0, scale=pwork[:, 68 + ii:69 + ii]),
                            waits=([(sV, vq)] if i == 0 else []), inc=sA)
                    an = K.c(sA)
                    abase = an - 16
                    for i in range(16):
                        K.emit("vector", lambda eng, ii=i: eng.tensor_scalar(
                            sc32[:, ii * 128:(ii + 1) * 128], pqi16[:, ii * 128:(ii + 1) * 128],
                            pwork[:, 84 + ii:85 + ii], None, ALU.mult),
                            waits=[(sA, abase + i + 1)] + (VW() if i == 0 else []), inc=sV)
                    vn = K.c(sV)
                    vbase = vn - 16
                    tbase = K.c(sT)
                    for b4 in range(4):
                        for g in range(4):
                            i = b4 * 4 + g
                            K.emit("tensor", lambda eng, ii=i, b=hf * 4 + b4, gg=g: eng.transpose(
                                pg[:, b * 512 + gg * 128: b * 512 + (gg + 1) * 128],
                                sc32[:, ii * 128:(ii + 1) * 128], id32sb[:, :]),
                                waits=[(sV, vbase + i + 1), (sA, abase + b4 * 4 + 4)], inc=sT)
                        for g in range(4):
                            i = b4 * 4 + g
                            K.emit("vector", lambda eng, b=hf * 4 + b4,
                                   hd=(q * 4 + g), t=hf * 4 + b4, gg=g: eng.tensor_copy(
                                qst[:, hd * TCORE + t * 128: hd * TCORE + t * 128 + 128],
                                pg[:, b * 512 + gg * 128: b * 512 + (gg + 1) * 128]),
                                waits=[(sT, tbase + b4 * 4 + 4)], inc=sV)

                prev = None
                tn_prev = 0
                for qq in range(4):
                    for hf in range(2):
                        for j in range(NCH):
                            bpar = j % 2
                            wq_n[bpar] = DMA("gpsimd", wqbuf[:, bpar * 512:(bpar + 1) * 512],
                                             wqq[j * 128:(j + 1) * 128, qq * 512:(qq + 1) * 512],
                                             waits=[(sT, wq_free[bpar])], sem="q%d" % bpar)
                            for tci in range(4):
                                tc = hf * 4 + tci
                                K.emit("tensor", lambda eng, jj=j, t=tc, b=hf * 4 + tci, bp=bpar: eng.matmul(
                                    pg[:, b * 512:(b + 1) * 512],
                                    xn16[:, jj * TCORE + t * 128: jj * TCORE + t * 128 + 128],
                                    wqbuf[:, bp * 512:(bp + 1) * 512],
                                    start=(jj == 0), stop=(jj == NCH - 1)),
                                    waits=([(sV, pg_free[hf]), ("q%d" % bpar, wq_n[bpar])]
                                           if tci == 0 else []), inc=sT)
                            wq_free[bpar] = K.c(sT)
                        tn_now = K.c(sT)
                        if prev is not None:
                            qevac(prev[0], prev[1], tn_prev)
                            pg_free[prev[1]] = K.c(sV)
                        prev = (qq, hf)
                        tn_prev = tn_now
                qevac(prev[0], prev[1], tn_prev)
                pg_free[prev[1]] = K.c(sV)
                K.barrier()
                K.flush(blk)

        def p5_attention(qst):
            with (nc.sbuf_tensor(NM("kbuf"), [128, 2 * 4096], F16) as kbuf,
                  nc.sbuf_tensor(NM("vbuf"), [128, 2 * 4096], F16) as vbuf,
                  nc.sbuf_tensor(NM("scsc"), [128, 4096], F32) as scsc,
                  nc.sbuf_tensor(NM("wi16"), [128, 1024], I16) as wi16,
                  nc.sbuf_tensor(NM("w16"), [128, 6 * 512], F16) as w16,
                  nc.sbuf_tensor(NM("wyi16"), [128, 512], I16) as wyi16,
                  nc.sbuf_tensor(NM("mx32"), [128, 32], F32) as mx32,
                  nc.sbuf_tensor(NM("uq32"), [128, 32], F32) as uq32,
                  nc.sbuf_tensor(NM("cwk"), [128, 48], F32) as cwk,
                  nc.sbuf_tensor(NM("mt16"), [128, 16], F16) as mt16,
                  nc.sbuf_tensor(NM("mrow2"), [2, 1024], F16) as mrow2,
                  nc.psum_tensor(NM("psA"), [128, 2048], F32) as psA,
                  nc.psum_tensor(NM("psB"), [128, 1024], F32) as psB,
                  nc.psum_tensor(NM("psY"), [128, 512], F32) as psY,
                  nc.psum_tensor(NM("pstr"), [128, 1024], F16) as pstr,
                  nc.Block() as blk):
                kv_free = [0, 0]
                kv_ld = [0, 0]
                psA_free = [0, 0]
                scq_free = [0, 0, 0, 0]
                psB_free = [0, 0]
                wi_free = [0, 0]
                w16_free = [0, 0, 0, 0, 0, 0]
                psY_free = [0]
                yq_done = [0, 0]
                pstr_free = [0, 0]
                ytr_free = 0
                qctr = [0]
                SEQ = [(h, g) for h in range(H) for g in range(2)]

                def load_kv(h):
                    par = h % 2
                    w = [(sT, kv_free[par]), (sCC, 1)]
                    for r in range(4):
                        DMA("sync", kbuf[:, par * 4096 + r * 1024: par * 4096 + (r + 1) * 1024],
                            bass.AP(cck_out, (r * 2048 + h * 128) * TCORE, [[1024, 128], [1, 1024]]),
                            waits=w, sem="kv%d" % par)
                        w = []
                    wv = [(sCC, 2)]
                    for r in range(4):
                        kv_ld[par] = DMA(
                            "sync",
                            vbuf[:, par * 4096 + r * 1024: par * 4096 + (r + 1) * 1024]
                            .rearrange("p (lc d) -> p lc d", d=128),
                            bass.AP(ccv_out, (r * 1024) * 2048 + h * 128,
                                    [[2048, 128], [128 * 2048, 8], [1, 128]]),
                            waits=wv, sem="kv%d" % par)
                        wv = []

                def make_A_jobs(i):
                    h, g = SEQ[i]
                    par = h % 2
                    ip = i % 2
                    jobs = []
                    for tci in range(4):
                        tc = g * 4 + tci
                        lhs = qst[:, h * TCORE + tc * 128: h * TCORE + tc * 128 + 128]
                        for qd in range(4):
                            def job(l=lhs, p=par, q=qd, col=(ip * 16 + tci * 4 + qd)):
                                slot = qctr[0] % 2
                                qb = qctr[0] % 4
                                qctr[0] += 1
                                for half in range(2):
                                    K.emit("tensor", lambda eng, hf=half, s=slot, l2=l, p2=p, q2=q: eng.matmul(
                                        psA[:, s * 1024 + hf * 512: s * 1024 + (hf + 1) * 512],
                                        l2, kbuf[:, p2 * 4096 + q2 * 1024 + hf * 512: p2 * 4096 + q2 * 1024 + (hf + 1) * 512],
                                        start=True, stop=True),
                                        waits=([(sA, psA_free[slot]),
                                                ("kv%d" % p, kv_ld[p])] if half == 0 else []),
                                        inc=sT)
                                tmm = K.c(sT)
                                K.emit("scalar", lambda eng, qq=qb, s=slot, c2=col: eng.activation(
                                    scsc[:, qq * 1024:(qq + 1) * 1024],
                                    psA[:, s * 1024:(s + 1) * 1024], AF.Exp,
                                    accum_out=uq32[:, c2:c2 + 1]),
                                    waits=[(sT, tmm), (sV, scq_free[qb])], inc=sA)
                                psA_free[slot] = K.c(sA)
                                K.emit("vector", lambda eng, qq=qb, c2=col: eng.reduce_max(
                                    mx32[:, c2:c2 + 1], scsc[:, qq * 1024:(qq + 1) * 1024], axis=AX),
                                    waits=[(sA, psA_free[slot])], inc=sV)
                                scq_free[qb] = K.c(sV)
                            jobs.append(job)
                    return jobs

                def do_A(i):
                    for jb in make_A_jobs(i):
                        jb()

                def do_Acomb(i):
                    h, g = SEQ[i]
                    ip = i % 2
                    b = ip * 16
                    gi = h * 8 + g * 4
                    cb = ip * 24
                    K.emit("vector", lambda eng: eng.tensor_max(
                        cwk[:, cb:cb + 8], mx32[:, b:b + 16:2], mx32[:, b + 1:b + 16:2]),
                        waits=VW(), inc=sV)
                    K.emit("vector", lambda eng: eng.tensor_max(
                        cwk[:, cb + 8:cb + 12], cwk[:, cb:cb + 8:2], cwk[:, cb + 1:cb + 8:2]),
                        waits=VW(), inc=sV)
                    K.emit("vector", lambda eng: eng.tensor_add(
                        cwk[:, cb:cb + 8], uq32[:, b:b + 16:2], uq32[:, b + 1:b + 16:2]),
                        waits=VW(), inc=sV)
                    K.emit("vector", lambda eng: eng.tensor_add(
                        cwk[:, cb + 12:cb + 16], cwk[:, cb:cb + 8:2], cwk[:, cb + 1:cb + 8:2]),
                        waits=VW(), inc=sV)
                    K.emit("vector", lambda eng: eng.reciprocal(cwk[:, cb + 16:cb + 20], cwk[:, cb + 8:cb + 12]),
                           waits=VW(), inc=sV)
                    K.emit("vector", lambda eng, z=gi: eng.tensor_mul(
                        zbuf[:, z:z + 4], cwk[:, cb + 12:cb + 16], cwk[:, cb + 16:cb + 20]),
                        waits=VW(), inc=sV)
                    TS("vector", cwk[:, cb:cb + 4], zbuf[:, gi:gi + 4], 127.0, ALU.mult,
                       waits=VW(), inc=sV)
                    K.emit("vector", lambda eng, z=gi: eng.reciprocal(
                        z127i[:, z:z + 4], cwk[:, cb:cb + 4]), waits=VW(), inc=sV)
                    vz = K.c(sV)
                    K.emit("scalar", lambda eng: eng.activation(
                        cwk[:, cb + 4:cb + 8], cwk[:, cb + 8:cb + 12], AF.Ln),
                        waits=[(sV, vz)], inc=sA)
                    am = K.c(sA)
                    TS("vector", cwk[:, cb + 20:cb + 24], cwk[:, cb + 4:cb + 8], -1.0, ALU.mult,
                       waits=[(sA, am)], inc=sV)
                    mb = ip * 8
                    K.emit("vector", lambda eng: eng.tensor_copy(
                        mt16[:, mb:mb + 8:2], cwk[:, cb + 20:cb + 24]), waits=VW(), inc=sV)
                    K.emit("vector", lambda eng: eng.tensor_copy(
                        cwk[:, cb:cb + 4], mt16[:, mb:mb + 8:2]), waits=VW(), inc=sV)
                    K.emit("vector", lambda eng: eng.tensor_sub(
                        cwk[:, cb + 4:cb + 8], cwk[:, cb + 20:cb + 24], cwk[:, cb:cb + 4]),
                        waits=VW(), inc=sV)
                    K.emit("vector", lambda eng: eng.tensor_copy(
                        mt16[:, mb + 1:mb + 8:2], cwk[:, cb + 4:cb + 8]), waits=VW(), inc=sV)
                    return K.c(sV)

                def do_mrow(i, vm):
                    ip = i % 2
                    mb = ip * 8
                    for t in range(4):
                        K.emit("tensor", lambda eng, tt=t: eng.transpose(
                            pstr[0:2, ip * 512 + tt * 128: ip * 512 + (tt + 1) * 128],
                            mt16[:, mb + 2 * tt: mb + 2 * tt + 2], id16sb[:, :]),
                            waits=([(sV, vm), (sA, pstr_free[ip])] if t == 0 else []), inc=sT)
                    tmr = K.c(sT)
                    K.emit("scalar", lambda eng: eng.copy(
                        mrow2[0:2, ip * 512:(ip + 1) * 512], pstr[0:2, ip * 512:(ip + 1) * 512]),
                        waits=[(sT, tmr)], inc=sA)
                    pstr_free[ip] = K.c(sA)

                def make_y_jobs(i, tpv):
                    h, g = SEQ[i]
                    gi = h * 8 + g * 4
                    done = [0, 0]

                    def j0():
                        K.emit("scalar", lambda eng: eng.copy(ysc[:, :], psY[:, :]),
                               waits=[(sT, tpv)], inc=sA)
                        psY_free[0] = K.c(sA)
                        for j in range(4):
                            K.emit("tensor", lambda eng, q=j: eng.transpose(
                                psY[:, q * 128:(q + 1) * 128], ysc[:, q * 128:(q + 1) * 128],
                                id32sb[:, :]),
                                waits=([(sA, psY_free[0])] if j == 0 else []), inc=sT)

                    def j1():
                        ty = K.c(sT)
                        for j in range(4):
                            K.emit("vector", lambda eng, q=j: eng.reduce_max(
                                ywork[:, q:q + 1], psY[:, q * 128:(q + 1) * 128], axis=AX,
                                apply_absolute_value=True),
                                waits=([(sT, ty)] if j == 0 else []), inc=sV)
                        zsl = z127i[:, gi:gi + 4]
                        K.emit("vector", lambda eng, z=zsl: eng.tensor_mul(
                            ywork[:, 4:8], ywork[:, 0:4], z), waits=VW(), inc=sV)
                        TS("vector", ywork[:, 8:12], ywork[:, 4:8], EPS, ALU.max,
                           waits=VW(), inc=sV)
                        K.emit("vector", lambda eng: eng.reciprocal(ywork[:, 12:16], ywork[:, 8:12]),
                               waits=VW(), inc=sV)
                        K.emit("vector", lambda eng, z=zsl: eng.tensor_mul(
                            ywork[:, 16:20], ywork[:, 12:16], z), waits=VW(), inc=sV)
                        TS("vector", ywork[:, 20:24], ywork[:, 16:20], 127.0, ALU.mult,
                           waits=VW(), inc=sV)
                        TS("vector", ywork[:, 24:28], ywork[:, 8:12], 1.0 / 127.0, ALU.mult,
                           waits=VW(), inc=sV)

                    def j2():
                        vy = K.c(sV)
                        for j in range(4):
                            K.emit("scalar", lambda eng, q=j: eng.activation(
                                wyi16[:, q * 128:(q + 1) * 128], psY[:, q * 128:(q + 1) * 128],
                                AF.Copy, bias=0.0, scale=ywork[:, 20 + q:21 + q]),
                                waits=([(sV, vy)] if j == 0 else []), inc=sA)
                        ay = K.c(sA)
                        for j in range(4):
                            K.emit("vector", lambda eng, q=j: eng.tensor_scalar(
                                ysc[:, q * 128:(q + 1) * 128], wyi16[:, q * 128:(q + 1) * 128],
                                ywork[:, 24 + q:25 + q], None, ALU.mult),
                                waits=([(sA, ay)] + VW() if j == 0 else []), inc=sV)

                    def j3():
                        vy2 = K.c(sV)
                        ay = K.c(sA)
                        for j in range(4):
                            K.emit("tensor", lambda eng, q=j: eng.transpose(
                                psY[:, q * 128: (q + 1) * 128],
                                ysc[:, q * 128:(q + 1) * 128], id32sb[:, :]),
                                waits=([(sV, vy2), (sA, ay)] if j == 0 else []), inc=sT)
                        done[0] = K.c(sT)

                    def j4():
                        ty2 = done[0]
                        for j in range(4):
                            K.emit("vector", lambda eng, q=j, hh=h, gg=g: eng.tensor_copy(
                                qst[:, hh * TCORE + gg * 512 + q * 128: hh * TCORE + gg * 512 + (q + 1) * 128],
                                psY[:, q * 128: (q + 1) * 128]),
                                waits=([(sT, ty2)] if j == 0 else []), inc=sV)
                        done[1] = K.c(sV)

                    return [j0, j1, j2, j3, j4], done

                PVLAG = 6

                def do_B(i, y_jobs, a_jobs, comb_job, yq):
                    h, g = SEQ[i]
                    par = h % 2
                    ip = i % 2
                    pv_pend = []

                    def emit_scores(sc):
                        pp = sc % 2
                        sl = sc % 6
                        K.emit("tensor", lambda eng, s=sc, p=par, q=pp, hh=h, gg=g: eng.matmul(
                            psB[:, q * 512:(q + 1) * 512],
                            kbuf[:, p * 4096 + s * 128: p * 4096 + s * 128 + 128],
                            qst[:, hh * TCORE + gg * 512: hh * TCORE + (gg + 1) * 512],
                            start=True, stop=False),
                            waits=[(sA, max(psB_free[pp], pstr_free[ip]))], inc=sT)
                        K.emit("tensor", lambda eng, q=pp, i2=ip: eng.matmul(
                            psB[:, q * 512:(q + 1) * 512], ones2sb[:, :],
                            mrow2[0:2, i2 * 512:(i2 + 1) * 512],
                            start=False, stop=True), inc=sT)
                        tmm = K.c(sT)
                        K.emit("scalar", lambda eng, q=pp: eng.activation(
                            wi16[:, q * 512:(q + 1) * 512], psB[:, q * 512:(q + 1) * 512],
                            AF.Exp, bias=ln127sb[:, 0:1], scale=1.0),
                            waits=[(sT, tmm), (sV, wi_free[pp])], inc=sA)
                        psB_free[pp] = K.c(sA)
                        K.emit("vector", lambda eng, q=pp, l=sl: eng.tensor_copy(
                            w16[:, l * 512:(l + 1) * 512], wi16[:, q * 512:(q + 1) * 512]),
                            waits=[(sA, psB_free[pp]), (sT, w16_free[sl])], inc=sV)
                        wi_free[pp] = K.c(sV)
                        pv_pend.append((sc, wi_free[pp]))

                    def emit_pv(sc, wn):
                        sl = sc % 6
                        K.emit("tensor", lambda eng, s=sc, p=par, l=sl: eng.matmul(
                            psY[:, :], vbuf[:, p * 4096 + s * 128: p * 4096 + s * 128 + 128],
                            w16[:, l * 512:(l + 1) * 512],
                            start=(s == 0), stop=(s == 31)),
                            waits=[(sV, wn)] + ([(sV, yq[1]), (sT, yq[0])] if sc == 0 else []),
                            inc=sT)
                        w16_free[sl] = K.c(sT)

                    emit_scores(0)
                    if y_jobs:
                        y_jobs.pop(0)()
                    for sc in range(1, 32):
                        emit_scores(sc)
                        if sc >= PVLAG:
                            emit_pv(*pv_pend[sc - PVLAG])
                        if y_jobs:
                            y_jobs.pop(0)()
                        if a_jobs:
                            a_jobs.pop(0)()
                        if sc == 20 and comb_job is not None:
                            comb_job()
                    for k in range(32 - PVLAG, 32):
                        emit_pv(*pv_pend[k])
                    if a_jobs:
                        while a_jobs:
                            a_jobs.pop(0)()
                    if comb_job is not None and not getattr(comb_job, "_done", False):
                        pass
                    return K.c(sT)

                load_kv(0)
                load_kv(1)
                do_A(0)
                vm = do_Acomb(0)
                do_mrow(0, vm)
                y_jobs = None
                yq = [0, 0]
                for i in range(len(SEQ)):
                    a_jobs = make_A_jobs(i + 1) if i + 1 < len(SEQ) else None

                    def comb_job(ii=i):
                        vm2 = do_Acomb(ii + 1)
                        do_mrow(ii + 1, vm2)
                    cj = comb_job if i + 1 < len(SEQ) else None
                    tpv = do_B(i, y_jobs, a_jobs, cj, yq)
                    y_jobs, yq = make_y_jobs(i, tpv)
                    h, g = SEQ[i]
                    if g == 1:
                        kv_free[h % 2] = K.c(sT)
                        if h + 2 < H:
                            load_kv(h + 2)
                for jb in y_jobs:
                    jb()
                K.barrier()
                K.flush(blk)

        def p6_outproj(qst):
            with (nc.sbuf_tensor(NM("wpbuf"), [128, 2 * 2048], F16) as wpbuf,
                  nc.sbuf_tensor(NM("pqi16"), [128, 4096], I16) as pqi16,
                  nc.sbuf_tensor(NM("ao32"), [128, 2048], F32) as ao32,
                  nc.psum_tensor(NM("po"), [128, 4096], F32) as po,
                  nc.Block() as blk):
                wp_free = [0, 0]
                wp_n = [0, 0]
                po_free = [0, 0]

                def p6_quant(tc, tn):
                    par = tc % 2
                    pov = po[:, par * 2048:(par + 1) * 2048]
                    K.emit("vector", lambda eng, pp=pov: eng.reduce_max(
                        pwork[:, 0:1], pp, axis=AX, apply_absolute_value=True),
                        waits=[(sT, tn)], inc=sV)
                    K.emit("vector", lambda eng: eng.tensor_mul(pwork[:, 1:2], pwork[:, 0:1], swb[:, 6:7]),
                           waits=VW(), inc=sV)
                    TS("vector", pwork[:, 2:3], pwork[:, 1:2], EPS, ALU.max, waits=VW(), inc=sV)
                    K.emit("vector", lambda eng: eng.reciprocal(pwork[:, 3:4], pwork[:, 2:3]),
                           waits=VW(), inc=sV)
                    K.emit("vector", lambda eng: eng.tensor_mul(pwork[:, 4:5], pwork[:, 3:4], swb[:, 6:7]),
                           waits=VW(), inc=sV)
                    TS("vector", pwork[:, 6:7], pwork[:, 4:5], 127.0, ALU.mult, waits=VW(), inc=sV)
                    TS("vector", pwork[:, 5:6], pwork[:, 2:3], 1.0 / 127.0, ALU.mult, waits=VW(), inc=sV)
                    vq = K.c(sV)
                    K.emit("scalar", lambda eng, pp=pov: eng.activation(
                        pqi16[:, 0:2048], pp, AF.Copy, bias=0.0, scale=pwork[:, 6:7]),
                        waits=[(sV, vq)], inc=sA)
                    an = K.c(sA)
                    K.emit("vector", lambda eng: eng.tensor_scalar(
                        ao32[:, :], pqi16[:, 0:2048], pwork[:, 5:6], None, ALU.mult),
                        waits=[(sA, an)] + VW(), inc=sV)
                    vd = K.c(sV)
                    for cc in range(NCH):
                        K.emit("tensor", lambda eng, c=cc, p=par: eng.transpose(
                            po[:, p * 2048 + c * 128: p * 2048 + (c + 1) * 128],
                            ao32[:, c * 128:(c + 1) * 128], id32sb[:, :]),
                            waits=[(sV, vd)], inc=sT)
                    tb = K.c(sT)
                    for cc in range(NCH):
                        K.emit("vector", lambda eng, c=cc, t=tc, p=par: eng.tensor_add(
                            xres[:, c * TCORE + t * 128: c * TCORE + t * 128 + 128],
                            xres[:, c * TCORE + t * 128: c * TCORE + t * 128 + 128],
                            po[:, p * 2048 + c * 128: p * 2048 + (c + 1) * 128]),
                            waits=[(sT, tb)], inc=sV)
                    po_free[par] = K.c(sV)

                prev = None
                for tc in range(8):
                    par = tc % 2
                    for j in range(NCH):
                        bpar = j % 2
                        wp_n[bpar] = DMA("gpsimd", wpbuf[:, bpar * 2048:(bpar + 1) * 2048],
                                         wpq[j * 128:(j + 1) * 128, :],
                                         waits=[(sT, wp_free[bpar])], sem="p%d" % bpar)
                        for isl in range(4):
                            K.emit("tensor", lambda eng, jj=j, t=tc, p=par, q=isl, bp=bpar: eng.matmul(
                                po[:, p * 2048 + q * 512: p * 2048 + (q + 1) * 512],
                                qst[:, jj * TCORE + t * 128: jj * TCORE + t * 128 + 128],
                                wpbuf[:, bp * 2048 + q * 512: bp * 2048 + (q + 1) * 512],
                                start=(jj == 0), stop=(jj == NCH - 1)),
                                waits=([(sV, po_free[par]), ("p%d" % bpar, wp_n[bpar])]
                                       if isl == 0 else []), inc=sT)
                        wp_free[bpar] = K.c(sT)
                    tn_now = K.c(sT)
                    if prev is not None:
                        p6_quant(prev, tn_prev)
                    prev = tc
                    tn_prev = tn_now
                p6_quant(prev, tn_prev)
                K.barrier()
                K.flush(blk)

        p3_ln2_quant()
        p4_projections()
        qstack = contextlib.ExitStack()
        qst = qstack.enter_context(nc.sbuf_tensor(NM("qst"), [128, H * TCORE], F16))
        p4_qproj(qst)
        if DBG:
            with nc.Block() as blkd2:
                DMA("sync", dbg_q, qst[:, :], waits=[(sV, K.c(sV))])
                DMA("sync", dbg_sw, swb[:, :])
                DMA("sync", dbg_kg, cck_out.ap()[:, :], waits=[(sCC, 2)])
                K.flush(blkd2)
        p5_attention(qst)
        if DBG:
            with nc.Block() as blkd3:
                DMA("sync", dbg_y, qst[:, :], waits=[(sV, K.c(sV))])
                DMA("sync", dbg_z, zbuf[:, :])
                K.flush(blkd3)
        p6_outproj(qst)
        qstack.close()

    if stage >= 3:
        ln_phase(0)
        ffn_phase(with_absums=False)

    with nc.Block() as blk8:
        DMA("sync", yt.rearrange("(cc p) t -> p cc t", p=128),
            xres[:, :].rearrange("p (cc t) -> p cc t", t=TCORE),
            waits=[(sV, K.c(sV)), (sA, K.c(sA)), (sG, K.c(sG)), (sT, K.c(sT))])
        fin = K.c(sL)
        K.emit("sync", lambda eng: eng.nop(), waits=[(sL, fin)])
        K.flush(blk8)

    st.close()
    K._sem_stack.close()
    return nc


def build_attention(env):
    raise NotImplementedError


_BUILT = {}


def kernel(L0_attn_x, L0_attn_Wq, L0_attn_Wk, L0_attn_Wv, L0_attn_Wp,
           L0_ffn_W0, L0_ffn_bias0, L0_ffn_W1, L0_ffn_bias1):
    x = np.asarray(L0_attn_x, np.float32)
    consts = dict(
        w0t=np.ascontiguousarray(np.asarray(L0_ffn_W0, np.float32).T),
        w1t=np.ascontiguousarray(np.asarray(L0_ffn_W1, np.float32).T),
        b0m=np.ascontiguousarray(np.asarray(L0_ffn_bias0, np.float32).reshape(NNCH, 128).T),
        b1m=np.ascontiguousarray(np.asarray(L0_ffn_bias1, np.float32).reshape(NCH, 128).T),
        wqt=np.ascontiguousarray(np.asarray(L0_attn_Wq, np.float32).T),
        wkt=np.ascontiguousarray(np.asarray(L0_attn_Wk, np.float32).T),
        wvt=np.ascontiguousarray(np.asarray(L0_attn_Wv, np.float32).T),
        wpt=np.ascontiguousarray(np.asarray(L0_attn_Wp, np.float32).T),
        id32=np.eye(128, dtype=np.float32),
        id16=np.eye(128, dtype=np.float16),
        ones2=np.ones((2, 128), np.float16),
        ones1r=np.ones((1, 128), np.float16),
        onescol=np.ones((128, 1), np.float16),
        ones32r=np.ones((1, 128), np.float32),
        ones32c=np.ones((128, 1), np.float32),
        invccr=np.full((1, 128), 1.0 / (C * C), np.float32),
    )
    if STAGE not in _BUILT:
        _BUILT[STAGE] = build_nc(STAGE)
    ncp = _BUILT[STAGE]
    in_maps = []
    for c in range(8):
        b, t0 = c // 4, (c % 4) * TCORE
        m = dict(consts)
        m["xt"] = np.ascontiguousarray(x[b, t0:t0 + TCORE, :].T)
        in_maps.append(m)
    res = run_bass_kernel_spmd(ncp, in_maps, core_ids=list(range(8)))
    out = np.empty((2, 4096, C), np.float32)
    for c in range(8):
        b, t0 = c // 4, (c % 4) * TCORE
        out[b, t0:t0 + TCORE, :] = res.results[c]["yt"].T
    return out


# version 15
# speedup vs baseline: 1.2201x; 1.0614x over previous
import math
import contextlib
import numpy as np
import concourse.bass as bass
import concourse.mybir as mybir
from concourse.bass_utils import run_bass_kernel_spmd

F32 = mybir.dt.float32
F16 = mybir.dt.float16
I16 = mybir.dt.int16
AF = mybir.ActivationFunctionType
ALU = mybir.AluOpType
AX = mybir.AxisListType.X

C = 2048
H = 16
DH = 128
NF = 8192
TCORE = 1024
NCH = C // 128
NNCH = NF // 128
ISQD = 1.0 / math.sqrt(128.0)
LN127 = float(np.log(np.float32(127.0)))
EPS = 1e-5
ENGINES = ("tensor", "vector", "scalar", "gpsimd", "sync")

DBG = False
STAGE = 3


class KB:
    def __init__(self, nc):
        self.nc = nc
        self.cnt = {}
        self.sems = {}
        self.streams = {e: [] for e in ENGINES}
        self.lastwait = {e: {} for e in ENGINES}
        self._sem_stack = contextlib.ExitStack()

    def sem(self, name):
        if name not in self.sems:
            self.sems[name] = self._sem_stack.enter_context(self.nc.semaphore(name))
            self.cnt[name] = 0
        return self.sems[name]

    def emit(self, eng, fn, waits=(), inc=None, incn=1):
        ws = []
        for (s, c) in waits:
            if c is None or c <= 0:
                continue
            self.sem(s)
            if c > self.lastwait[eng].get(s, -1):
                ws.append((s, c))
                self.lastwait[eng][s] = c
        out = None
        if inc is not None:
            self.sem(inc)
            self.cnt[inc] += incn
            out = self.cnt[inc]
        self.streams[eng].append((fn, ws, inc, incn))
        return out

    def c(self, name):
        self.sem(name)
        return self.cnt[name]

    def barrier(self):
        snaps = {}
        for e in ENGINES:
            snaps[e] = self.emit(e, lambda eng: eng.nop(), inc="bar_" + e)
        for e in ENGINES:
            self.emit(e, lambda eng: eng.nop(),
                      waits=[("bar_" + o, snaps[o]) for o in ENGINES if o != e])

    def flush(self, block):
        streams = {e: list(self.streams[e]) for e in ENGINES}
        for e in ENGINES:
            self.streams[e] = []

        def mk(e):
            def body(eng):
                for (fn, ws, inc, incn) in streams[e]:
                    for (s, cv) in ws:
                        eng.wait_ge(self.sems[s], cv)
                    instr = fn(eng)
                    if inc is not None:
                        instr.then_inc(self.sems[inc], incn)
            return body

        block.tensor(mk("tensor"))
        block.vector(mk("vector"))
        block.scalar(mk("scalar"))
        block.gpsimd(mk("gpsimd"))
        block.sync(mk("sync"))


sA, sV, sG, sT, sL, sW, sCC, sO = "a", "v", "g", "t", "l", "w", "cc", "o"


def build_nc(stage=STAGE):
    nc = bass.Bass("TRN2", target_bir_lowering=False)
    K = KB(nc)

    xt = nc.dram_tensor("xt", [C, TCORE], F32, kind="ExternalInput").ap()
    w0t = nc.dram_tensor("w0t", [128, C * NF // 128], F32, kind="ExternalInput").ap()
    w1t = nc.dram_tensor("w1t", [128, C * NF // 128], F32, kind="ExternalInput").ap()
    b0m = nc.dram_tensor("b0m", [128, NNCH], F32, kind="ExternalInput").ap()
    b1m = nc.dram_tensor("b1m", [128, NCH], F32, kind="ExternalInput").ap()
    wqt = nc.dram_tensor("wqt", [C, C], F32, kind="ExternalInput").ap()
    wkt = nc.dram_tensor("wkt", [C, C], F32, kind="ExternalInput").ap()
    wvt = nc.dram_tensor("wvt", [C, C], F32, kind="ExternalInput").ap()
    wpt = nc.dram_tensor("wpt", [C, C], F32, kind="ExternalInput").ap()
    id32 = nc.dram_tensor("id32", [128, 128], F32, kind="ExternalInput").ap()
    id16 = nc.dram_tensor("id16", [128, 128], F16, kind="ExternalInput").ap()
    ones2 = nc.dram_tensor("ones2", [2, 128], F16, kind="ExternalInput").ap()
    ones1r = nc.dram_tensor("ones1r", [1, 128], F16, kind="ExternalInput").ap()
    onescol = nc.dram_tensor("onescol", [128, 1], F16, kind="ExternalInput").ap()
    ones32r = nc.dram_tensor("ones32r", [1, 128], F32, kind="ExternalInput").ap()
    ones32c = nc.dram_tensor("ones32c", [128, 1], F32, kind="ExternalInput").ap()
    invccr = nc.dram_tensor("invccr", [1, 128], F32, kind="ExternalInput").ap()
    yt = nc.dram_tensor("yt", [C, TCORE], F32, kind="ExternalOutput").ap()
    dbg_xn = nc.dram_tensor("dbg_xn", [C, TCORE], F16, kind="ExternalOutput").ap()
    dbg_d = nc.dram_tensor("dbg_d", [NF, 512], F16, kind="ExternalOutput").ap()
    dbg_q = nc.dram_tensor("dbg_q", [128, H * TCORE], F16, kind="ExternalOutput").ap()
    dbg_y = nc.dram_tensor("dbg_y", [128, H * TCORE], F16, kind="ExternalOutput").ap()
    dbg_sw = nc.dram_tensor("dbg_sw", [128, 8], F32, kind="ExternalOutput").ap()
    dbg_z = nc.dram_tensor("dbg_z", [128, 128], F32, kind="ExternalOutput").ap()
    dbg_kg = nc.dram_tensor("dbg_kg", [4 * C, TCORE], F16, kind="ExternalOutput").ap()
    dbg_sc = nc.dram_tensor("dbg_sc", [128, 4096], F32, kind="ExternalOutput").ap()
    dbg_ysc = nc.dram_tensor("dbg_ysc", [128, 512], F32, kind="ExternalOutput").ap()
    dbg_mrow = nc.dram_tensor("dbg_mrow", [2, 512], F16, kind="ExternalOutput").ap()
    dbg_w16 = nc.dram_tensor("dbg_w16", [128, 1024], F16, kind="ExternalOutput").ap()
    dbg_mw = nc.dram_tensor("dbg_mw", [128, 16], F32, kind="ExternalOutput").ap()
    w0f16 = nc.dram_tensor("w0f16", [128, C * NF // 128], F16).ap()
    w1f16 = nc.dram_tensor("w1f16", [128, C * NF // 128], F16).ap()
    wqq = nc.dram_tensor("wqq", [C, C], F16).ap()
    wpq = nc.dram_tensor("wpq", [C, C], F16).ap()
    cck_in = nc.dram_tensor("cck_in", [C, TCORE], F16)
    cck_out = nc.dram_tensor("cck_out", [4 * C, TCORE], F16)
    ccv_in = nc.dram_tensor("ccv_in", [TCORE, C], F16)
    ccv_out = nc.dram_tensor("ccv_out", [4 * TCORE, C], F16)

    st = contextlib.ExitStack()
    SB = lambda *a: st.enter_context(nc.sbuf_tensor(*a))
    _nmc = [0]

    def NM(b):
        _nmc[0] += 1
        return "%s_%d" % (b, _nmc[0])

    xres = SB("xres", [128, NCH * TCORE], F32)
    xn16 = SB("xn16", [128, NCH * TCORE], F16)
    bm16 = SB("bm16", [128, TCORE], F16)
    br16 = SB("br16", [128, TCORE], F16)
    b0sb = SB("b0sb", [128, NNCH], F32)
    b1sb = SB("b1sb", [128, NCH], F32)
    b1h = SB("b1h", [128, NCH], F32)
    id32sb = SB("id32sb", [128, 128], F32)
    id16sb = SB("id16sb", [128, 128], F16)
    ones2sb = SB("ones2sb", [2, 128], F16)
    ones1sb = SB("ones1sb", [1, 128], F16)
    onescolsb = SB("onescolsb", [128, 1], F16)
    ones32rsb = SB("ones32rsb", [1, 128], F32)
    ones32csb = SB("ones32csb", [128, 1], F32)
    invccrsb = SB("invccrsb", [1, 128], F32)
    asum = SB("asum", [128, 24], F32)
    swsc = SB("swsc", [1, 16], F32)
    swb = SB("swb", [128, 8], F32)
    axinv = SB("axinv", [128, 8], F32)
    zbuf = SB("zbuf", [128, 128], F32)
    z127i = SB("z127i", [128, 128], F32)
    mwork = SB("mwork", [128, 16], F32)
    mtile = SB("mtile", [128, 2], F16)
    mrow = SB("mrow", [2, 512], F16)
    uq = SB("uq", [128, 4], F32)
    mxq = SB("mxq", [128, 4], F32)
    pwork = SB("pwork", [128, 128], F32)
    ywork = SB("ywork", [128, 32], F32)
    ysc = SB("ysc", [128, 512], F32)
    ln127sb = SB("ln127sb", [128, 1], F32)

    def TS(stream, out, in0, s1, op0, s2=None, op1=None, accum=None, waits=(), inc=None):
        def fn(eng, o=out, i=in0, a=s1, b=s2, p0=op0, p1=op1, ac=accum):
            if p1 is None and ac is None:
                return eng.tensor_scalar(o, i, a, None, p0)
            return eng.tensor_scalar(o, i, a, b, p0, p1 if p1 is not None else ALU.bypass,
                                     accum_out=ac)
        return K.emit(stream, fn, waits=waits, inc=inc)

    def VW():
        return [(sV, K.c(sV))]

    def DMA(q, out, in_, waits=(), sem=None):
        if sem is None:
            sem = sW if q == "gpsimd" else sL
        return K.emit(q, lambda eng, o=out, i=in_: eng.dma_start(out=o, in_=i),
                      waits=waits, inc=sem, incn=16)

    def p0_loads():
        DMA("sync", xres[:, :].rearrange("p (cc t) -> p cc t", t=TCORE),
            xt.rearrange("(cc p) t -> p cc t", p=128))
        for (dst, src) in [(b0sb, b0m), (b1sb, b1m), (id32sb, id32), (id16sb, id16),
                           (ones2sb, ones2), (ones1sb, ones1r), (onescolsb, onescol),
                           (ones32rsb, ones32r), (ones32csb, ones32c), (invccrsb, invccr)]:
            DMA("sync", dst[:, :], src)
        ld = K.c(sL)
        TS("vector", b1h[:, :], b1sb[:, :], 0.5, ALU.mult, waits=[(sL, ld)], inc=sV)
        K.emit("vector", lambda eng: eng.memset(ln127sb[:, :], LN127), inc=sV)
        return ld

    def ln_stats(pstat, pbc, xsq, stat, statrow, ld):
        xsq_free = {}
        for cc in range(NCH):
            TS("vector", xn16[:, cc * TCORE:(cc + 1) * TCORE],
               xres[:, cc * TCORE:(cc + 1) * TCORE], 1.0, ALU.mult,
               waits=[(sL, ld), (sT, xsq_free.get(cc - 2, 0))], inc=sV)
            K.emit("vector", lambda eng, c=cc: eng.tensor_mul(
                xsq[:, (c % 2) * TCORE:(c % 2 + 1) * TCORE],
                xn16[:, c * TCORE:(c + 1) * TCORE],
                xn16[:, c * TCORE:(c + 1) * TCORE]), inc=sV)
            vv = K.c(sV)
            for hf in range(2):
                K.emit("tensor", lambda eng, c=cc, h=hf: eng.matmul(
                    pstat[0:1, h * 512:(h + 1) * 512], onescolsb[:, :],
                    xn16[:, c * TCORE + h * 512: c * TCORE + (h + 1) * 512],
                    start=(c == 0), stop=(c == NCH - 1)),
                    waits=[(sV, vv)], inc=sT)
                K.emit("tensor", lambda eng, c=cc, h=hf: eng.matmul(
                    pstat[0:1, 1024 + h * 512: 1024 + (h + 1) * 512], onescolsb[:, :],
                    xsq[:, (c % 2) * TCORE + h * 512:(c % 2) * TCORE + (h + 1) * 512],
                    start=(c == 0), stop=(c == NCH - 1)), inc=sT)
            xsq_free[cc] = K.c(sT)
        tmm = K.c(sT)
        K.emit("scalar", lambda eng: eng.copy(stat[0:1, 0:2048], pstat[0:1, 0:2048]),
               waits=[(sT, tmm)], inc=sA)
        aev = K.c(sA)
        MEAN, EX2, MSQ, VAR, VEP, RVA = 2048, 3072, 4096, 5120, 6144, 7168
        TS("vector", stat[0:1, MEAN:MEAN + 1024], stat[0:1, 0:1024], 1.0 / C, ALU.mult, waits=[(sA, aev)], inc=sV)
        TS("vector", stat[0:1, EX2:EX2 + 1024], stat[0:1, 1024:2048], 1.0 / C, ALU.mult, inc=sV)
        K.emit("vector", lambda eng: eng.tensor_mul(stat[0:1, MSQ:MSQ + 1024], stat[0:1, MEAN:MEAN + 1024], stat[0:1, MEAN:MEAN + 1024]), inc=sV)
        K.emit("vector", lambda eng: eng.tensor_sub(stat[0:1, VAR:VAR + 1024], stat[0:1, EX2:EX2 + 1024], stat[0:1, MSQ:MSQ + 1024]), inc=sV)
        TS("vector", stat[0:1, VEP:VEP + 1024], stat[0:1, VAR:VAR + 1024], 1e-5, ALU.add, inc=sV)
        K.emit("vector", lambda eng: eng.reciprocal(stat[0:1, RVA:RVA + 1024], stat[0:1, VEP:VEP + 1024]), inc=sV)
        vv = K.c(sV)
        K.emit("scalar", lambda eng: eng.activation(stat[0:1, MSQ:MSQ + 1024], stat[0:1, RVA:RVA + 1024], AF.Sqrt),
               waits=[(sV, vv)], inc=sA)
        asq = K.c(sA)
        TS("vector", statrow[0:1, 0:1024], stat[0:1, MEAN:MEAN + 1024], -1.0, ALU.mult, waits=[(sA, asq)], inc=sV)
        K.emit("vector", lambda eng: eng.tensor_copy(statrow[0:1, 1024:2048], stat[0:1, MSQ:MSQ + 1024]), inc=sV)
        vv = K.c(sV)
        for hf in range(2):
            K.emit("tensor", lambda eng, h=hf: eng.matmul(
                pbc[:, h * 512:(h + 1) * 512], ones1sb[:, :],
                statrow[0:1, h * 512:(h + 1) * 512], start=True, stop=True),
                waits=[(sV, vv)], inc=sT)
            K.emit("tensor", lambda eng, h=hf: eng.matmul(
                pbc[:, 1024 + h * 512: 1024 + (h + 1) * 512], ones1sb[:, :],
                statrow[0:1, 1024 + h * 512: 1024 + (h + 1) * 512], start=True, stop=True), inc=sT)
        tb = K.c(sT)
        K.emit("scalar", lambda eng: eng.copy(bm16[:, :], pbc[:, 0:1024]), waits=[(sT, tb)], inc=sA)
        K.emit("scalar", lambda eng: eng.copy(br16[:, :], pbc[:, 1024:2048]), inc=sA)
        return K.c(sA)

    def ln_apply(bc):
        for cc in range(NCH):
            e, sem = (("gpsimd", sG) if cc % 2 == 0 else ("vector", sV))
            K.emit(e, lambda eng, c=cc: eng.tensor_add(
                xn16[:, c * TCORE:(c + 1) * TCORE], xres[:, c * TCORE:(c + 1) * TCORE],
                bm16[:, :]), waits=[(sA, bc)], inc=sem)
            K.emit(e, lambda eng, c=cc: eng.tensor_mul(
                xn16[:, c * TCORE:(c + 1) * TCORE], xn16[:, c * TCORE:(c + 1) * TCORE],
                br16[:, :]), inc=sem)
        return K.c(sG), K.c(sV)

    def make_absum_pump(pbc, ab32):
        jobs = []
        for (mi, wsrc) in enumerate((wqt, wkt, wvt, wpt)):
            for j in range(16):
                jobs.append((mi, j, wsrc))
        state = {"i": 0, "red": {}, "pa": 0}

        def pump(n=1):
            for _ in range(n):
                i = state["i"]
                if i >= len(jobs):
                    return
                mi, j, wsrc = jobs[i]
                ldn = DMA("sync", ab32[:, 0:C],
                          wsrc[j * 128:(j + 1) * 128, :],
                          waits=[(sV, state["red"].get(i - 1, 0))], sem="ab")
                K.emit("vector", lambda eng, jj=j: eng.reduce_sum(
                    asum[:, jj:jj + 1], ab32[:, 0:C], axis=AX,
                    apply_absolute_value=True), waits=[("ab", ldn)], inc=sV)
                state["red"][i] = K.c(sV)
                state["i"] = i + 1
                if (i + 1) % 16 == 0:
                    m = i // 16
                    K.emit("vector", lambda eng: eng.reduce_sum(
                        asum[:, 16:17], asum[:, 0:16], axis=AX), waits=VW(), inc=sV)
                    vv = K.c(sV)
                    K.emit("tensor", lambda eng: eng.matmul(
                        pbc[0:1, 512:513], asum[:, 16:17], ones32csb[:, :],
                        start=True, stop=True), waits=[(sV, vv), (sA, state["pa"])], inc=sT)
                    tv = K.c(sT)
                    K.emit("scalar", lambda eng, mm=m: eng.copy(swsc[0:1, mm:mm + 1], pbc[0:1, 512:513]),
                           waits=[(sT, tv)], inc=sA)
                    av = K.c(sA)
                    K.emit("tensor", lambda eng, mm=m: eng.matmul(
                        pbc[:, 0:1], invccrsb[:, :], swsc[0:1, mm:mm + 1],
                        start=True, stop=True), waits=[(sA, av)], inc=sT)
                    tv2 = K.c(sT)
                    K.emit("scalar", lambda eng: eng.copy(asum[:, 20:21], pbc[:, 0:1]),
                           waits=[(sT, tv2)], inc=sA)
                    av2 = K.c(sA)
                    state["pa"] = av2
                    TS("vector", swb[:, 2 * m:2 * m + 1], asum[:, 20:21], EPS, ALU.max,
                       waits=[(sA, av2)] + VW(), inc=sV)
                    K.emit("vector", lambda eng, mm=m: eng.reciprocal(
                        swb[:, 2 * mm + 1:2 * mm + 2], swb[:, 2 * mm:2 * mm + 1]),
                        waits=VW(), inc=sV)
        return pump

    def ffn(dtile, wreg, ehalf, pd, pe, pump=None, first=False):

        wreg_free = [0, 0]
        wb_n = [0, 0]
        pd_free = [0, 0]
        pe_free = [0, 0]
        eh_free = [0, 0]
        for th in range(2):
            fp32src = first and th == 0
            for sl in range(32):
                buf = sl % 2
                wv = wreg[:, buf * 4096:(buf + 1) * 4096]
                if fp32src:
                    wn = DMA("gpsimd", wv, w0t[:, sl * 4096:(sl + 1) * 4096],
                             waits=[(sT, wreg_free[buf]), ("wb%d" % buf, wb_n[buf])],
                             sem="w%d" % buf)
                    wb_n[buf] = DMA("sync", w0f16[:, sl * 4096:(sl + 1) * 4096], wv,
                                    waits=[("w%d" % buf, wn)], sem="wb%d" % buf)
                else:
                    wn = DMA("sync", wv, w0f16[:, sl * 4096:(sl + 1) * 4096],
                             waits=[(sT, wreg_free[buf]),
                                    ("wb0", K.c("wb0")), ("wb1", K.c("wb1"))],
                             sem="w%d" % buf)
                if pump:
                    pump(2)
                for nb in range(2):
                    n = sl * 2 + nb
                    ps = n % 2
                    for ccc in range(NCH):
                        K.emit("tensor", lambda eng, b=buf, q=nb, c=ccc, p=ps, t=th: eng.matmul(
                            pd[:, p * 512:(p + 1) * 512],
                            wreg[:, b * 4096 + c * 256 + q * 128: b * 4096 + c * 256 + q * 128 + 128],
                            xn16[:, c * TCORE + t * 512: c * TCORE + (t + 1) * 512],
                            start=(c == 0), stop=(c == NCH - 1)),
                            waits=([("w%d" % buf, wn), (sA, pd_free[ps])] if ccc == 0 else []),
                            inc=sT)
                    tn = K.c(sT)
                    K.emit("scalar", lambda eng, nn=n, p=ps: eng.activation(
                        dtile[:, nn * 512:(nn + 1) * 512], pd[:, p * 512:(p + 1) * 512],
                        AF.Gelu, bias=b0sb[:, nn:nn + 1], scale=1.0),
                        waits=[(sT, tn)], inc=sA)
                    pd_free[ps] = K.c(sA)
                wreg_free[buf] = K.c(sT)
            dready = K.c(sA)
            if DBG and th == 0:
                DMA("sync", dbg_d.rearrange("(nn p) t -> p nn t", p=128),
                    dtile[:, :].rearrange("p (nn t) -> p nn t", t=512),
                    waits=[(sA, dready)])
            for cc in range(NCH):
                es = cc % 2
                for hv in range(2):
                    buf = (cc * 2 + hv) % 2
                    wv = wreg[:, buf * 4096:(buf + 1) * 4096]
                    w1ix = cc * 2 + hv
                    if fp32src:
                        wn = DMA("gpsimd", wv, w1t[:, w1ix * 4096:(w1ix + 1) * 4096],
                                 waits=[(sT, wreg_free[buf]), ("wb%d" % buf, wb_n[buf])],
                                 sem="w%d" % buf)
                        wb_n[buf] = DMA("sync", w1f16[:, w1ix * 4096:(w1ix + 1) * 4096],
                                        wv, waits=[("w%d" % buf, wn)], sem="wb%d" % buf)
                    else:
                        wn = DMA("sync", wv, w1f16[:, w1ix * 4096:(w1ix + 1) * 4096],
                                 waits=[(sT, wreg_free[buf]),
                                        ("wb0", K.c("wb0")), ("wb1", K.c("wb1"))],
                                 sem="w%d" % buf)
                    if pump:
                        pump(2)
                    for nb in range(32):
                        n = hv * 32 + nb
                        K.emit("tensor", lambda eng, b=buf, q=nb, nn=n, e2=es, t=th: eng.matmul(
                            pe[:, e2 * 512:(e2 + 1) * 512],
                            wreg[:, b * 4096 + q * 128: b * 4096 + q * 128 + 128],
                            dtile[:, nn * 512:(nn + 1) * 512],
                            start=(nn == 0), stop=(nn == NNCH - 1)),
                            waits=([("w%d" % buf, wn), (sA, max(dready, pe_free[es])),
                                    (sV, eh_free[es])] if nb == 0 else []),
                            inc=sT)
                    wreg_free[buf] = K.c(sT)
                tn = K.c(sT)
                K.emit("scalar", lambda eng, c=cc, e2=es: eng.activation(
                    ehalf[:, e2 * 512:(e2 + 1) * 512], pe[:, e2 * 512:(e2 + 1) * 512],
                    AF.Identity, bias=b1h[:, c:c + 1], scale=0.5),
                    waits=[(sT, tn)], inc=sA)
                pe_free[es] = K.c(sA)
                an = pe_free[es]
                K.emit("vector", lambda eng, c=cc, e2=es, t=th: eng.tensor_add(
                    xres[:, c * TCORE + t * 512: c * TCORE + (t + 1) * 512],
                    xres[:, c * TCORE + t * 512: c * TCORE + (t + 1) * 512],
                    ehalf[:, e2 * 512:(e2 + 1) * 512]),
                    waits=[(sA, an)], inc=sV)
                eh_free[es] = K.c(sV)
        if pump:
            pump(64)

    with nc.Block() as blk0:
        ld0 = p0_loads()
        K.flush(blk0)

    def ln_phase(ld):
        with (nc.psum_tensor(NM("pstat"), [2, 2048], F32) as pstat,
              nc.psum_tensor(NM("pbcp"), [128, 2048], F32) as pbcp,
              nc.sbuf_tensor(NM("xsq"), [128, 2 * TCORE], F16) as xsq,
              nc.sbuf_tensor(NM("stat"), [1, 8192], F32) as stat,
              nc.sbuf_tensor(NM("statrow"), [1, 2048], F16) as statrow,
              nc.Block() as blk):
            bc = ln_stats(pstat, pbcp, xsq, stat, statrow, ld)
            ln_apply(bc)
            K.barrier()
            K.flush(blk)

    def ffn_phase(with_absums, first=False):
        with (nc.sbuf_tensor(NM("dtile"), [128, NNCH * 512], F16) as dtile,
              nc.sbuf_tensor(NM("wreg"), [128, 8192], F16) as wreg,
              nc.sbuf_tensor(NM("ehalf"), [128, 1024], F16) as ehalf,
              nc.psum_tensor(NM("pd"), [128, 1024], F32) as pd,
              nc.psum_tensor(NM("pe"), [128, 1024], F32) as pe,
              nc.Block() as blk):
            pump = None
            if with_absums:
                with (nc.sbuf_tensor(NM("ab32"), [128, C], F32) as ab32,
                      nc.psum_tensor(NM("pbcf"), [128, 1024], F32) as pbcf):
                    pump = make_absum_pump(pbcf, ab32)
                    ffn(dtile, wreg, ehalf, pd, pe, pump, first=first)
                    K.barrier()
                    K.flush(blk)
            else:
                ffn(dtile, wreg, ehalf, pd, pe, None, first=first)
                K.barrier()
                K.flush(blk)

    _FIRST_FFN = [True]
    ln_phase(ld0)
    if DBG:
        with nc.Block() as blkd:
            DMA("sync", dbg_xn.rearrange("(cc p) t -> p cc t", p=128),
                xn16[:, :].rearrange("p (cc t) -> p cc t", t=TCORE),
                waits=[(sV, K.c(sV)), (sG, K.c(sG))])
            K.flush(blkd)
    ffn_phase(with_absums=(stage >= 2), first=True)

    if stage >= 2:
        def p3_ln2_quant():
            ld = 0
            with (nc.psum_tensor(NM("pstat"), [2, 2048], F32) as pstat,
                  nc.psum_tensor(NM("pbcp"), [128, 2048], F32) as pbcp,
                  nc.sbuf_tensor(NM("xsq"), [128, 2 * TCORE], F16) as xsq,
                  nc.sbuf_tensor(NM("stat"), [1, 8192], F32) as stat,
                  nc.sbuf_tensor(NM("statrow"), [1, 2048], F16) as statrow,
                  nc.Block() as blk):
                bc = ln_stats(pstat, pbcp, xsq, stat, statrow, ld)
                K.barrier()
                K.flush(blk)
            with (nc.psum_tensor(NM("pt32"), [128, 2048], F32) as pt32,
                  nc.sbuf_tensor(NM("xn32t"), [128, 2048], F32) as xn32t,
                  nc.sbuf_tensor(NM("xqi16t"), [128, 2048], I16) as xqi16t,
                  nc.sbuf_tensor(NM("xq32t"), [128, 2048], F32) as xq32t,
                  nc.Block() as blk):
                pt_free = 0
                xn32_free = 0
                for tc in range(8):
                    for cc in range(NCH):
                        K.emit("vector", lambda eng, c=cc, t=tc: eng.tensor_add(
                            xn32t[:, c * 128:(c + 1) * 128],
                            xres[:, c * TCORE + t * 128: c * TCORE + t * 128 + 128],
                            bm16[:, t * 128:(t + 1) * 128]),
                            waits=[(sT, xn32_free)], inc=sV)
                        K.emit("vector", lambda eng, c=cc, t=tc: eng.tensor_mul(
                            xn32t[:, c * 128:(c + 1) * 128], xn32t[:, c * 128:(c + 1) * 128],
                            br16[:, t * 128:(t + 1) * 128]), waits=VW(), inc=sV)
                    vn = K.c(sV)
                    for cc in range(NCH):
                        K.emit("tensor", lambda eng, c=cc: eng.transpose(
                            pt32[:, c * 128:(c + 1) * 128], xn32t[:, c * 128:(c + 1) * 128],
                            id32sb[:, :]), waits=[(sV, max(vn, pt_free))], inc=sT)
                    xn32_free = K.c(sT)
                    tn = K.c(sT)
                    K.emit("vector", lambda eng: eng.reduce_max(
                        pwork[:, 0:1], pt32[:, :], axis=AX, apply_absolute_value=True),
                        waits=[(sT, tn)], inc=sV)
                    TS("vector", pwork[:, 1:2], pwork[:, 0:1], EPS, ALU.max, waits=VW(), inc=sV)
                    K.emit("vector", lambda eng: eng.reciprocal(pwork[:, 2:3], pwork[:, 1:2]),
                           waits=VW(), inc=sV)
                    TS("vector", pwork[:, 3:4], pwork[:, 2:3], 127.0, ALU.mult, waits=VW(), inc=sV)
                    TS("vector", axinv[:, tc:tc + 1], pwork[:, 1:2], 1.0 / 127.0, ALU.mult,
                       waits=VW(), inc=sV)
                    vq = K.c(sV)
                    K.emit("scalar", lambda eng: eng.activation(
                        xqi16t[:, :], pt32[:, :], AF.Copy, bias=0.0, scale=pwork[:, 3:4]),
                        waits=[(sV, vq)], inc=sA)
                    an = K.c(sA)
                    K.emit("vector", lambda eng: eng.tensor_copy(xq32t[:, :], xqi16t[:, :]),
                           waits=[(sA, an)], inc=sV)
                    vc = K.c(sV)
                    for cc in range(NCH):
                        K.emit("tensor", lambda eng, c=cc: eng.transpose(
                            pt32[:, c * 128:(c + 1) * 128], xq32t[:, c * 128:(c + 1) * 128],
                            id32sb[:, :]), waits=[(sV, vc)], inc=sT)
                    tb = K.c(sT)
                    for cc in range(NCH):
                        K.emit("vector", lambda eng, c=cc, t=tc: eng.tensor_copy(
                            xn16[:, c * TCORE + t * 128: c * TCORE + t * 128 + 128],
                            pt32[:, c * 128:(c + 1) * 128]),
                            waits=[(sT, tb)], inc=sV)
                    pt_free = K.c(sV)
                K.barrier()
                K.flush(blk)

        RG = [[0, 1, 2, 3], [4, 5, 6, 7]]

        def tern_resident(mi, wsrc, wdst, f32buf, pqi16):
            acts = {}
            dves = {}
            for j in range(NCH):
                b = j % 2
                ldn = DMA("gpsimd", f32buf[:, 0:2048],
                          wsrc[j * 128:(j + 1) * 128, :],
                          waits=[(sA, acts.get(j - 1, 0))], sem="tn%d" % b)
                K.emit("scalar", lambda eng, m=mi, bb=b: eng.activation(
                    pqi16[:, bb * 2048:(bb + 1) * 2048], f32buf[:, 0:2048],
                    AF.Copy, bias=0.0, scale=swb[:, 2 * m + 1:2 * m + 2]),
                    waits=[("tn%d" % b, ldn), (sV, dves.get(j - 2, 0))], inc=sA)
                acts[j] = K.c(sA)
                TS("vector", wdst[:, j * 2048:(j + 1) * 2048], pqi16[:, b * 2048:(b + 1) * 2048],
                   -1.0, ALU.max, 1.0, ALU.min, waits=[(sA, acts[j])], inc=sV)
                dves[j] = K.c(sV)
            return dves

        def tern_jobs(mi, wsrc, hdst, f32buf, pqi16hi, f16buf, w0=None):
            st = {"acts": {}, "dves": {}, "outs": {}}

            def mk(j):
                def go():
                    b = j % 2
                    ldn = DMA("gpsimd", f32buf[:, 0:2048],
                              wsrc[j * 128:(j + 1) * 128, :],
                              waits=[(sA, st["acts"].get(j - 1, 0))] + ([w0] if w0 and j < 1 else []),
                              sem="tn%d" % b)
                    K.emit("scalar", lambda eng, m=mi, bb=b: eng.activation(
                        pqi16hi[:, :], f32buf[:, 0:2048],
                        AF.Copy, bias=0.0, scale=swb[:, 2 * m + 1:2 * m + 2]),
                        waits=[("tn%d" % b, ldn), (sV, st["dves"].get(j - 1, 0))], inc=sA)
                    st["acts"][j] = K.c(sA)
                    TS("vector", f16buf[:, 0:2048], pqi16hi[:, :],
                       -1.0, ALU.max, 1.0, ALU.min,
                       waits=[(sA, st["acts"][j]), ("th0", st["outs"].get(j - 1, 0))], inc=sV)
                    st["dves"][j] = K.c(sV)
                    st["outs"][j] = DMA("sync", hdst[j * 128:(j + 1) * 128, :],
                                        f16buf[:, 0:2048],
                                        waits=[(sV, st["dves"][j])], sem="th0")
                return go
            return [mk(j) for j in range(NCH)]

        def proj_quant_evac(tc, pgv, pqi16, mi, dq_extra, out_cb, tn=None):
            if tn is None:
                tn = K.c(sT)
            for seg in range(H):
                K.emit("vector", lambda eng, s=seg: eng.reduce_max(
                    pwork[:, s:s + 1], pgv[:, s * 128:(s + 1) * 128], axis=AX,
                    apply_absolute_value=True), waits=[(sT, tn)], inc=sV)
            K.emit("vector", lambda eng, m=mi, t=tc: eng.tensor_mul(
                pwork[:, 16:17], swb[:, 2 * m:2 * m + 1], axinv[:, t:t + 1]), waits=VW(), inc=sV)
            TS("vector", pwork[:, 17:33], pwork[:, 0:16], pwork[:, 16:17], ALU.mult,
               waits=VW(), inc=sV)
            TS("vector", pwork[:, 33:49], pwork[:, 17:33], EPS, ALU.max, waits=VW(), inc=sV)
            K.emit("vector", lambda eng: eng.reciprocal(pwork[:, 49:65], pwork[:, 33:49]),
                   waits=VW(), inc=sV)
            TS("vector", pwork[:, 65:81], pwork[:, 49:65], pwork[:, 16:17], ALU.mult,
               s2=127.0, op1=ALU.mult, waits=VW(), inc=sV)
            TS("vector", pwork[:, 81:97], pwork[:, 33:49], dq_extra / 127.0, ALU.mult,
               waits=VW(), inc=sV)
            vq = K.c(sV)
            aprev = 0
            for seg in range(H):
                K.emit("scalar", lambda eng, s=seg: eng.activation(
                    pqi16[:, s * 128:(s + 1) * 128], pgv[:, s * 128:(s + 1) * 128],
                    AF.Copy, bias=0.0, scale=pwork[:, 65 + s:66 + s]),
                    waits=[(sV, vq)], inc=sA)
            an = K.c(sA)
            for seg in range(H):
                out_cb(seg, pqi16[:, seg * 128:(seg + 1) * 128], pwork[:, 81 + seg:82 + seg], an, pgv)
            return an

        def proj_phase(mi, rhs_tiles, pg, pqi16, dq_extra, out_cb, side_jobs=None,
                       evac_fn=None):
            if evac_fn is None:
                evac_fn = proj_quant_evac
            """TM projection: G[tc] = xq_blocks.T @ rhs; quant; out_cb per seg.
            rhs_tiles: fn(j) -> (ap [128, 2048], wait_tuple or None)."""
            pg_free = [0, 0]
            prev = None
            for tc in range(8):
                par = tc % 2
                for j in range(NCH):
                    rap, rw = rhs_tiles(j)
                    w = []
                    if j == 0:
                        w.append((sV, pg_free[par]))
                    if rw is not None:
                        w.append(rw)
                    for isl in range(4):
                        K.emit("tensor", lambda eng, jj=j, t=tc, p=par, q=isl, r=rap: eng.matmul(
                            pg[:, p * 2048 + q * 512: p * 2048 + (q + 1) * 512],
                            xn16[:, jj * TCORE + t * 128: jj * TCORE + t * 128 + 128],
                            r[:, q * 512:(q + 1) * 512],
                            start=(jj == 0), stop=(jj == NCH - 1)),
                            waits=(w if isl == 0 else []), inc=sT)
                tn_now = K.c(sT)
                if prev is not None:
                    evac_fn(prev, pg[:, (prev % 2) * 2048:(prev % 2 + 1) * 2048],
                            pqi16, mi, dq_extra, out_cb, tn=tn_prev)
                    pg_free[prev % 2] = K.c(sV)
                    if side_jobs:
                        for _ in range(2):
                            if side_jobs:
                                side_jobs.pop(0)()
                prev = tc
                tn_prev = tn_now
            evac_fn(prev, pg[:, (prev % 2) * 2048:(prev % 2 + 1) * 2048],
                    pqi16, mi, dq_extra, out_cb, tn=tn_prev)
            pg_free[prev % 2] = K.c(sV)
            while side_jobs:
                side_jobs.pop(0)()

        def p4_projections():
            kasm = SB("kasm", [128, 2048], F16)
            with (nc.sbuf_tensor(NM("wres"), [128, NCH * 2048], F16) as wres,
                  nc.sbuf_tensor(NM("f32buf"), [128, 2048], F32) as f32buf,
                  nc.sbuf_tensor(NM("pqi16"), [128, 4096], I16) as pqi16,
                  nc.sbuf_tensor(NM("sc32"), [128, 2048], F32) as sc32,
                  nc.sbuf_tensor(NM("f16b"), [128, 2048], F16) as f16b,
                  nc.psum_tensor(NM("pg"), [128, 4096], F32) as pg,
                  nc.Block() as blk):
                tden = tern_resident(1, wkt, wres, f32buf, pqi16)
                tjobs = tern_jobs(0, wqt, wqq, f32buf, pqi16[:, 2048:4096], f16b,
                                  w0=(sV, tden[NCH - 1]))
                ck = {"n": 0}

                def k_evac(tc, pgv, pqi16_, mi, dq_extra, out_cb, tn=None):
                    if tn is None:
                        tn = K.c(sT)
                    for i in range(16):
                        K.emit("vector", lambda eng, ii=i, pp=pgv: eng.reduce_max(
                            pwork[:, ii:ii + 1], pp[:, ii * 128:(ii + 1) * 128],
                            axis=AX, apply_absolute_value=True),
                            waits=([(sT, tn)] if i == 0 else []), inc=sV)
                    K.emit("vector", lambda eng, m=mi, t=tc: eng.tensor_mul(
                        pwork[:, 16:17], swb[:, 2 * m:2 * m + 1], axinv[:, t:t + 1]), inc=sV)
                    TS("vector", pwork[:, 17:33], pwork[:, 0:16], pwork[:, 16:17],
                       ALU.mult, waits=VW(), inc=sV)
                    TS("vector", pwork[:, 33:49], pwork[:, 17:33], EPS, ALU.max, waits=VW(), inc=sV)
                    K.emit("vector", lambda eng: eng.reciprocal(pwork[:, 49:65], pwork[:, 33:49]),
                           waits=VW(), inc=sV)
                    TS("vector", pwork[:, 65:81], pwork[:, 49:65], pwork[:, 16:17],
                       ALU.mult, s2=127.0, op1=ALU.mult, waits=VW(), inc=sV)
                    vq = K.c(sV)
                    TS("vector", pwork[:, 81:97], pwork[:, 33:49], dq_extra / 127.0, ALU.mult,
                       waits=VW(), inc=sV)
                    for i in range(16):
                        K.emit("scalar", lambda eng, ii=i, pp=pgv: eng.activation(
                            pqi16_[:, ii * 128:(ii + 1) * 128], pp[:, ii * 128:(ii + 1) * 128],
                            AF.Copy, bias=0.0, scale=pwork[:, 65 + ii:66 + ii]),
                            waits=([(sV, vq)] if i == 0 else []), inc=sA)
                    an = K.c(sA)
                    abase = an - 16
                    for i in range(16):
                        K.emit("vector", lambda eng, ii=i: eng.tensor_scalar(
                            sc32[:, ii * 128:(ii + 1) * 128], pqi16_[:, ii * 128:(ii + 1) * 128],
                            pwork[:, 81 + ii:82 + ii], None, ALU.mult),
                            waits=[(sA, abase + i + 1)] + (VW() if i == 0 else []), inc=sV)
                    vbase = K.c(sV) - 16
                    tbase = K.c(sT)
                    for b4 in range(4):
                        for g in range(4):
                            i = b4 * 4 + g
                            K.emit("tensor", lambda eng, ii=i, pp=pgv: eng.transpose(
                                pp[:, ii * 128:(ii + 1) * 128],
                                sc32[:, ii * 128:(ii + 1) * 128], id32sb[:, :]),
                                waits=[(sV, vbase + i + 1), (sA, abase + b4 * 4 + 4)], inc=sT)
                        for g in range(4):
                            i = b4 * 4 + g
                            K.emit("vector", lambda eng, ii=i, pp=pgv: eng.tensor_copy(
                                kasm[:, ii * 128:(ii + 1) * 128], pp[:, ii * 128:(ii + 1) * 128]),
                                waits=[(sT, tbase + b4 * 4 + 4)]
                                + ([("ck", ck["n"])] if i == 0 else []), inc=sV)
                    vn = K.c(sV)
                    ck["n"] = DMA("sync",
                                  bass.AP(cck_in, tc * 128, [[1024, 128], [1024 * 128, 16], [1, 128]]),
                                  kasm[:, :].rearrange("p (ic d) -> p ic d", d=128),
                                  waits=[(sV, vn)], sem="ck")

                proj_phase(1, lambda j: (wres[:, j * 2048:(j + 1) * 2048],
                                         (sV, tden[j])), pg, pqi16, 1.0, None,
                           side_jobs=tjobs, evac_fn=k_evac)
                K.emit("gpsimd", lambda eng: eng.collective_compute(
                    "AllGather", ALU.bypass, replica_groups=RG,
                    ins=[cck_in.ap().opt()], outs=[cck_out.ap().opt()]),
                    waits=[("ck", ck["n"])], inc=sCC)
                K.barrier()
                K.flush(blk)
            with (nc.sbuf_tensor(NM("wres"), [128, NCH * 2048], F16) as wres,
                  nc.sbuf_tensor(NM("f32buf"), [128, 2048], F32) as f32buf,
                  nc.sbuf_tensor(NM("pqi16"), [128, 4096], I16) as pqi16,
                  nc.sbuf_tensor(NM("vst16"), [128, 2048], F16) as vst16,
                  nc.sbuf_tensor(NM("f16b"), [128, 2048], F16) as f16b,
                  nc.psum_tensor(NM("pg"), [128, 4096], F32) as pg,
                  nc.Block() as blk):
                tden = tern_resident(2, wvt, wres, f32buf, pqi16)
                tjobs = tern_jobs(3, wpt, wpq, f32buf, pqi16[:, 2048:4096], f16b,
                                  w0=(sV, tden[NCH - 1]))
                cv = {"n": 0}
                tc_idx = {"tc": 0}

                def v_out(seg, i16sl, dqcol, an, pgv):
                    K.emit("vector", lambda eng, s=i16sl, d=dqcol, g=seg: eng.tensor_scalar(
                        vst16[:, g * 128:(g + 1) * 128], s, d, None, ALU.mult),
                        waits=[(sA, an)] + (VW() + [("cv", cv["n"])] if seg == 0 else []),
                        inc=sV)
                    if seg == H - 1:
                        tc = tc_idx["tc"]
                        vn = K.c(sV)
                        cv["n"] = DMA("sync", ccv_in.ap()[tc * 128:(tc + 1) * 128, :],
                                      vst16[:, :], waits=[(sV, vn)], sem="cv")
                        tc_idx["tc"] = tc + 1

                proj_phase(2, lambda j: (wres[:, j * 2048:(j + 1) * 2048],
                                         (sV, tden[j])), pg, pqi16, 1.0, v_out,
                           side_jobs=tjobs)
                K.emit("gpsimd", lambda eng: eng.collective_compute(
                    "AllGather", ALU.bypass, replica_groups=RG,
                    ins=[ccv_in.ap().opt()], outs=[ccv_out.ap().opt()]),
                    waits=[("cv", cv["n"])], inc=sCC)
                K.barrier()
                K.flush(blk)

        def p4_qproj(qst):
            with (nc.sbuf_tensor(NM("wqbuf"), [128, 2 * 512], F16) as wqbuf,
                  nc.sbuf_tensor(NM("pqi16"), [128, 2048], I16) as pqi16,
                  nc.sbuf_tensor(NM("sc32"), [128, 2048], F32) as sc32,
                  nc.psum_tensor(NM("pg"), [128, 4096], F32) as pg,
                  nc.Block() as blk):
                wq_free = [0, 0]
                wq_n = [0, 0]
                pg_free = [0, 0]

                def qevac(q, hf, tn):
                    for i in range(16):
                        tci, sgi = i // 4, i % 4
                        K.emit("vector", lambda eng, ii=i, b=hf * 4 + i // 4, g=i % 4: eng.reduce_max(
                            pwork[:, ii:ii + 1], pg[:, b * 512 + g * 128: b * 512 + (g + 1) * 128],
                            axis=AX, apply_absolute_value=True),
                            waits=([(sT, tn)] if i == 0 else []), inc=sV)
                    for tci in range(4):
                        K.emit("vector", lambda eng, t=hf * 4 + tci, c=tci: eng.tensor_mul(
                            pwork[:, 16 + c:17 + c], swb[:, 0:1], axinv[:, t:t + 1]), inc=sV)
                    for tci in range(4):
                        TS("vector", pwork[:, 20 + tci * 4:24 + tci * 4],
                           pwork[:, tci * 4:tci * 4 + 4], pwork[:, 16 + tci:17 + tci],
                           ALU.mult, waits=(VW() if tci == 0 else []), inc=sV)
                    TS("vector", pwork[:, 36:52], pwork[:, 20:36], EPS, ALU.max, waits=VW(), inc=sV)
                    K.emit("vector", lambda eng: eng.reciprocal(pwork[:, 52:68], pwork[:, 36:52]),
                           waits=VW(), inc=sV)
                    for tci in range(4):
                        TS("vector", pwork[:, 68 + tci * 4:72 + tci * 4],
                           pwork[:, 52 + tci * 4:56 + tci * 4], pwork[:, 16 + tci:17 + tci],
                           ALU.mult, s2=127.0, op1=ALU.mult,
                           waits=(VW() if tci == 0 else []), inc=sV)
                    vq = K.c(sV)
                    TS("vector", pwork[:, 84:100], pwork[:, 36:52], ISQD / 127.0, ALU.mult,
                       waits=VW(), inc=sV)
                    for i in range(16):
                        K.emit("scalar", lambda eng, ii=i, b=hf * 4 + i // 4, g=i % 4: eng.activation(
                            pqi16[:, ii * 128:(ii + 1) * 128],
                            pg[:, b * 512 + g * 128: b * 512 + (g + 1) * 128],
                            AF.Copy, bias=0.0, scale=pwork[:, 68 + ii:69 + ii]),
                            waits=([(sV, vq)] if i == 0 else []), inc=sA)
                    an = K.c(sA)
                    abase = an - 16
                    for i in range(16):
                        K.emit("vector", lambda eng, ii=i: eng.tensor_scalar(
                            sc32[:, ii * 128:(ii + 1) * 128], pqi16[:, ii * 128:(ii + 1) * 128],
                            pwork[:, 84 + ii:85 + ii], None, ALU.mult),
                            waits=[(sA, abase + i + 1)] + (VW() if i == 0 else []), inc=sV)
                    vn = K.c(sV)
                    vbase = vn - 16
                    tbase = K.c(sT)
                    for b4 in range(4):
                        for g in range(4):
                            i = b4 * 4 + g
                            K.emit("tensor", lambda eng, ii=i, b=hf * 4 + b4, gg=g: eng.transpose(
                                pg[:, b * 512 + gg * 128: b * 512 + (gg + 1) * 128],
                                sc32[:, ii * 128:(ii + 1) * 128], id32sb[:, :]),
                                waits=[(sV, vbase + i + 1), (sA, abase + b4 * 4 + 4)], inc=sT)
                        for g in range(4):
                            i = b4 * 4 + g
                            K.emit("vector", lambda eng, b=hf * 4 + b4,
                                   hd=(q * 4 + g), t=hf * 4 + b4, gg=g: eng.tensor_copy(
                                qst[:, hd * TCORE + t * 128: hd * TCORE + t * 128 + 128],
                                pg[:, b * 512 + gg * 128: b * 512 + (gg + 1) * 128]),
                                waits=[(sT, tbase + b4 * 4 + 4)], inc=sV)

                prev = None
                tn_prev = 0
                for qq in range(4):
                    for hf in range(2):
                        for j in range(NCH):
                            bpar = j % 2
                            wq_n[bpar] = DMA("gpsimd", wqbuf[:, bpar * 512:(bpar + 1) * 512],
                                             wqq[j * 128:(j + 1) * 128, qq * 512:(qq + 1) * 512],
                                             waits=[(sT, wq_free[bpar])], sem="q%d" % bpar)
                            for tci in range(4):
                                tc = hf * 4 + tci
                                K.emit("tensor", lambda eng, jj=j, t=tc, b=hf * 4 + tci, bp=bpar: eng.matmul(
                                    pg[:, b * 512:(b + 1) * 512],
                                    xn16[:, jj * TCORE + t * 128: jj * TCORE + t * 128 + 128],
                                    wqbuf[:, bp * 512:(bp + 1) * 512],
                                    start=(jj == 0), stop=(jj == NCH - 1)),
                                    waits=([(sV, pg_free[hf]), ("q%d" % bpar, wq_n[bpar])]
                                           if tci == 0 else []), inc=sT)
                            wq_free[bpar] = K.c(sT)
                        tn_now = K.c(sT)
                        if prev is not None:
                            qevac(prev[0], prev[1], tn_prev)
                            pg_free[prev[1]] = K.c(sV)
                        prev = (qq, hf)
                        tn_prev = tn_now
                qevac(prev[0], prev[1], tn_prev)
                pg_free[prev[1]] = K.c(sV)
                K.barrier()
                K.flush(blk)

        def p5_attention(qst):
            with (nc.sbuf_tensor(NM("kbuf"), [128, 2 * 4096], F16) as kbuf,
                  nc.sbuf_tensor(NM("vbuf"), [128, 2 * 4096], F16) as vbuf,
                  nc.sbuf_tensor(NM("scsc"), [128, 4096], F32) as scsc,
                  nc.sbuf_tensor(NM("wi16"), [128, 1024], I16) as wi16,
                  nc.sbuf_tensor(NM("w16"), [128, 6 * 512], F16) as w16,
                  nc.sbuf_tensor(NM("wyi16"), [128, 512], I16) as wyi16,
                  nc.sbuf_tensor(NM("mx32"), [128, 32], F32) as mx32,
                  nc.sbuf_tensor(NM("uq32"), [128, 32], F32) as uq32,
                  nc.sbuf_tensor(NM("cwk"), [128, 48], F32) as cwk,
                  nc.sbuf_tensor(NM("mt16"), [128, 16], F16) as mt16,
                  nc.sbuf_tensor(NM("mrow2"), [2, 1024], F16) as mrow2,
                  nc.psum_tensor(NM("psA"), [128, 2048], F32) as psA,
                  nc.psum_tensor(NM("psB"), [128, 1024], F32) as psB,
                  nc.psum_tensor(NM("psY"), [128, 512], F32) as psY,
                  nc.psum_tensor(NM("pstr"), [128, 1024], F16) as pstr,
                  nc.Block() as blk):
                kv_free = [0, 0]
                kv_ld = [0, 0]
                psA_free = [0, 0]
                scq_free = [0, 0, 0, 0]
                psB_free = [0, 0]
                wi_free = [0, 0]
                w16_free = [0, 0, 0, 0, 0, 0]
                psY_free = [0]
                yq_done = [0, 0]
                pstr_free = [0, 0]
                ytr_free = 0
                qctr = [0]
                SEQ = [(h, g) for h in range(H) for g in range(2)]

                def load_kv(h):
                    par = h % 2
                    w = [(sT, kv_free[par]), (sCC, 1)]
                    for r in range(4):
                        DMA("sync", kbuf[:, par * 4096 + r * 1024: par * 4096 + (r + 1) * 1024],
                            bass.AP(cck_out, (r * 2048 + h * 128) * TCORE, [[1024, 128], [1, 1024]]),
                            waits=w, sem="kv%d" % par)
                        w = []
                    wv = [(sCC, 2)]
                    for r in range(4):
                        kv_ld[par] = DMA(
                            "sync",
                            vbuf[:, par * 4096 + r * 1024: par * 4096 + (r + 1) * 1024]
                            .rearrange("p (lc d) -> p lc d", d=128),
                            bass.AP(ccv_out, (r * 1024) * 2048 + h * 128,
                                    [[2048, 128], [128 * 2048, 8], [1, 128]]),
                            waits=wv, sem="kv%d" % par)
                        wv = []

                def make_A_jobs(i):
                    h, g = SEQ[i]
                    par = h % 2
                    ip = i % 2
                    jobs = []
                    for tci in range(4):
                        tc = g * 4 + tci
                        lhs = qst[:, h * TCORE + tc * 128: h * TCORE + tc * 128 + 128]
                        for qd in range(4):
                            def job(l=lhs, p=par, q=qd, col=(ip * 16 + tci * 4 + qd)):
                                slot = qctr[0] % 2
                                qb = qctr[0] % 4
                                qctr[0] += 1
                                for half in range(2):
                                    K.emit("tensor", lambda eng, hf=half, s=slot, l2=l, p2=p, q2=q: eng.matmul(
                                        psA[:, s * 1024 + hf * 512: s * 1024 + (hf + 1) * 512],
                                        l2, kbuf[:, p2 * 4096 + q2 * 1024 + hf * 512: p2 * 4096 + q2 * 1024 + (hf + 1) * 512],
                                        start=True, stop=True),
                                        waits=([(sA, psA_free[slot]),
                                                ("kv%d" % p, kv_ld[p])] if half == 0 else []),
                                        inc=sT)
                                tmm = K.c(sT)
                                K.emit("scalar", lambda eng, qq=qb, s=slot, c2=col: eng.activation(
                                    scsc[:, qq * 1024:(qq + 1) * 1024],
                                    psA[:, s * 1024:(s + 1) * 1024], AF.Exp,
                                    accum_out=uq32[:, c2:c2 + 1]),
                                    waits=[(sT, tmm), (sV, scq_free[qb])], inc=sA)
                                psA_free[slot] = K.c(sA)
                                K.emit("vector", lambda eng, qq=qb, c2=col: eng.reduce_max(
                                    mx32[:, c2:c2 + 1], scsc[:, qq * 1024:(qq + 1) * 1024], axis=AX),
                                    waits=[(sA, psA_free[slot])], inc=sV)
                                scq_free[qb] = K.c(sV)
                            jobs.append(job)
                    return jobs

                def do_A(i):
                    for jb in make_A_jobs(i):
                        jb()

                def do_Acomb(i):
                    h, g = SEQ[i]
                    ip = i % 2
                    b = ip * 16
                    gi = h * 8 + g * 4
                    cb = ip * 24
                    K.emit("vector", lambda eng: eng.tensor_max(
                        cwk[:, cb:cb + 8], mx32[:, b:b + 16:2], mx32[:, b + 1:b + 16:2]),
                        waits=VW(), inc=sV)
                    K.emit("vector", lambda eng: eng.tensor_max(
                        cwk[:, cb + 8:cb + 12], cwk[:, cb:cb + 8:2], cwk[:, cb + 1:cb + 8:2]),
                        waits=VW(), inc=sV)
                    K.emit("vector", lambda eng: eng.tensor_add(
                        cwk[:, cb:cb + 8], uq32[:, b:b + 16:2], uq32[:, b + 1:b + 16:2]),
                        waits=VW(), inc=sV)
                    K.emit("vector", lambda eng: eng.tensor_add(
                        cwk[:, cb + 12:cb + 16], cwk[:, cb:cb + 8:2], cwk[:, cb + 1:cb + 8:2]),
                        waits=VW(), inc=sV)
                    K.emit("vector", lambda eng: eng.reciprocal(cwk[:, cb + 16:cb + 20], cwk[:, cb + 8:cb + 12]),
                           waits=VW(), inc=sV)
                    K.emit("vector", lambda eng, z=gi: eng.tensor_mul(
                        zbuf[:, z:z + 4], cwk[:, cb + 12:cb + 16], cwk[:, cb + 16:cb + 20]),
                        waits=VW(), inc=sV)
                    TS("vector", cwk[:, cb:cb + 4], zbuf[:, gi:gi + 4], 127.0, ALU.mult,
                       waits=VW(), inc=sV)
                    K.emit("vector", lambda eng, z=gi: eng.reciprocal(
                        z127i[:, z:z + 4], cwk[:, cb:cb + 4]), waits=VW(), inc=sV)
                    vz = K.c(sV)
                    K.emit("scalar", lambda eng: eng.activation(
                        cwk[:, cb + 4:cb + 8], cwk[:, cb + 8:cb + 12], AF.Ln),
                        waits=[(sV, vz)], inc=sA)
                    am = K.c(sA)
                    TS("vector", cwk[:, cb + 20:cb + 24], cwk[:, cb + 4:cb + 8], -1.0, ALU.mult,
                       waits=[(sA, am)], inc=sV)
                    mb = ip * 8
                    K.emit("vector", lambda eng: eng.tensor_copy(
                        mt16[:, mb:mb + 8:2], cwk[:, cb + 20:cb + 24]), waits=VW(), inc=sV)
                    K.emit("vector", lambda eng: eng.tensor_copy(
                        cwk[:, cb:cb + 4], mt16[:, mb:mb + 8:2]), waits=VW(), inc=sV)
                    K.emit("vector", lambda eng: eng.tensor_sub(
                        cwk[:, cb + 4:cb + 8], cwk[:, cb + 20:cb + 24], cwk[:, cb:cb + 4]),
                        waits=VW(), inc=sV)
                    K.emit("vector", lambda eng: eng.tensor_copy(
                        mt16[:, mb + 1:mb + 8:2], cwk[:, cb + 4:cb + 8]), waits=VW(), inc=sV)
                    return K.c(sV)

                def do_mrow(i, vm):
                    ip = i % 2
                    mb = ip * 8
                    for t in range(4):
                        K.emit("tensor", lambda eng, tt=t: eng.transpose(
                            pstr[0:2, ip * 512 + tt * 128: ip * 512 + (tt + 1) * 128],
                            mt16[:, mb + 2 * tt: mb + 2 * tt + 2], id16sb[:, :]),
                            waits=([(sV, vm), (sA, pstr_free[ip])] if t == 0 else []), inc=sT)
                    tmr = K.c(sT)
                    K.emit("scalar", lambda eng: eng.copy(
                        mrow2[0:2, ip * 512:(ip + 1) * 512], pstr[0:2, ip * 512:(ip + 1) * 512]),
                        waits=[(sT, tmr)], inc=sA)
                    pstr_free[ip] = K.c(sA)

                def make_y_jobs(i, tpv):
                    h, g = SEQ[i]
                    gi = h * 8 + g * 4
                    done = [0, 0]

                    def j0():
                        K.emit("scalar", lambda eng: eng.copy(ysc[:, :], psY[:, :]),
                               waits=[(sT, tpv)], inc=sA)
                        psY_free[0] = K.c(sA)
                        for j in range(4):
                            K.emit("tensor", lambda eng, q=j: eng.transpose(
                                psY[:, q * 128:(q + 1) * 128], ysc[:, q * 128:(q + 1) * 128],
                                id32sb[:, :]),
                                waits=([(sA, psY_free[0])] if j == 0 else []), inc=sT)

                    def j1():
                        ty = K.c(sT)
                        for j in range(4):
                            K.emit("vector", lambda eng, q=j: eng.reduce_max(
                                ywork[:, q:q + 1], psY[:, q * 128:(q + 1) * 128], axis=AX,
                                apply_absolute_value=True),
                                waits=([(sT, ty)] if j == 0 else []), inc=sV)
                        zsl = z127i[:, gi:gi + 4]
                        K.emit("vector", lambda eng, z=zsl: eng.tensor_mul(
                            ywork[:, 4:8], ywork[:, 0:4], z), waits=VW(), inc=sV)
                        TS("vector", ywork[:, 8:12], ywork[:, 4:8], EPS, ALU.max,
                           waits=VW(), inc=sV)
                        K.emit("vector", lambda eng: eng.reciprocal(ywork[:, 12:16], ywork[:, 8:12]),
                               waits=VW(), inc=sV)
                        K.emit("vector", lambda eng, z=zsl: eng.tensor_mul(
                            ywork[:, 16:20], ywork[:, 12:16], z), waits=VW(), inc=sV)
                        TS("vector", ywork[:, 20:24], ywork[:, 16:20], 127.0, ALU.mult,
                           waits=VW(), inc=sV)
                        TS("vector", ywork[:, 24:28], ywork[:, 8:12], 1.0 / 127.0, ALU.mult,
                           waits=VW(), inc=sV)

                    def j2():
                        vy = K.c(sV)
                        for j in range(4):
                            K.emit("scalar", lambda eng, q=j: eng.activation(
                                wyi16[:, q * 128:(q + 1) * 128], psY[:, q * 128:(q + 1) * 128],
                                AF.Copy, bias=0.0, scale=ywork[:, 20 + q:21 + q]),
                                waits=([(sV, vy)] if j == 0 else []), inc=sA)
                        ay = K.c(sA)
                        for j in range(4):
                            K.emit("vector", lambda eng, q=j: eng.tensor_scalar(
                                ysc[:, q * 128:(q + 1) * 128], wyi16[:, q * 128:(q + 1) * 128],
                                ywork[:, 24 + q:25 + q], None, ALU.mult),
                                waits=([(sA, ay)] + VW() if j == 0 else []), inc=sV)

                    def j3():
                        vy2 = K.c(sV)
                        ay = K.c(sA)
                        for j in range(4):
                            K.emit("tensor", lambda eng, q=j: eng.transpose(
                                psY[:, q * 128: (q + 1) * 128],
                                ysc[:, q * 128:(q + 1) * 128], id32sb[:, :]),
                                waits=([(sV, vy2), (sA, ay)] if j == 0 else []), inc=sT)
                        done[0] = K.c(sT)

                    def j4():
                        ty2 = done[0]
                        for j in range(4):
                            K.emit("vector", lambda eng, q=j, hh=h, gg=g: eng.tensor_copy(
                                qst[:, hh * TCORE + gg * 512 + q * 128: hh * TCORE + gg * 512 + (q + 1) * 128],
                                psY[:, q * 128: (q + 1) * 128]),
                                waits=([(sT, ty2)] if j == 0 else []), inc=sV)
                        done[1] = K.c(sV)

                    return [j0, j1, j2, j3, j4], done

                PVLAG = 6

                def do_B(i, y_jobs, a_jobs, comb_job, yq):
                    h, g = SEQ[i]
                    par = h % 2
                    ip = i % 2
                    pv_pend = []

                    def emit_scores(sc):
                        pp = sc % 2
                        sl = sc % 6
                        K.emit("tensor", lambda eng, s=sc, p=par, q=pp, hh=h, gg=g: eng.matmul(
                            psB[:, q * 512:(q + 1) * 512],
                            kbuf[:, p * 4096 + s * 128: p * 4096 + s * 128 + 128],
                            qst[:, hh * TCORE + gg * 512: hh * TCORE + (gg + 1) * 512],
                            start=True, stop=False),
                            waits=[(sA, max(psB_free[pp], pstr_free[ip]))], inc=sT)
                        K.emit("tensor", lambda eng, q=pp, i2=ip: eng.matmul(
                            psB[:, q * 512:(q + 1) * 512], ones2sb[:, :],
                            mrow2[0:2, i2 * 512:(i2 + 1) * 512],
                            start=False, stop=True), inc=sT)
                        tmm = K.c(sT)
                        K.emit("scalar", lambda eng, q=pp: eng.activation(
                            wi16[:, q * 512:(q + 1) * 512], psB[:, q * 512:(q + 1) * 512],
                            AF.Exp, bias=ln127sb[:, 0:1], scale=1.0),
                            waits=[(sT, tmm), (sV, wi_free[pp])], inc=sA)
                        psB_free[pp] = K.c(sA)
                        K.emit("vector", lambda eng, q=pp, l=sl: eng.tensor_copy(
                            w16[:, l * 512:(l + 1) * 512], wi16[:, q * 512:(q + 1) * 512]),
                            waits=[(sA, psB_free[pp]), (sT, w16_free[sl])], inc=sV)
                        wi_free[pp] = K.c(sV)
                        pv_pend.append((sc, wi_free[pp]))

                    def emit_pv(sc, wn):
                        sl = sc % 6
                        K.emit("tensor", lambda eng, s=sc, p=par, l=sl: eng.matmul(
                            psY[:, :], vbuf[:, p * 4096 + s * 128: p * 4096 + s * 128 + 128],
                            w16[:, l * 512:(l + 1) * 512],
                            start=(s == 0), stop=(s == 31)),
                            waits=[(sV, wn)] + ([(sV, yq[1]), (sT, yq[0])] if sc == 0 else []),
                            inc=sT)
                        w16_free[sl] = K.c(sT)

                    emit_scores(0)
                    if y_jobs:
                        y_jobs.pop(0)()
                    for sc in range(1, 32):
                        emit_scores(sc)
                        if sc >= PVLAG:
                            emit_pv(*pv_pend[sc - PVLAG])
                        if y_jobs:
                            y_jobs.pop(0)()
                        if a_jobs:
                            a_jobs.pop(0)()
                        if sc == 20 and comb_job is not None:
                            comb_job()
                    for k in range(32 - PVLAG, 32):
                        emit_pv(*pv_pend[k])
                    if a_jobs:
                        while a_jobs:
                            a_jobs.pop(0)()
                    if comb_job is not None and not getattr(comb_job, "_done", False):
                        pass
                    return K.c(sT)

                load_kv(0)
                load_kv(1)
                do_A(0)
                vm = do_Acomb(0)
                do_mrow(0, vm)
                y_jobs = None
                yq = [0, 0]
                for i in range(len(SEQ)):
                    a_jobs = make_A_jobs(i + 1) if i + 1 < len(SEQ) else None

                    def comb_job(ii=i):
                        vm2 = do_Acomb(ii + 1)
                        do_mrow(ii + 1, vm2)
                    cj = comb_job if i + 1 < len(SEQ) else None
                    tpv = do_B(i, y_jobs, a_jobs, cj, yq)
                    y_jobs, yq = make_y_jobs(i, tpv)
                    h, g = SEQ[i]
                    if g == 1:
                        kv_free[h % 2] = K.c(sT)
                        if h + 2 < H:
                            load_kv(h + 2)
                for jb in y_jobs:
                    jb()
                K.barrier()
                K.flush(blk)

        def p6_outproj(qst):
            with (nc.sbuf_tensor(NM("wpbuf"), [128, 2 * 2048], F16) as wpbuf,
                  nc.sbuf_tensor(NM("pqi16"), [128, 4096], I16) as pqi16,
                  nc.sbuf_tensor(NM("ao32"), [128, 2048], F32) as ao32,
                  nc.psum_tensor(NM("po"), [128, 4096], F32) as po,
                  nc.Block() as blk):
                wp_free = [0, 0]
                wp_n = [0, 0]
                po_free = [0, 0]

                def p6_quant(tc, tn):
                    par = tc % 2
                    pov = po[:, par * 2048:(par + 1) * 2048]
                    K.emit("vector", lambda eng, pp=pov: eng.reduce_max(
                        pwork[:, 0:1], pp, axis=AX, apply_absolute_value=True),
                        waits=[(sT, tn)], inc=sV)
                    K.emit("vector", lambda eng: eng.tensor_mul(pwork[:, 1:2], pwork[:, 0:1], swb[:, 6:7]),
                           waits=VW(), inc=sV)
                    TS("vector", pwork[:, 2:3], pwork[:, 1:2], EPS, ALU.max, waits=VW(), inc=sV)
                    K.emit("vector", lambda eng: eng.reciprocal(pwork[:, 3:4], pwork[:, 2:3]),
                           waits=VW(), inc=sV)
                    K.emit("vector", lambda eng: eng.tensor_mul(pwork[:, 4:5], pwork[:, 3:4], swb[:, 6:7]),
                           waits=VW(), inc=sV)
                    TS("vector", pwork[:, 6:7], pwork[:, 4:5], 127.0, ALU.mult, waits=VW(), inc=sV)
                    TS("vector", pwork[:, 5:6], pwork[:, 2:3], 1.0 / 127.0, ALU.mult, waits=VW(), inc=sV)
                    vq = K.c(sV)
                    K.emit("scalar", lambda eng, pp=pov: eng.activation(
                        pqi16[:, 0:2048], pp, AF.Copy, bias=0.0, scale=pwork[:, 6:7]),
                        waits=[(sV, vq)], inc=sA)
                    an = K.c(sA)
                    K.emit("vector", lambda eng: eng.tensor_scalar(
                        ao32[:, :], pqi16[:, 0:2048], pwork[:, 5:6], None, ALU.mult),
                        waits=[(sA, an)] + VW(), inc=sV)
                    vd = K.c(sV)
                    for cc in range(NCH):
                        K.emit("tensor", lambda eng, c=cc, p=par: eng.transpose(
                            po[:, p * 2048 + c * 128: p * 2048 + (c + 1) * 128],
                            ao32[:, c * 128:(c + 1) * 128], id32sb[:, :]),
                            waits=[(sV, vd)], inc=sT)
                    tb = K.c(sT)
                    for cc in range(NCH):
                        K.emit("vector", lambda eng, c=cc, t=tc, p=par: eng.tensor_add(
                            xres[:, c * TCORE + t * 128: c * TCORE + t * 128 + 128],
                            xres[:, c * TCORE + t * 128: c * TCORE + t * 128 + 128],
                            po[:, p * 2048 + c * 128: p * 2048 + (c + 1) * 128]),
                            waits=[(sT, tb)], inc=sV)
                    po_free[par] = K.c(sV)

                prev = None
                for tc in range(8):
                    par = tc % 2
                    for j in range(NCH):
                        bpar = j % 2
                        wp_n[bpar] = DMA("gpsimd", wpbuf[:, bpar * 2048:(bpar + 1) * 2048],
                                         wpq[j * 128:(j + 1) * 128, :],
                                         waits=[(sT, wp_free[bpar])], sem="p%d" % bpar)
                        for isl in range(4):
                            K.emit("tensor", lambda eng, jj=j, t=tc, p=par, q=isl, bp=bpar: eng.matmul(
                                po[:, p * 2048 + q * 512: p * 2048 + (q + 1) * 512],
                                qst[:, jj * TCORE + t * 128: jj * TCORE + t * 128 + 128],
                                wpbuf[:, bp * 2048 + q * 512: bp * 2048 + (q + 1) * 512],
                                start=(jj == 0), stop=(jj == NCH - 1)),
                                waits=([(sV, po_free[par]), ("p%d" % bpar, wp_n[bpar])]
                                       if isl == 0 else []), inc=sT)
                        wp_free[bpar] = K.c(sT)
                    tn_now = K.c(sT)
                    if prev is not None:
                        p6_quant(prev, tn_prev)
                    prev = tc
                    tn_prev = tn_now
                p6_quant(prev, tn_prev)
                K.barrier()
                K.flush(blk)

        p3_ln2_quant()
        p4_projections()
        qstack = contextlib.ExitStack()
        qst = qstack.enter_context(nc.sbuf_tensor(NM("qst"), [128, H * TCORE], F16))
        p4_qproj(qst)
        if DBG:
            with nc.Block() as blkd2:
                DMA("sync", dbg_q, qst[:, :], waits=[(sV, K.c(sV))])
                DMA("sync", dbg_sw, swb[:, :])
                DMA("sync", dbg_kg, cck_out.ap()[:, :], waits=[(sCC, 2)])
                K.flush(blkd2)
        p5_attention(qst)
        if DBG:
            with nc.Block() as blkd3:
                DMA("sync", dbg_y, qst[:, :], waits=[(sV, K.c(sV))])
                DMA("sync", dbg_z, zbuf[:, :])
                K.flush(blkd3)
        p6_outproj(qst)
        qstack.close()

    if stage >= 3:
        ln_phase(0)
        ffn_phase(with_absums=False)

    with nc.Block() as blk8:
        DMA("sync", yt.rearrange("(cc p) t -> p cc t", p=128),
            xres[:, :].rearrange("p (cc t) -> p cc t", t=TCORE),
            waits=[(sV, K.c(sV)), (sA, K.c(sA)), (sG, K.c(sG)), (sT, K.c(sT))])
        fin = K.c(sL)
        K.emit("sync", lambda eng: eng.nop(), waits=[(sL, fin)])
        K.flush(blk8)

    st.close()
    K._sem_stack.close()
    return nc


def build_attention(env):
    raise NotImplementedError


_BUILT = {}


def kernel(L0_attn_x, L0_attn_Wq, L0_attn_Wk, L0_attn_Wv, L0_attn_Wp,
           L0_ffn_W0, L0_ffn_bias0, L0_ffn_W1, L0_ffn_bias1):
    x = np.asarray(L0_attn_x, np.float32)
    consts = dict(
        w0t=np.ascontiguousarray(np.asarray(L0_ffn_W0, np.float32).T
                                 .reshape(16, 128, 32, 256).transpose(1, 2, 0, 3)
                                 .reshape(128, 131072)),
        w1t=np.ascontiguousarray(np.asarray(L0_ffn_W1, np.float32).T
                                 .reshape(2, 32, 128, 16, 128).transpose(2, 3, 0, 1, 4)
                                 .reshape(128, 131072)),
        b0m=np.ascontiguousarray(np.asarray(L0_ffn_bias0, np.float32).reshape(NNCH, 128).T),
        b1m=np.ascontiguousarray(np.asarray(L0_ffn_bias1, np.float32).reshape(NCH, 128).T),
        wqt=np.ascontiguousarray(np.asarray(L0_attn_Wq, np.float32).T),
        wkt=np.ascontiguousarray(np.asarray(L0_attn_Wk, np.float32).T),
        wvt=np.ascontiguousarray(np.asarray(L0_attn_Wv, np.float32).T),
        wpt=np.ascontiguousarray(np.asarray(L0_attn_Wp, np.float32).T),
        id32=np.eye(128, dtype=np.float32),
        id16=np.eye(128, dtype=np.float16),
        ones2=np.ones((2, 128), np.float16),
        ones1r=np.ones((1, 128), np.float16),
        onescol=np.ones((128, 1), np.float16),
        ones32r=np.ones((1, 128), np.float32),
        ones32c=np.ones((128, 1), np.float32),
        invccr=np.full((1, 128), 1.0 / (C * C), np.float32),
    )
    if STAGE not in _BUILT:
        _BUILT[STAGE] = build_nc(STAGE)
    ncp = _BUILT[STAGE]
    in_maps = []
    for c in range(8):
        b, t0 = c // 4, (c % 4) * TCORE
        m = dict(consts)
        m["xt"] = np.ascontiguousarray(x[b, t0:t0 + TCORE, :].T)
        in_maps.append(m)
    res = run_bass_kernel_spmd(ncp, in_maps, core_ids=list(range(8)))
    out = np.empty((2, 4096, C), np.float32)
    for c in range(8):
        b, t0 = c // 4, (c % 4) * TCORE
        out[b, t0:t0 + TCORE, :] = res.results[c]["yt"].T
    return out


# version 16 (final)
# speedup vs baseline: 1.2703x; 1.0412x over previous
import math
import contextlib
import numpy as np
import concourse.bass as bass
import concourse.mybir as mybir
from concourse.bass_utils import run_bass_kernel_spmd

F32 = mybir.dt.float32
F16 = mybir.dt.float16
I16 = mybir.dt.int16
AF = mybir.ActivationFunctionType
ALU = mybir.AluOpType
AX = mybir.AxisListType.X

C = 2048
H = 16
DH = 128
NF = 8192
TCORE = 1024
NCH = C // 128
NNCH = NF // 128
ISQD = 1.0 / math.sqrt(128.0)
LN127 = float(np.log(np.float32(127.0)))
EPS = 1e-5
ENGINES = ("tensor", "vector", "scalar", "gpsimd", "sync")

DBG = False
STAGE = 3


class KB:
    def __init__(self, nc):
        self.nc = nc
        self.cnt = {}
        self.sems = {}
        self.streams = {e: [] for e in ENGINES}
        self.lastwait = {e: {} for e in ENGINES}
        self._sem_stack = contextlib.ExitStack()

    def sem(self, name):
        if name not in self.sems:
            self.sems[name] = self._sem_stack.enter_context(self.nc.semaphore(name))
            self.cnt[name] = 0
        return self.sems[name]

    def emit(self, eng, fn, waits=(), inc=None, incn=1):
        ws = []
        for (s, c) in waits:
            if c is None or c <= 0:
                continue
            self.sem(s)
            if c > self.lastwait[eng].get(s, -1):
                ws.append((s, c))
                self.lastwait[eng][s] = c
        out = None
        if inc is not None:
            self.sem(inc)
            self.cnt[inc] += incn
            out = self.cnt[inc]
        self.streams[eng].append((fn, ws, inc, incn))
        return out

    def c(self, name):
        self.sem(name)
        return self.cnt[name]

    def barrier(self):
        snaps = {}
        for e in ENGINES:
            snaps[e] = self.emit(e, lambda eng: eng.nop(), inc="bar_" + e)
        for e in ENGINES:
            self.emit(e, lambda eng: eng.nop(),
                      waits=[("bar_" + o, snaps[o]) for o in ENGINES if o != e])

    def flush(self, block):
        streams = {e: list(self.streams[e]) for e in ENGINES}
        for e in ENGINES:
            self.streams[e] = []

        def mk(e):
            def body(eng):
                for (fn, ws, inc, incn) in streams[e]:
                    for (s, cv) in ws:
                        eng.wait_ge(self.sems[s], cv)
                    instr = fn(eng)
                    if inc is not None:
                        instr.then_inc(self.sems[inc], incn)
            return body

        block.tensor(mk("tensor"))
        block.vector(mk("vector"))
        block.scalar(mk("scalar"))
        block.gpsimd(mk("gpsimd"))
        block.sync(mk("sync"))


sA, sV, sG, sT, sL, sW, sCC, sO = "a", "v", "g", "t", "l", "w", "cc", "o"


def build_nc(stage=STAGE):
    nc = bass.Bass("TRN2", target_bir_lowering=False)
    K = KB(nc)

    xt = nc.dram_tensor("xt", [C, TCORE], F32, kind="ExternalInput").ap()
    w0t = nc.dram_tensor("w0t", [128, C * NF // 128], F32, kind="ExternalInput").ap()
    w1t = nc.dram_tensor("w1t", [128, C * NF // 128], F32, kind="ExternalInput").ap()
    b0m = nc.dram_tensor("b0m", [128, NNCH], F32, kind="ExternalInput").ap()
    b1m = nc.dram_tensor("b1m", [128, NCH], F32, kind="ExternalInput").ap()
    wqt = nc.dram_tensor("wqt", [C, C], F32, kind="ExternalInput").ap()
    wkt = nc.dram_tensor("wkt", [C, C], F32, kind="ExternalInput").ap()
    wvt = nc.dram_tensor("wvt", [C, C], F32, kind="ExternalInput").ap()
    wpt = nc.dram_tensor("wpt", [C, C], F32, kind="ExternalInput").ap()
    id32 = nc.dram_tensor("id32", [128, 128], F32, kind="ExternalInput").ap()
    id16 = nc.dram_tensor("id16", [128, 128], F16, kind="ExternalInput").ap()
    ones2 = nc.dram_tensor("ones2", [128, 128], F16, kind="ExternalInput").ap()
    ones1r = nc.dram_tensor("ones1r", [1, 128], F16, kind="ExternalInput").ap()
    onescol = nc.dram_tensor("onescol", [128, 1], F16, kind="ExternalInput").ap()
    ones32r = nc.dram_tensor("ones32r", [1, 128], F32, kind="ExternalInput").ap()
    ones32c = nc.dram_tensor("ones32c", [128, 1], F32, kind="ExternalInput").ap()
    invccr = nc.dram_tensor("invccr", [1, 128], F32, kind="ExternalInput").ap()
    yt = nc.dram_tensor("yt", [C, TCORE], F32, kind="ExternalOutput").ap()
    dbg_xn = nc.dram_tensor("dbg_xn", [C, TCORE], F16, kind="ExternalOutput").ap()
    dbg_d = nc.dram_tensor("dbg_d", [NF, 512], F16, kind="ExternalOutput").ap()
    dbg_q = nc.dram_tensor("dbg_q", [128, H * TCORE], F16, kind="ExternalOutput").ap()
    dbg_y = nc.dram_tensor("dbg_y", [128, H * TCORE], F16, kind="ExternalOutput").ap()
    dbg_sw = nc.dram_tensor("dbg_sw", [128, 8], F32, kind="ExternalOutput").ap()
    dbg_z = nc.dram_tensor("dbg_z", [128, 128], F32, kind="ExternalOutput").ap()
    dbg_kg = nc.dram_tensor("dbg_kg", [4 * C, TCORE], F16, kind="ExternalOutput").ap()
    dbg_sc = nc.dram_tensor("dbg_sc", [128, 4096], F32, kind="ExternalOutput").ap()
    dbg_ysc = nc.dram_tensor("dbg_ysc", [128, 512], F32, kind="ExternalOutput").ap()
    dbg_mrow = nc.dram_tensor("dbg_mrow", [2, 512], F16, kind="ExternalOutput").ap()
    dbg_w16 = nc.dram_tensor("dbg_w16", [128, 1024], F16, kind="ExternalOutput").ap()
    dbg_mw = nc.dram_tensor("dbg_mw", [128, 16], F32, kind="ExternalOutput").ap()
    w0f16 = nc.dram_tensor("w0f16", [128, C * NF // 128], F16).ap()
    w1f16 = nc.dram_tensor("w1f16", [128, C * NF // 128], F16).ap()
    wqq = nc.dram_tensor("wqq", [C, C], F16).ap()
    wpq = nc.dram_tensor("wpq", [C, C], F16).ap()
    cck_in = nc.dram_tensor("cck_in", [C, TCORE], F16)
    cck_out = nc.dram_tensor("cck_out", [4 * C, TCORE], F16)
    ccv_in = nc.dram_tensor("ccv_in", [TCORE, C], F16)
    ccv_out = nc.dram_tensor("ccv_out", [4 * TCORE, C], F16)

    st = contextlib.ExitStack()
    SB = lambda *a: st.enter_context(nc.sbuf_tensor(*a))
    _nmc = [0]

    def NM(b):
        _nmc[0] += 1
        return "%s_%d" % (b, _nmc[0])

    xres = SB("xres", [128, NCH * TCORE], F32)
    xn16 = SB("xn16", [128, NCH * TCORE], F16)
    bm16 = SB("bm16", [128, TCORE], F16)
    br16 = SB("br16", [128, TCORE], F16)
    b0sb = SB("b0sb", [128, NNCH], F32)
    b1sb = SB("b1sb", [128, NCH], F32)
    b1h = SB("b1h", [128, NCH], F32)
    id32sb = SB("id32sb", [128, 128], F32)
    id16sb = SB("id16sb", [128, 128], F16)
    ones2sb = SB("ones2sb", [128, 128], F16)
    ones1sb = SB("ones1sb", [1, 128], F16)
    onescolsb = SB("onescolsb", [128, 1], F16)
    ones32rsb = SB("ones32rsb", [1, 128], F32)
    ones32csb = SB("ones32csb", [128, 1], F32)
    invccrsb = SB("invccrsb", [1, 128], F32)
    asum = SB("asum", [128, 24], F32)
    swsc = SB("swsc", [1, 16], F32)
    swb = SB("swb", [128, 8], F32)
    axinv = SB("axinv", [128, 8], F32)
    zbuf = SB("zbuf", [128, 128], F32)
    z127i = SB("z127i", [128, 128], F32)
    mwork = SB("mwork", [128, 16], F32)
    mtile = SB("mtile", [128, 2], F16)
    mrow = SB("mrow", [2, 512], F16)
    uq = SB("uq", [128, 4], F32)
    mxq = SB("mxq", [128, 4], F32)
    pwork = SB("pwork", [128, 128], F32)
    ywork = SB("ywork", [128, 32], F32)
    ysc = SB("ysc", [128, 512], F32)
    ln127sb = SB("ln127sb", [128, 1], F32)

    def TS(stream, out, in0, s1, op0, s2=None, op1=None, accum=None, waits=(), inc=None):
        def fn(eng, o=out, i=in0, a=s1, b=s2, p0=op0, p1=op1, ac=accum):
            if p1 is None and ac is None:
                return eng.tensor_scalar(o, i, a, None, p0)
            return eng.tensor_scalar(o, i, a, b, p0, p1 if p1 is not None else ALU.bypass,
                                     accum_out=ac)
        return K.emit(stream, fn, waits=waits, inc=inc)

    def VW():
        return [(sV, K.c(sV))]

    def DMA(q, out, in_, waits=(), sem=None):
        if sem is None:
            sem = sW if q == "gpsimd" else sL
        return K.emit(q, lambda eng, o=out, i=in_: eng.dma_start(out=o, in_=i),
                      waits=waits, inc=sem, incn=16)

    def p0_loads():
        DMA("sync", xres[:, :].rearrange("p (cc t) -> p cc t", t=TCORE),
            xt.rearrange("(cc p) t -> p cc t", p=128))
        for (dst, src) in [(b0sb, b0m), (b1sb, b1m), (id32sb, id32), (id16sb, id16),
                           (ones2sb, ones2), (ones1sb, ones1r), (onescolsb, onescol),
                           (ones32rsb, ones32r), (ones32csb, ones32c), (invccrsb, invccr)]:
            DMA("sync", dst[:, :], src)
        ld = K.c(sL)
        TS("vector", b1h[:, :], b1sb[:, :], 0.5, ALU.mult, waits=[(sL, ld)], inc=sV)
        K.emit("vector", lambda eng: eng.memset(ln127sb[:, :], LN127), inc=sV)
        return ld

    def ln_stats(pstat, pbc, xsq, stat, statrow, ld):
        xsq_free = {}
        for cc in range(NCH):
            TS("vector", xn16[:, cc * TCORE:(cc + 1) * TCORE],
               xres[:, cc * TCORE:(cc + 1) * TCORE], 1.0, ALU.mult,
               waits=[(sL, ld), (sT, xsq_free.get(cc - 2, 0))], inc=sV)
            K.emit("vector", lambda eng, c=cc: eng.tensor_mul(
                xsq[:, (c % 2) * TCORE:(c % 2 + 1) * TCORE],
                xn16[:, c * TCORE:(c + 1) * TCORE],
                xn16[:, c * TCORE:(c + 1) * TCORE]), inc=sV)
            vv = K.c(sV)
            for hf in range(2):
                K.emit("tensor", lambda eng, c=cc, h=hf: eng.matmul(
                    pstat[0:1, h * 512:(h + 1) * 512], onescolsb[:, :],
                    xn16[:, c * TCORE + h * 512: c * TCORE + (h + 1) * 512],
                    start=(c == 0), stop=(c == NCH - 1)),
                    waits=[(sV, vv)], inc=sT)
                K.emit("tensor", lambda eng, c=cc, h=hf: eng.matmul(
                    pstat[0:1, 1024 + h * 512: 1024 + (h + 1) * 512], onescolsb[:, :],
                    xsq[:, (c % 2) * TCORE + h * 512:(c % 2) * TCORE + (h + 1) * 512],
                    start=(c == 0), stop=(c == NCH - 1)), inc=sT)
            xsq_free[cc] = K.c(sT)
        tmm = K.c(sT)
        K.emit("scalar", lambda eng: eng.copy(stat[0:1, 0:2048], pstat[0:1, 0:2048]),
               waits=[(sT, tmm)], inc=sA)
        aev = K.c(sA)
        MEAN, EX2, MSQ, VAR, VEP, RVA = 2048, 3072, 4096, 5120, 6144, 7168
        TS("vector", stat[0:1, MEAN:MEAN + 1024], stat[0:1, 0:1024], 1.0 / C, ALU.mult, waits=[(sA, aev)], inc=sV)
        TS("vector", stat[0:1, EX2:EX2 + 1024], stat[0:1, 1024:2048], 1.0 / C, ALU.mult, inc=sV)
        K.emit("vector", lambda eng: eng.tensor_mul(stat[0:1, MSQ:MSQ + 1024], stat[0:1, MEAN:MEAN + 1024], stat[0:1, MEAN:MEAN + 1024]), inc=sV)
        K.emit("vector", lambda eng: eng.tensor_sub(stat[0:1, VAR:VAR + 1024], stat[0:1, EX2:EX2 + 1024], stat[0:1, MSQ:MSQ + 1024]), inc=sV)
        TS("vector", stat[0:1, VEP:VEP + 1024], stat[0:1, VAR:VAR + 1024], 1e-5, ALU.add, inc=sV)
        K.emit("vector", lambda eng: eng.reciprocal(stat[0:1, RVA:RVA + 1024], stat[0:1, VEP:VEP + 1024]), inc=sV)
        vv = K.c(sV)
        K.emit("scalar", lambda eng: eng.activation(stat[0:1, MSQ:MSQ + 1024], stat[0:1, RVA:RVA + 1024], AF.Sqrt),
               waits=[(sV, vv)], inc=sA)
        asq = K.c(sA)
        TS("vector", statrow[0:1, 0:1024], stat[0:1, MEAN:MEAN + 1024], -1.0, ALU.mult, waits=[(sA, asq)], inc=sV)
        K.emit("vector", lambda eng: eng.tensor_copy(statrow[0:1, 1024:2048], stat[0:1, MSQ:MSQ + 1024]), inc=sV)
        vv = K.c(sV)
        for hf in range(2):
            K.emit("tensor", lambda eng, h=hf: eng.matmul(
                pbc[:, h * 512:(h + 1) * 512], ones1sb[:, :],
                statrow[0:1, h * 512:(h + 1) * 512], start=True, stop=True),
                waits=[(sV, vv)], inc=sT)
            K.emit("tensor", lambda eng, h=hf: eng.matmul(
                pbc[:, 1024 + h * 512: 1024 + (h + 1) * 512], ones1sb[:, :],
                statrow[0:1, 1024 + h * 512: 1024 + (h + 1) * 512], start=True, stop=True), inc=sT)
        tb = K.c(sT)
        K.emit("scalar", lambda eng: eng.copy(bm16[:, :], pbc[:, 0:1024]), waits=[(sT, tb)], inc=sA)
        K.emit("scalar", lambda eng: eng.copy(br16[:, :], pbc[:, 1024:2048]), inc=sA)
        return K.c(sA)

    def ln_apply(bc):
        for cc in range(NCH):
            e, sem = (("gpsimd", sG) if cc % 2 == 0 else ("vector", sV))
            K.emit(e, lambda eng, c=cc: eng.tensor_add(
                xn16[:, c * TCORE:(c + 1) * TCORE], xres[:, c * TCORE:(c + 1) * TCORE],
                bm16[:, :]), waits=[(sA, bc)], inc=sem)
            K.emit(e, lambda eng, c=cc: eng.tensor_mul(
                xn16[:, c * TCORE:(c + 1) * TCORE], xn16[:, c * TCORE:(c + 1) * TCORE],
                br16[:, :]), inc=sem)
        return K.c(sG), K.c(sV)

    def make_absum_pump(pbc, ab32):
        jobs = []
        for (mi, wsrc) in enumerate((wqt, wkt, wvt, wpt)):
            for j in range(16):
                jobs.append((mi, j, wsrc))
        state = {"i": 0, "red": {}, "pa": 0}

        def pump(n=1):
            for _ in range(n):
                i = state["i"]
                if i >= len(jobs):
                    return
                mi, j, wsrc = jobs[i]
                ldn = DMA("sync", ab32[:, 0:C],
                          wsrc[j * 128:(j + 1) * 128, :],
                          waits=[(sV, state["red"].get(i - 1, 0))], sem="ab")
                K.emit("vector", lambda eng, jj=j: eng.reduce_sum(
                    asum[:, jj:jj + 1], ab32[:, 0:C], axis=AX,
                    apply_absolute_value=True), waits=[("ab", ldn)], inc=sV)
                state["red"][i] = K.c(sV)
                state["i"] = i + 1
                if (i + 1) % 16 == 0:
                    m = i // 16
                    K.emit("vector", lambda eng: eng.reduce_sum(
                        asum[:, 16:17], asum[:, 0:16], axis=AX), waits=VW(), inc=sV)
                    vv = K.c(sV)
                    K.emit("tensor", lambda eng: eng.matmul(
                        pbc[0:1, 512:513], asum[:, 16:17], ones32csb[:, :],
                        start=True, stop=True), waits=[(sV, vv), (sA, state["pa"])], inc=sT)
                    tv = K.c(sT)
                    K.emit("scalar", lambda eng, mm=m: eng.copy(swsc[0:1, mm:mm + 1], pbc[0:1, 512:513]),
                           waits=[(sT, tv)], inc=sA)
                    av = K.c(sA)
                    K.emit("tensor", lambda eng, mm=m: eng.matmul(
                        pbc[:, 0:1], invccrsb[:, :], swsc[0:1, mm:mm + 1],
                        start=True, stop=True), waits=[(sA, av)], inc=sT)
                    tv2 = K.c(sT)
                    K.emit("scalar", lambda eng: eng.copy(asum[:, 20:21], pbc[:, 0:1]),
                           waits=[(sT, tv2)], inc=sA)
                    av2 = K.c(sA)
                    state["pa"] = av2
                    TS("vector", swb[:, 2 * m:2 * m + 1], asum[:, 20:21], EPS, ALU.max,
                       waits=[(sA, av2)] + VW(), inc=sV)
                    K.emit("vector", lambda eng, mm=m: eng.reciprocal(
                        swb[:, 2 * mm + 1:2 * mm + 2], swb[:, 2 * mm:2 * mm + 1]),
                        waits=VW(), inc=sV)
        return pump

    def ffn(dtile, wreg, ehalf, pd, pe, pump=None, first=False):

        wreg_free = [0, 0]
        wb_n = [0, 0]
        pd_free = [0, 0]
        pe_free = [0, 0]
        eh_free = [0, 0]
        for th in range(2):
            fp32src = first and th == 0
            for sl in range(32):
                buf = sl % 2
                wv = wreg[:, buf * 4096:(buf + 1) * 4096]
                if fp32src:
                    wn = DMA("gpsimd", wv, w0t[:, sl * 4096:(sl + 1) * 4096],
                             waits=[(sT, wreg_free[buf]), ("wb%d" % buf, wb_n[buf])],
                             sem="w%d" % buf)
                    wb_n[buf] = DMA("sync", w0f16[:, sl * 4096:(sl + 1) * 4096], wv,
                                    waits=[("w%d" % buf, wn)], sem="wb%d" % buf)
                else:
                    wn = DMA("sync", wv, w0f16[:, sl * 4096:(sl + 1) * 4096],
                             waits=[(sT, wreg_free[buf]),
                                    ("wb0", K.c("wb0")), ("wb1", K.c("wb1"))],
                             sem="w%d" % buf)
                if pump:
                    pump(2)
                for nb in range(2):
                    n = sl * 2 + nb
                    ps = n % 2
                    for ccc in range(NCH):
                        K.emit("tensor", lambda eng, b=buf, q=nb, c=ccc, p=ps, t=th: eng.matmul(
                            pd[:, p * 512:(p + 1) * 512],
                            wreg[:, b * 4096 + c * 256 + q * 128: b * 4096 + c * 256 + q * 128 + 128],
                            xn16[:, c * TCORE + t * 512: c * TCORE + (t + 1) * 512],
                            start=(c == 0), stop=(c == NCH - 1)),
                            waits=([("w%d" % buf, wn), (sA, pd_free[ps])] if ccc == 0 else []),
                            inc=sT)
                    tn = K.c(sT)
                    K.emit("scalar", lambda eng, nn=n, p=ps: eng.activation(
                        dtile[:, nn * 512:(nn + 1) * 512], pd[:, p * 512:(p + 1) * 512],
                        AF.Gelu, bias=b0sb[:, nn:nn + 1], scale=1.0),
                        waits=[(sT, tn)], inc=sA)
                    pd_free[ps] = K.c(sA)
                wreg_free[buf] = K.c(sT)
            dready = K.c(sA)
            if DBG and th == 0:
                DMA("sync", dbg_d.rearrange("(nn p) t -> p nn t", p=128),
                    dtile[:, :].rearrange("p (nn t) -> p nn t", t=512),
                    waits=[(sA, dready)])
            for cc in range(NCH):
                es = cc % 2
                for hv in range(2):
                    buf = (cc * 2 + hv) % 2
                    wv = wreg[:, buf * 4096:(buf + 1) * 4096]
                    w1ix = cc * 2 + hv
                    if fp32src:
                        wn = DMA("gpsimd", wv, w1t[:, w1ix * 4096:(w1ix + 1) * 4096],
                                 waits=[(sT, wreg_free[buf]), ("wb%d" % buf, wb_n[buf])],
                                 sem="w%d" % buf)
                        wb_n[buf] = DMA("sync", w1f16[:, w1ix * 4096:(w1ix + 1) * 4096],
                                        wv, waits=[("w%d" % buf, wn)], sem="wb%d" % buf)
                    else:
                        wn = DMA("sync", wv, w1f16[:, w1ix * 4096:(w1ix + 1) * 4096],
                                 waits=[(sT, wreg_free[buf]),
                                        ("wb0", K.c("wb0")), ("wb1", K.c("wb1"))],
                                 sem="w%d" % buf)
                    if pump:
                        pump(2)
                    for nb in range(32):
                        n = hv * 32 + nb
                        K.emit("tensor", lambda eng, b=buf, q=nb, nn=n, e2=es, t=th: eng.matmul(
                            pe[:, e2 * 512:(e2 + 1) * 512],
                            wreg[:, b * 4096 + q * 128: b * 4096 + q * 128 + 128],
                            dtile[:, nn * 512:(nn + 1) * 512],
                            start=(nn == 0), stop=(nn == NNCH - 1)),
                            waits=([("w%d" % buf, wn), (sA, max(dready, pe_free[es])),
                                    (sV, eh_free[es])] if nb == 0 else []),
                            inc=sT)
                    wreg_free[buf] = K.c(sT)
                tn = K.c(sT)
                K.emit("scalar", lambda eng, c=cc, e2=es: eng.activation(
                    ehalf[:, e2 * 512:(e2 + 1) * 512], pe[:, e2 * 512:(e2 + 1) * 512],
                    AF.Identity, bias=b1h[:, c:c + 1], scale=0.5),
                    waits=[(sT, tn)], inc=sA)
                pe_free[es] = K.c(sA)
                an = pe_free[es]
                K.emit("vector", lambda eng, c=cc, e2=es, t=th: eng.tensor_add(
                    xres[:, c * TCORE + t * 512: c * TCORE + (t + 1) * 512],
                    xres[:, c * TCORE + t * 512: c * TCORE + (t + 1) * 512],
                    ehalf[:, e2 * 512:(e2 + 1) * 512]),
                    waits=[(sA, an)], inc=sV)
                eh_free[es] = K.c(sV)
        if pump:
            pump(64)

    with nc.Block() as blk0:
        ld0 = p0_loads()
        K.flush(blk0)

    def ln_phase(ld):
        with (nc.psum_tensor(NM("pstat"), [2, 2048], F32) as pstat,
              nc.psum_tensor(NM("pbcp"), [128, 2048], F32) as pbcp,
              nc.sbuf_tensor(NM("xsq"), [128, 2 * TCORE], F16) as xsq,
              nc.sbuf_tensor(NM("stat"), [1, 8192], F32) as stat,
              nc.sbuf_tensor(NM("statrow"), [1, 2048], F16) as statrow,
              nc.Block() as blk):
            bc = ln_stats(pstat, pbcp, xsq, stat, statrow, ld)
            ln_apply(bc)
            K.barrier()
            K.flush(blk)

    def ffn_phase(with_absums, first=False):
        with (nc.sbuf_tensor(NM("dtile"), [128, NNCH * 512], F16) as dtile,
              nc.sbuf_tensor(NM("wreg"), [128, 8192], F16) as wreg,
              nc.sbuf_tensor(NM("ehalf"), [128, 1024], F16) as ehalf,
              nc.psum_tensor(NM("pd"), [128, 1024], F32) as pd,
              nc.psum_tensor(NM("pe"), [128, 1024], F32) as pe,
              nc.Block() as blk):
            pump = None
            if with_absums:
                with (nc.sbuf_tensor(NM("ab32"), [128, C], F32) as ab32,
                      nc.psum_tensor(NM("pbcf"), [128, 1024], F32) as pbcf):
                    pump = make_absum_pump(pbcf, ab32)
                    ffn(dtile, wreg, ehalf, pd, pe, pump, first=first)
                    K.barrier()
                    K.flush(blk)
            else:
                ffn(dtile, wreg, ehalf, pd, pe, None, first=first)
                K.barrier()
                K.flush(blk)

    _FIRST_FFN = [True]
    ln_phase(ld0)
    if DBG:
        with nc.Block() as blkd:
            DMA("sync", dbg_xn.rearrange("(cc p) t -> p cc t", p=128),
                xn16[:, :].rearrange("p (cc t) -> p cc t", t=TCORE),
                waits=[(sV, K.c(sV)), (sG, K.c(sG))])
            K.flush(blkd)
    ffn_phase(with_absums=(stage >= 2), first=True)

    if stage >= 2:
        def p3_ln2_quant():
            ld = 0
            with (nc.psum_tensor(NM("pstat"), [2, 2048], F32) as pstat,
                  nc.psum_tensor(NM("pbcp"), [128, 2048], F32) as pbcp,
                  nc.sbuf_tensor(NM("xsq"), [128, 2 * TCORE], F16) as xsq,
                  nc.sbuf_tensor(NM("stat"), [1, 8192], F32) as stat,
                  nc.sbuf_tensor(NM("statrow"), [1, 2048], F16) as statrow,
                  nc.Block() as blk):
                bc = ln_stats(pstat, pbcp, xsq, stat, statrow, ld)
                K.barrier()
                K.flush(blk)
            with (nc.psum_tensor(NM("pt32"), [128, 2048], F32) as pt32,
                  nc.sbuf_tensor(NM("xn32t"), [128, 2048], F32) as xn32t,
                  nc.sbuf_tensor(NM("xqi16t"), [128, 2048], I16) as xqi16t,
                  nc.sbuf_tensor(NM("xq32t"), [128, 2048], F32) as xq32t,
                  nc.Block() as blk):
                pt_free = 0
                xn32_free = 0
                for tc in range(8):
                    for cc in range(NCH):
                        K.emit("vector", lambda eng, c=cc, t=tc: eng.tensor_add(
                            xn32t[:, c * 128:(c + 1) * 128],
                            xres[:, c * TCORE + t * 128: c * TCORE + t * 128 + 128],
                            bm16[:, t * 128:(t + 1) * 128]),
                            waits=[(sT, xn32_free)], inc=sV)
                        K.emit("vector", lambda eng, c=cc, t=tc: eng.tensor_mul(
                            xn32t[:, c * 128:(c + 1) * 128], xn32t[:, c * 128:(c + 1) * 128],
                            br16[:, t * 128:(t + 1) * 128]), waits=VW(), inc=sV)
                    vn = K.c(sV)
                    for cc in range(NCH):
                        K.emit("tensor", lambda eng, c=cc: eng.transpose(
                            pt32[:, c * 128:(c + 1) * 128], xn32t[:, c * 128:(c + 1) * 128],
                            id32sb[:, :]), waits=[(sV, max(vn, pt_free))], inc=sT)
                    xn32_free = K.c(sT)
                    tn = K.c(sT)
                    K.emit("vector", lambda eng: eng.reduce_max(
                        pwork[:, 0:1], pt32[:, :], axis=AX, apply_absolute_value=True),
                        waits=[(sT, tn)], inc=sV)
                    TS("vector", pwork[:, 1:2], pwork[:, 0:1], EPS, ALU.max, waits=VW(), inc=sV)
                    K.emit("vector", lambda eng: eng.reciprocal(pwork[:, 2:3], pwork[:, 1:2]),
                           waits=VW(), inc=sV)
                    TS("vector", pwork[:, 3:4], pwork[:, 2:3], 127.0, ALU.mult, waits=VW(), inc=sV)
                    TS("vector", axinv[:, tc:tc + 1], pwork[:, 1:2], 1.0 / 127.0, ALU.mult,
                       waits=VW(), inc=sV)
                    vq = K.c(sV)
                    K.emit("scalar", lambda eng: eng.activation(
                        xqi16t[:, :], pt32[:, :], AF.Copy, bias=0.0, scale=pwork[:, 3:4]),
                        waits=[(sV, vq)], inc=sA)
                    an = K.c(sA)
                    K.emit("vector", lambda eng: eng.tensor_copy(xq32t[:, :], xqi16t[:, :]),
                           waits=[(sA, an)], inc=sV)
                    vc = K.c(sV)
                    for cc in range(NCH):
                        K.emit("tensor", lambda eng, c=cc: eng.transpose(
                            pt32[:, c * 128:(c + 1) * 128], xq32t[:, c * 128:(c + 1) * 128],
                            id32sb[:, :]), waits=[(sV, vc)], inc=sT)
                    tb = K.c(sT)
                    for cc in range(NCH):
                        K.emit("vector", lambda eng, c=cc, t=tc: eng.tensor_copy(
                            xn16[:, c * TCORE + t * 128: c * TCORE + t * 128 + 128],
                            pt32[:, c * 128:(c + 1) * 128]),
                            waits=[(sT, tb)], inc=sV)
                    pt_free = K.c(sV)
                K.barrier()
                K.flush(blk)

        RG = [[0, 1, 2, 3], [4, 5, 6, 7]]

        def tern_resident(mi, wsrc, wdst, f32buf, pqi16):
            acts = {}
            dves = {}
            for j in range(NCH):
                b = j % 2
                ldn = DMA("gpsimd", f32buf[:, 0:2048],
                          wsrc[j * 128:(j + 1) * 128, :],
                          waits=[(sA, acts.get(j - 1, 0))], sem="tn%d" % b)
                K.emit("scalar", lambda eng, m=mi, bb=b: eng.activation(
                    pqi16[:, bb * 2048:(bb + 1) * 2048], f32buf[:, 0:2048],
                    AF.Copy, bias=0.0, scale=swb[:, 2 * m + 1:2 * m + 2]),
                    waits=[("tn%d" % b, ldn), (sV, dves.get(j - 2, 0))], inc=sA)
                acts[j] = K.c(sA)
                TS("vector", wdst[:, j * 2048:(j + 1) * 2048], pqi16[:, b * 2048:(b + 1) * 2048],
                   -1.0, ALU.max, 1.0, ALU.min, waits=[(sA, acts[j])], inc=sV)
                dves[j] = K.c(sV)
            return dves

        def tern_jobs(mi, wsrc, hdst, f32buf, pqi16hi, f16buf, w0=None):
            st = {"acts": {}, "dves": {}, "outs": {}}

            def mk(j):
                def go():
                    b = j % 2
                    ldn = DMA("gpsimd", f32buf[:, 0:2048],
                              wsrc[j * 128:(j + 1) * 128, :],
                              waits=[(sA, st["acts"].get(j - 1, 0))] + ([w0] if w0 and j < 1 else []),
                              sem="tn%d" % b)
                    K.emit("scalar", lambda eng, m=mi, bb=b: eng.activation(
                        pqi16hi[:, :], f32buf[:, 0:2048],
                        AF.Copy, bias=0.0, scale=swb[:, 2 * m + 1:2 * m + 2]),
                        waits=[("tn%d" % b, ldn), (sV, st["dves"].get(j - 1, 0))], inc=sA)
                    st["acts"][j] = K.c(sA)
                    TS("vector", f16buf[:, 0:2048], pqi16hi[:, :],
                       -1.0, ALU.max, 1.0, ALU.min,
                       waits=[(sA, st["acts"][j]), ("th0", st["outs"].get(j - 1, 0))], inc=sV)
                    st["dves"][j] = K.c(sV)
                    st["outs"][j] = DMA("sync", hdst[j * 128:(j + 1) * 128, :],
                                        f16buf[:, 0:2048],
                                        waits=[(sV, st["dves"][j])], sem="th0")
                return go
            return [mk(j) for j in range(NCH)]

        def proj_quant_evac(tc, pgv, pqi16, mi, dq_extra, out_cb, tn=None):
            if tn is None:
                tn = K.c(sT)
            for seg in range(H):
                K.emit("vector", lambda eng, s=seg: eng.reduce_max(
                    pwork[:, s:s + 1], pgv[:, s * 128:(s + 1) * 128], axis=AX,
                    apply_absolute_value=True), waits=[(sT, tn)], inc=sV)
            K.emit("vector", lambda eng, m=mi, t=tc: eng.tensor_mul(
                pwork[:, 16:17], swb[:, 2 * m:2 * m + 1], axinv[:, t:t + 1]), waits=VW(), inc=sV)
            TS("vector", pwork[:, 17:33], pwork[:, 0:16], pwork[:, 16:17], ALU.mult,
               waits=VW(), inc=sV)
            TS("vector", pwork[:, 33:49], pwork[:, 17:33], EPS, ALU.max, waits=VW(), inc=sV)
            K.emit("vector", lambda eng: eng.reciprocal(pwork[:, 49:65], pwork[:, 33:49]),
                   waits=VW(), inc=sV)
            TS("vector", pwork[:, 65:81], pwork[:, 49:65], pwork[:, 16:17], ALU.mult,
               s2=127.0, op1=ALU.mult, waits=VW(), inc=sV)
            TS("vector", pwork[:, 81:97], pwork[:, 33:49], dq_extra / 127.0, ALU.mult,
               waits=VW(), inc=sV)
            vq = K.c(sV)
            aprev = 0
            for seg in range(H):
                K.emit("scalar", lambda eng, s=seg: eng.activation(
                    pqi16[:, s * 128:(s + 1) * 128], pgv[:, s * 128:(s + 1) * 128],
                    AF.Copy, bias=0.0, scale=pwork[:, 65 + s:66 + s]),
                    waits=[(sV, vq)], inc=sA)
            an = K.c(sA)
            for seg in range(H):
                out_cb(seg, pqi16[:, seg * 128:(seg + 1) * 128], pwork[:, 81 + seg:82 + seg], an, pgv)
            return an

        def proj_phase(mi, rhs_tiles, pg, pqi16, dq_extra, out_cb, side_jobs=None,
                       evac_fn=None):
            if evac_fn is None:
                evac_fn = proj_quant_evac
            """TM projection: G[tc] = xq_blocks.T @ rhs; quant; out_cb per seg.
            rhs_tiles: fn(j) -> (ap [128, 2048], wait_tuple or None)."""
            pg_free = [0, 0]
            prev = None
            for tc in range(8):
                par = tc % 2
                for j in range(NCH):
                    rap, rw = rhs_tiles(j)
                    w = []
                    if j == 0:
                        w.append((sV, pg_free[par]))
                    if rw is not None:
                        w.append(rw)
                    for isl in range(4):
                        K.emit("tensor", lambda eng, jj=j, t=tc, p=par, q=isl, r=rap: eng.matmul(
                            pg[:, p * 2048 + q * 512: p * 2048 + (q + 1) * 512],
                            xn16[:, jj * TCORE + t * 128: jj * TCORE + t * 128 + 128],
                            r[:, q * 512:(q + 1) * 512],
                            start=(jj == 0), stop=(jj == NCH - 1)),
                            waits=(w if isl == 0 else []), inc=sT)
                tn_now = K.c(sT)
                if prev is not None:
                    evac_fn(prev, pg[:, (prev % 2) * 2048:(prev % 2 + 1) * 2048],
                            pqi16, mi, dq_extra, out_cb, tn=tn_prev)
                    pg_free[prev % 2] = K.c(sV)
                    if side_jobs:
                        for _ in range(2):
                            if side_jobs:
                                side_jobs.pop(0)()
                prev = tc
                tn_prev = tn_now
            evac_fn(prev, pg[:, (prev % 2) * 2048:(prev % 2 + 1) * 2048],
                    pqi16, mi, dq_extra, out_cb, tn=tn_prev)
            pg_free[prev % 2] = K.c(sV)
            while side_jobs:
                side_jobs.pop(0)()

        def p4_projections():
            kasm = SB("kasm", [128, 2048], F16)
            with (nc.sbuf_tensor(NM("wres"), [128, NCH * 2048], F16) as wres,
                  nc.sbuf_tensor(NM("f32buf"), [128, 2048], F32) as f32buf,
                  nc.sbuf_tensor(NM("pqi16"), [128, 4096], I16) as pqi16,
                  nc.sbuf_tensor(NM("sc32"), [128, 2048], F32) as sc32,
                  nc.sbuf_tensor(NM("f16b"), [128, 2048], F16) as f16b,
                  nc.psum_tensor(NM("pg"), [128, 4096], F32) as pg,
                  nc.Block() as blk):
                tden = tern_resident(1, wkt, wres, f32buf, pqi16)
                tjobs = tern_jobs(0, wqt, wqq, f32buf, pqi16[:, 2048:4096], f16b,
                                  w0=(sV, tden[NCH - 1]))
                ck = {"n": 0}

                def k_evac(tc, pgv, pqi16_, mi, dq_extra, out_cb, tn=None):
                    if tn is None:
                        tn = K.c(sT)
                    for i in range(16):
                        K.emit("vector", lambda eng, ii=i, pp=pgv: eng.reduce_max(
                            pwork[:, ii:ii + 1], pp[:, ii * 128:(ii + 1) * 128],
                            axis=AX, apply_absolute_value=True),
                            waits=([(sT, tn)] if i == 0 else []), inc=sV)
                    K.emit("vector", lambda eng, m=mi, t=tc: eng.tensor_mul(
                        pwork[:, 16:17], swb[:, 2 * m:2 * m + 1], axinv[:, t:t + 1]), inc=sV)
                    TS("vector", pwork[:, 17:33], pwork[:, 0:16], pwork[:, 16:17],
                       ALU.mult, waits=VW(), inc=sV)
                    TS("vector", pwork[:, 33:49], pwork[:, 17:33], EPS, ALU.max, waits=VW(), inc=sV)
                    K.emit("vector", lambda eng: eng.reciprocal(pwork[:, 49:65], pwork[:, 33:49]),
                           waits=VW(), inc=sV)
                    TS("vector", pwork[:, 65:81], pwork[:, 49:65], pwork[:, 16:17],
                       ALU.mult, s2=127.0, op1=ALU.mult, waits=VW(), inc=sV)
                    vq = K.c(sV)
                    TS("vector", pwork[:, 81:97], pwork[:, 33:49], dq_extra / 127.0, ALU.mult,
                       waits=VW(), inc=sV)
                    for i in range(16):
                        K.emit("scalar", lambda eng, ii=i, pp=pgv: eng.activation(
                            pqi16_[:, ii * 128:(ii + 1) * 128], pp[:, ii * 128:(ii + 1) * 128],
                            AF.Copy, bias=0.0, scale=pwork[:, 65 + ii:66 + ii]),
                            waits=([(sV, vq)] if i == 0 else []), inc=sA)
                    an = K.c(sA)
                    abase = an - 16
                    for i in range(16):
                        K.emit("vector", lambda eng, ii=i: eng.tensor_scalar(
                            sc32[:, ii * 128:(ii + 1) * 128], pqi16_[:, ii * 128:(ii + 1) * 128],
                            pwork[:, 81 + ii:82 + ii], None, ALU.mult),
                            waits=[(sA, abase + i + 1)] + (VW() if i == 0 else []), inc=sV)
                    vbase = K.c(sV) - 16
                    tbase = K.c(sT)
                    for b4 in range(4):
                        for g in range(4):
                            i = b4 * 4 + g
                            K.emit("tensor", lambda eng, ii=i, pp=pgv: eng.transpose(
                                pp[:, ii * 128:(ii + 1) * 128],
                                sc32[:, ii * 128:(ii + 1) * 128], id32sb[:, :]),
                                waits=[(sV, vbase + i + 1), (sA, abase + b4 * 4 + 4)], inc=sT)
                        for g in range(4):
                            i = b4 * 4 + g
                            K.emit("vector", lambda eng, ii=i, pp=pgv: eng.tensor_copy(
                                kasm[:, ii * 128:(ii + 1) * 128], pp[:, ii * 128:(ii + 1) * 128]),
                                waits=[(sT, tbase + b4 * 4 + 4)]
                                + ([("ck", ck["n"])] if i == 0 else []), inc=sV)
                    vn = K.c(sV)
                    ck["n"] = DMA("sync",
                                  bass.AP(cck_in, tc * 128, [[1024, 128], [1024 * 128, 16], [1, 128]]),
                                  kasm[:, :].rearrange("p (ic d) -> p ic d", d=128),
                                  waits=[(sV, vn)], sem="ck")

                proj_phase(1, lambda j: (wres[:, j * 2048:(j + 1) * 2048],
                                         (sV, tden[j])), pg, pqi16, 1.0, None,
                           side_jobs=tjobs, evac_fn=k_evac)
                K.emit("gpsimd", lambda eng: eng.collective_compute(
                    "AllGather", ALU.bypass, replica_groups=RG,
                    ins=[cck_in.ap().opt()], outs=[cck_out.ap().opt()]),
                    waits=[("ck", ck["n"])], inc=sCC)
                K.barrier()
                K.flush(blk)
            with (nc.sbuf_tensor(NM("wres"), [128, NCH * 2048], F16) as wres,
                  nc.sbuf_tensor(NM("f32buf"), [128, 2048], F32) as f32buf,
                  nc.sbuf_tensor(NM("pqi16"), [128, 4096], I16) as pqi16,
                  nc.sbuf_tensor(NM("vst16"), [128, 2048], F16) as vst16,
                  nc.sbuf_tensor(NM("f16b"), [128, 2048], F16) as f16b,
                  nc.psum_tensor(NM("pg"), [128, 4096], F32) as pg,
                  nc.Block() as blk):
                tden = tern_resident(2, wvt, wres, f32buf, pqi16)
                tjobs = tern_jobs(3, wpt, wpq, f32buf, pqi16[:, 2048:4096], f16b,
                                  w0=(sV, tden[NCH - 1]))
                cv = {"n": 0}
                tc_idx = {"tc": 0}

                def v_out(seg, i16sl, dqcol, an, pgv):
                    K.emit("vector", lambda eng, s=i16sl, d=dqcol, g=seg: eng.tensor_scalar(
                        vst16[:, g * 128:(g + 1) * 128], s, d, None, ALU.mult),
                        waits=[(sA, an)] + (VW() + [("cv", cv["n"])] if seg == 0 else []),
                        inc=sV)
                    if seg == H - 1:
                        tc = tc_idx["tc"]
                        vn = K.c(sV)
                        cv["n"] = DMA("sync", ccv_in.ap()[tc * 128:(tc + 1) * 128, :],
                                      vst16[:, :], waits=[(sV, vn)], sem="cv")
                        tc_idx["tc"] = tc + 1

                proj_phase(2, lambda j: (wres[:, j * 2048:(j + 1) * 2048],
                                         (sV, tden[j])), pg, pqi16, 1.0, v_out,
                           side_jobs=tjobs)
                K.emit("gpsimd", lambda eng: eng.collective_compute(
                    "AllGather", ALU.bypass, replica_groups=RG,
                    ins=[ccv_in.ap().opt()], outs=[ccv_out.ap().opt()]),
                    waits=[("cv", cv["n"])], inc=sCC)
                K.barrier()
                K.flush(blk)

        def p4_qproj(qst):
            with (nc.sbuf_tensor(NM("wqbuf"), [128, 2 * 512], F16) as wqbuf,
                  nc.sbuf_tensor(NM("pqi16"), [128, 2048], I16) as pqi16,
                  nc.sbuf_tensor(NM("sc32"), [128, 2048], F32) as sc32,
                  nc.psum_tensor(NM("pg"), [128, 4096], F32) as pg,
                  nc.Block() as blk):
                wq_free = [0, 0]
                wq_n = [0, 0]
                pg_free = [0, 0]

                def qevac(q, hf, tn):
                    for i in range(16):
                        tci, sgi = i // 4, i % 4
                        K.emit("vector", lambda eng, ii=i, b=hf * 4 + i // 4, g=i % 4: eng.reduce_max(
                            pwork[:, ii:ii + 1], pg[:, b * 512 + g * 128: b * 512 + (g + 1) * 128],
                            axis=AX, apply_absolute_value=True),
                            waits=([(sT, tn)] if i == 0 else []), inc=sV)
                    for tci in range(4):
                        K.emit("vector", lambda eng, t=hf * 4 + tci, c=tci: eng.tensor_mul(
                            pwork[:, 16 + c:17 + c], swb[:, 0:1], axinv[:, t:t + 1]), inc=sV)
                    for tci in range(4):
                        TS("vector", pwork[:, 20 + tci * 4:24 + tci * 4],
                           pwork[:, tci * 4:tci * 4 + 4], pwork[:, 16 + tci:17 + tci],
                           ALU.mult, waits=(VW() if tci == 0 else []), inc=sV)
                    TS("vector", pwork[:, 36:52], pwork[:, 20:36], EPS, ALU.max, waits=VW(), inc=sV)
                    K.emit("vector", lambda eng: eng.reciprocal(pwork[:, 52:68], pwork[:, 36:52]),
                           waits=VW(), inc=sV)
                    for tci in range(4):
                        TS("vector", pwork[:, 68 + tci * 4:72 + tci * 4],
                           pwork[:, 52 + tci * 4:56 + tci * 4], pwork[:, 16 + tci:17 + tci],
                           ALU.mult, s2=127.0, op1=ALU.mult,
                           waits=(VW() if tci == 0 else []), inc=sV)
                    vq = K.c(sV)
                    TS("vector", pwork[:, 84:100], pwork[:, 36:52], ISQD / 127.0, ALU.mult,
                       waits=VW(), inc=sV)
                    for i in range(16):
                        K.emit("scalar", lambda eng, ii=i, b=hf * 4 + i // 4, g=i % 4: eng.activation(
                            pqi16[:, ii * 128:(ii + 1) * 128],
                            pg[:, b * 512 + g * 128: b * 512 + (g + 1) * 128],
                            AF.Copy, bias=0.0, scale=pwork[:, 68 + ii:69 + ii]),
                            waits=([(sV, vq)] if i == 0 else []), inc=sA)
                    an = K.c(sA)
                    abase = an - 16
                    for i in range(16):
                        K.emit("vector", lambda eng, ii=i: eng.tensor_scalar(
                            sc32[:, ii * 128:(ii + 1) * 128], pqi16[:, ii * 128:(ii + 1) * 128],
                            pwork[:, 84 + ii:85 + ii], None, ALU.mult),
                            waits=[(sA, abase + i + 1)] + (VW() if i == 0 else []), inc=sV)
                    vn = K.c(sV)
                    vbase = vn - 16
                    tbase = K.c(sT)
                    for b4 in range(4):
                        for g in range(4):
                            i = b4 * 4 + g
                            K.emit("tensor", lambda eng, ii=i, b=hf * 4 + b4, gg=g: eng.transpose(
                                pg[:, b * 512 + gg * 128: b * 512 + (gg + 1) * 128],
                                sc32[:, ii * 128:(ii + 1) * 128], id32sb[:, :]),
                                waits=[(sV, vbase + i + 1), (sA, abase + b4 * 4 + 4)], inc=sT)
                        for g in range(4):
                            i = b4 * 4 + g
                            K.emit("vector", lambda eng, b=hf * 4 + b4,
                                   hd=(q * 4 + g), t=hf * 4 + b4, gg=g: eng.tensor_copy(
                                qst[:, hd * TCORE + t * 128: hd * TCORE + t * 128 + 128],
                                pg[:, b * 512 + gg * 128: b * 512 + (gg + 1) * 128]),
                                waits=[(sT, tbase + b4 * 4 + 4)], inc=sV)

                prev = None
                tn_prev = 0
                for qq in range(4):
                    for hf in range(2):
                        for j in range(NCH):
                            bpar = j % 2
                            wq_n[bpar] = DMA("gpsimd", wqbuf[:, bpar * 512:(bpar + 1) * 512],
                                             wqq[j * 128:(j + 1) * 128, qq * 512:(qq + 1) * 512],
                                             waits=[(sT, wq_free[bpar])], sem="q%d" % bpar)
                            for tci in range(4):
                                tc = hf * 4 + tci
                                K.emit("tensor", lambda eng, jj=j, t=tc, b=hf * 4 + tci, bp=bpar: eng.matmul(
                                    pg[:, b * 512:(b + 1) * 512],
                                    xn16[:, jj * TCORE + t * 128: jj * TCORE + t * 128 + 128],
                                    wqbuf[:, bp * 512:(bp + 1) * 512],
                                    start=(jj == 0), stop=(jj == NCH - 1)),
                                    waits=([(sV, pg_free[hf]), ("q%d" % bpar, wq_n[bpar])]
                                           if tci == 0 else []), inc=sT)
                            wq_free[bpar] = K.c(sT)
                        tn_now = K.c(sT)
                        if prev is not None:
                            qevac(prev[0], prev[1], tn_prev)
                            pg_free[prev[1]] = K.c(sV)
                        prev = (qq, hf)
                        tn_prev = tn_now
                qevac(prev[0], prev[1], tn_prev)
                pg_free[prev[1]] = K.c(sV)
                K.barrier()
                K.flush(blk)

        def p5_attention(qst):
            with (nc.sbuf_tensor(NM("kbuf"), [128, 2 * 4096], F16) as kbuf,
                  nc.sbuf_tensor(NM("vbuf"), [128, 2 * 4096], F16) as vbuf,
                  nc.sbuf_tensor(NM("scsc"), [128, 4096], F32) as scsc,
                  nc.sbuf_tensor(NM("wi16"), [128, 1024], I16) as wi16,
                  nc.sbuf_tensor(NM("w16"), [128, 6 * 512], F16) as w16,
                  nc.sbuf_tensor(NM("wyi16"), [128, 512], I16) as wyi16,
                  nc.sbuf_tensor(NM("mx32"), [128, 32], F32) as mx32,
                  nc.sbuf_tensor(NM("uq32"), [128, 32], F32) as uq32,
                  nc.sbuf_tensor(NM("cwk"), [128, 48], F32) as cwk,
                  nc.sbuf_tensor(NM("mt16"), [128, 16], F16) as mt16,
                  nc.sbuf_tensor(NM("mrow2"), [128, 1024], F16) as mrow2,
                  nc.psum_tensor(NM("psA"), [128, 2048], F32) as psA,
                  nc.psum_tensor(NM("psB"), [128, 1024], F32) as psB,
                  nc.psum_tensor(NM("psY"), [128, 512], F32) as psY,
                  nc.psum_tensor(NM("pstr"), [128, 1024], F16) as pstr,
                  nc.Block() as blk):
                kv_free = [0, 0]
                kv_ld = [0, 0]
                psA_free = [0, 0]
                scq_free = [0, 0, 0, 0]
                psB_free = [0, 0]
                wi_free = [0, 0]
                w16_free = [0, 0, 0, 0, 0, 0]
                psY_free = [0]
                yq_done = [0, 0]
                pstr_free = [0, 0]
                ytr_free = 0
                qctr = [0]
                SEQ = [(h, g) for h in range(H) for g in range(2)]

                def load_kv(h):
                    par = h % 2
                    w = [(sT, kv_free[par]), (sCC, 1)]
                    for r in range(4):
                        DMA("sync", kbuf[:, par * 4096 + r * 1024: par * 4096 + (r + 1) * 1024],
                            bass.AP(cck_out, (r * 2048 + h * 128) * TCORE, [[1024, 128], [1, 1024]]),
                            waits=w, sem="kv%d" % par)
                        w = []
                    wv = [(sCC, 2)]
                    for r in range(4):
                        kv_ld[par] = DMA(
                            "sync",
                            vbuf[:, par * 4096 + r * 1024: par * 4096 + (r + 1) * 1024]
                            .rearrange("p (lc d) -> p lc d", d=128),
                            bass.AP(ccv_out, (r * 1024) * 2048 + h * 128,
                                    [[2048, 128], [128 * 2048, 8], [1, 128]]),
                            waits=wv, sem="kv%d" % par)
                        wv = []

                def make_A_jobs(i):
                    h, g = SEQ[i]
                    par = h % 2
                    ip = i % 2
                    jobs = []
                    for tci in range(4):
                        tc = g * 4 + tci
                        lhs = qst[:, h * TCORE + tc * 128: h * TCORE + tc * 128 + 128]
                        for qd in range(4):
                            def job(l=lhs, p=par, q=qd, col=(ip * 16 + tci * 4 + qd)):
                                slot = qctr[0] % 2
                                qb = qctr[0] % 4
                                qctr[0] += 1
                                for half in range(2):
                                    K.emit("tensor", lambda eng, hf=half, s=slot, l2=l, p2=p, q2=q: eng.matmul(
                                        psA[:, s * 1024 + hf * 512: s * 1024 + (hf + 1) * 512],
                                        l2, kbuf[:, p2 * 4096 + q2 * 1024 + hf * 512: p2 * 4096 + q2 * 1024 + (hf + 1) * 512],
                                        start=True, stop=True),
                                        waits=([(sA, psA_free[slot]),
                                                ("kv%d" % p, kv_ld[p])] if half == 0 else []),
                                        inc=sT)
                                tmm = K.c(sT)
                                K.emit("scalar", lambda eng, qq=qb, s=slot, c2=col: eng.activation(
                                    scsc[:, qq * 1024:(qq + 1) * 1024],
                                    psA[:, s * 1024:(s + 1) * 1024], AF.Exp,
                                    accum_out=uq32[:, c2:c2 + 1]),
                                    waits=[(sT, tmm), (sV, scq_free[qb])], inc=sA)
                                psA_free[slot] = K.c(sA)
                                K.emit("vector", lambda eng, qq=qb, c2=col: eng.reduce_max(
                                    mx32[:, c2:c2 + 1], scsc[:, qq * 1024:(qq + 1) * 1024], axis=AX),
                                    waits=[(sA, psA_free[slot])], inc=sV)
                                scq_free[qb] = K.c(sV)
                            jobs.append(job)
                    return jobs

                def do_A(i):
                    for jb in make_A_jobs(i):
                        jb()

                def do_Acomb(i):
                    h, g = SEQ[i]
                    ip = i % 2
                    b = ip * 16
                    gi = h * 8 + g * 4
                    cb = ip * 24
                    K.emit("vector", lambda eng: eng.tensor_max(
                        cwk[:, cb:cb + 8], mx32[:, b:b + 16:2], mx32[:, b + 1:b + 16:2]),
                        waits=VW(), inc=sV)
                    K.emit("vector", lambda eng: eng.tensor_max(
                        cwk[:, cb + 8:cb + 12], cwk[:, cb:cb + 8:2], cwk[:, cb + 1:cb + 8:2]),
                        waits=VW(), inc=sV)
                    K.emit("vector", lambda eng: eng.tensor_add(
                        cwk[:, cb:cb + 8], uq32[:, b:b + 16:2], uq32[:, b + 1:b + 16:2]),
                        waits=VW(), inc=sV)
                    K.emit("vector", lambda eng: eng.tensor_add(
                        cwk[:, cb + 12:cb + 16], cwk[:, cb:cb + 8:2], cwk[:, cb + 1:cb + 8:2]),
                        waits=VW(), inc=sV)
                    K.emit("vector", lambda eng: eng.reciprocal(cwk[:, cb + 16:cb + 20], cwk[:, cb + 8:cb + 12]),
                           waits=VW(), inc=sV)
                    K.emit("vector", lambda eng, z=gi: eng.tensor_mul(
                        zbuf[:, z:z + 4], cwk[:, cb + 12:cb + 16], cwk[:, cb + 16:cb + 20]),
                        waits=VW(), inc=sV)
                    TS("vector", cwk[:, cb:cb + 4], zbuf[:, gi:gi + 4], 127.0, ALU.mult,
                       waits=VW(), inc=sV)
                    K.emit("vector", lambda eng, z=gi: eng.reciprocal(
                        z127i[:, z:z + 4], cwk[:, cb:cb + 4]), waits=VW(), inc=sV)
                    vz = K.c(sV)
                    K.emit("scalar", lambda eng: eng.activation(
                        cwk[:, cb + 4:cb + 8], cwk[:, cb + 8:cb + 12], AF.Ln),
                        waits=[(sV, vz)], inc=sA)
                    am = K.c(sA)
                    TS("vector", cwk[:, cb + 20:cb + 24], cwk[:, cb + 4:cb + 8], -1.0, ALU.mult,
                       waits=[(sA, am)], inc=sV)
                    mb = ip * 8
                    K.emit("vector", lambda eng: eng.tensor_copy(
                        mt16[:, mb:mb + 8:2], cwk[:, cb + 20:cb + 24]), waits=VW(), inc=sV)
                    K.emit("vector", lambda eng: eng.tensor_copy(
                        cwk[:, cb:cb + 4], mt16[:, mb:mb + 8:2]), waits=VW(), inc=sV)
                    K.emit("vector", lambda eng: eng.tensor_sub(
                        cwk[:, cb + 4:cb + 8], cwk[:, cb + 20:cb + 24], cwk[:, cb:cb + 4]),
                        waits=VW(), inc=sV)
                    K.emit("vector", lambda eng: eng.tensor_copy(
                        mt16[:, mb + 1:mb + 8:2], cwk[:, cb + 4:cb + 8]), waits=VW(), inc=sV)
                    return K.c(sV)

                def do_mrow(i, vm):
                    ip = i % 2
                    mb = ip * 8
                    for t in range(4):
                        K.emit("tensor", lambda eng, tt=t: eng.transpose(
                            pstr[0:2, ip * 512 + tt * 128: ip * 512 + (tt + 1) * 128],
                            mt16[:, mb + 2 * tt: mb + 2 * tt + 2], id16sb[:, :]),
                            waits=([(sV, vm), (sA, pstr_free[ip])] if t == 0 else []), inc=sT)
                    tmr = K.c(sT)
                    K.emit("scalar", lambda eng: eng.copy(
                        mrow2[0:2, ip * 512:(ip + 1) * 512], pstr[0:2, ip * 512:(ip + 1) * 512]),
                        waits=[(sT, tmr)], inc=sA)
                    pstr_free[ip] = K.c(sA)

                def make_y_jobs(i, tpv):
                    h, g = SEQ[i]
                    gi = h * 8 + g * 4
                    done = [0, 0]

                    def j0():
                        K.emit("scalar", lambda eng: eng.copy(ysc[:, :], psY[:, :]),
                               waits=[(sT, tpv)], inc=sA)
                        psY_free[0] = K.c(sA)
                        for j in range(4):
                            K.emit("tensor", lambda eng, q=j: eng.transpose(
                                psY[:, q * 128:(q + 1) * 128], ysc[:, q * 128:(q + 1) * 128],
                                id32sb[:, :]),
                                waits=([(sA, psY_free[0])] if j == 0 else []), inc=sT)

                    def j1():
                        ty = K.c(sT)
                        for j in range(4):
                            K.emit("vector", lambda eng, q=j: eng.reduce_max(
                                ywork[:, q:q + 1], psY[:, q * 128:(q + 1) * 128], axis=AX,
                                apply_absolute_value=True),
                                waits=([(sT, ty)] if j == 0 else []), inc=sV)
                        zsl = z127i[:, gi:gi + 4]
                        K.emit("vector", lambda eng, z=zsl: eng.tensor_mul(
                            ywork[:, 4:8], ywork[:, 0:4], z), waits=VW(), inc=sV)
                        TS("vector", ywork[:, 8:12], ywork[:, 4:8], EPS, ALU.max,
                           waits=VW(), inc=sV)
                        K.emit("vector", lambda eng: eng.reciprocal(ywork[:, 12:16], ywork[:, 8:12]),
                               waits=VW(), inc=sV)
                        K.emit("vector", lambda eng, z=zsl: eng.tensor_mul(
                            ywork[:, 16:20], ywork[:, 12:16], z), waits=VW(), inc=sV)
                        TS("vector", ywork[:, 20:24], ywork[:, 16:20], 127.0, ALU.mult,
                           waits=VW(), inc=sV)
                        TS("vector", ywork[:, 24:28], ywork[:, 8:12], 1.0 / 127.0, ALU.mult,
                           waits=VW(), inc=sV)

                    def j2():
                        vy = K.c(sV)
                        for j in range(4):
                            K.emit("scalar", lambda eng, q=j: eng.activation(
                                wyi16[:, q * 128:(q + 1) * 128], psY[:, q * 128:(q + 1) * 128],
                                AF.Copy, bias=0.0, scale=ywork[:, 20 + q:21 + q]),
                                waits=([(sV, vy)] if j == 0 else []), inc=sA)
                        ay = K.c(sA)
                        for j in range(4):
                            K.emit("vector", lambda eng, q=j: eng.tensor_scalar(
                                ysc[:, q * 128:(q + 1) * 128], wyi16[:, q * 128:(q + 1) * 128],
                                ywork[:, 24 + q:25 + q], None, ALU.mult),
                                waits=([(sA, ay)] + VW() if j == 0 else []), inc=sV)

                    def j3():
                        vy2 = K.c(sV)
                        ay = K.c(sA)
                        for j in range(4):
                            K.emit("tensor", lambda eng, q=j: eng.transpose(
                                psY[:, q * 128: (q + 1) * 128],
                                ysc[:, q * 128:(q + 1) * 128], id32sb[:, :]),
                                waits=([(sV, vy2), (sA, ay)] if j == 0 else []), inc=sT)
                        done[0] = K.c(sT)

                    def j4():
                        ty2 = done[0]
                        for j in range(4):
                            K.emit("vector", lambda eng, q=j, hh=h, gg=g: eng.tensor_copy(
                                qst[:, hh * TCORE + gg * 512 + q * 128: hh * TCORE + gg * 512 + (q + 1) * 128],
                                psY[:, q * 128: (q + 1) * 128]),
                                waits=([(sT, ty2)] if j == 0 else []), inc=sV)
                        done[1] = K.c(sV)

                    return [j0, j1, j2, j3, j4], done

                PVLAG = 6

                def do_B(i, y_jobs, a_jobs, comb_job, yq):
                    h, g = SEQ[i]
                    par = h % 2
                    ip = i % 2
                    pv_pend = []

                    def emit_scores(sc):
                        pp = sc % 2
                        sl = sc % 6
                        K.emit("tensor", lambda eng, s=sc, p=par, q=pp, hh=h, gg=g: eng.matmul(
                            psB[:, q * 512:(q + 1) * 512],
                            kbuf[:, p * 4096 + s * 128: p * 4096 + s * 128 + 128],
                            qst[:, hh * TCORE + gg * 512: hh * TCORE + (gg + 1) * 512],
                            start=True, stop=False),
                            waits=[(sA, max(psB_free[pp], pstr_free[ip]))], inc=sT)
                        K.emit("tensor", lambda eng, q=pp, i2=ip: eng.matmul(
                            psB[:, q * 512:(q + 1) * 512], ones2sb[:, :],
                            mrow2[:, i2 * 512:(i2 + 1) * 512],
                            start=False, stop=True), inc=sT)
                        tmm = K.c(sT)
                        K.emit("scalar", lambda eng, q=pp: eng.activation(
                            wi16[:, q * 512:(q + 1) * 512], psB[:, q * 512:(q + 1) * 512],
                            AF.Exp, bias=ln127sb[:, 0:1], scale=1.0),
                            waits=[(sT, tmm), (sV, wi_free[pp])], inc=sA)
                        psB_free[pp] = K.c(sA)
                        K.emit("vector", lambda eng, q=pp, l=sl: eng.tensor_copy(
                            w16[:, l * 512:(l + 1) * 512], wi16[:, q * 512:(q + 1) * 512]),
                            waits=[(sA, psB_free[pp]), (sT, w16_free[sl])], inc=sV)
                        wi_free[pp] = K.c(sV)
                        pv_pend.append((sc, wi_free[pp]))

                    def emit_pv(sc, wn):
                        sl = sc % 6
                        K.emit("tensor", lambda eng, s=sc, p=par, l=sl: eng.matmul(
                            psY[:, :], vbuf[:, p * 4096 + s * 128: p * 4096 + s * 128 + 128],
                            w16[:, l * 512:(l + 1) * 512],
                            start=(s == 0), stop=(s == 31)),
                            waits=[(sV, wn)] + ([(sV, yq[1]), (sT, yq[0])] if sc == 0 else []),
                            inc=sT)
                        w16_free[sl] = K.c(sT)

                    if i == 0:
                        K.emit("tensor", lambda eng: eng.nop(), waits=[(sG, msn)], inc=sT)
                    emit_scores(0)
                    if y_jobs:
                        y_jobs.pop(0)()
                    for sc in range(1, 32):
                        emit_scores(sc)
                        if sc >= PVLAG:
                            emit_pv(*pv_pend[sc - PVLAG])
                        if y_jobs:
                            y_jobs.pop(0)()
                        if a_jobs:
                            a_jobs.pop(0)()
                        if sc == 20 and comb_job is not None:
                            comb_job()
                    for k in range(32 - PVLAG, 32):
                        emit_pv(*pv_pend[k])
                    if a_jobs:
                        while a_jobs:
                            a_jobs.pop(0)()
                    if comb_job is not None and not getattr(comb_job, "_done", False):
                        pass
                    return K.c(sT)

                K.emit("gpsimd", lambda eng: eng.memset(mrow2[:, :], 0.0), inc=sG)
                msn = K.c(sG)
                load_kv(0)
                load_kv(1)
                do_A(0)
                vm = do_Acomb(0)
                do_mrow(0, vm)
                y_jobs = None
                yq = [0, 0]
                for i in range(len(SEQ)):
                    a_jobs = make_A_jobs(i + 1) if i + 1 < len(SEQ) else None

                    def comb_job(ii=i):
                        vm2 = do_Acomb(ii + 1)
                        do_mrow(ii + 1, vm2)
                    cj = comb_job if i + 1 < len(SEQ) else None
                    tpv = do_B(i, y_jobs, a_jobs, cj, yq)
                    y_jobs, yq = make_y_jobs(i, tpv)
                    h, g = SEQ[i]
                    if g == 1:
                        kv_free[h % 2] = K.c(sT)
                        if h + 2 < H:
                            load_kv(h + 2)
                for jb in y_jobs:
                    jb()
                K.barrier()
                K.flush(blk)

        def p6_outproj(qst):
            with (nc.sbuf_tensor(NM("wpbuf"), [128, 2 * 2048], F16) as wpbuf,
                  nc.sbuf_tensor(NM("pqi16"), [128, 4096], I16) as pqi16,
                  nc.sbuf_tensor(NM("ao32"), [128, 2048], F32) as ao32,
                  nc.psum_tensor(NM("po"), [128, 4096], F32) as po,
                  nc.Block() as blk):
                wp_free = [0, 0]
                wp_n = [0, 0]
                po_free = [0, 0]

                def p6_quant(tc, tn):
                    par = tc % 2
                    pov = po[:, par * 2048:(par + 1) * 2048]
                    K.emit("vector", lambda eng, pp=pov: eng.reduce_max(
                        pwork[:, 0:1], pp, axis=AX, apply_absolute_value=True),
                        waits=[(sT, tn)], inc=sV)
                    K.emit("vector", lambda eng: eng.tensor_mul(pwork[:, 1:2], pwork[:, 0:1], swb[:, 6:7]),
                           waits=VW(), inc=sV)
                    TS("vector", pwork[:, 2:3], pwork[:, 1:2], EPS, ALU.max, waits=VW(), inc=sV)
                    K.emit("vector", lambda eng: eng.reciprocal(pwork[:, 3:4], pwork[:, 2:3]),
                           waits=VW(), inc=sV)
                    K.emit("vector", lambda eng: eng.tensor_mul(pwork[:, 4:5], pwork[:, 3:4], swb[:, 6:7]),
                           waits=VW(), inc=sV)
                    TS("vector", pwork[:, 6:7], pwork[:, 4:5], 127.0, ALU.mult, waits=VW(), inc=sV)
                    TS("vector", pwork[:, 5:6], pwork[:, 2:3], 1.0 / 127.0, ALU.mult, waits=VW(), inc=sV)
                    vq = K.c(sV)
                    K.emit("scalar", lambda eng, pp=pov: eng.activation(
                        pqi16[:, 0:2048], pp, AF.Copy, bias=0.0, scale=pwork[:, 6:7]),
                        waits=[(sV, vq)], inc=sA)
                    an = K.c(sA)
                    K.emit("vector", lambda eng: eng.tensor_scalar(
                        ao32[:, :], pqi16[:, 0:2048], pwork[:, 5:6], None, ALU.mult),
                        waits=[(sA, an)] + VW(), inc=sV)
                    vd = K.c(sV)
                    for cc in range(NCH):
                        K.emit("tensor", lambda eng, c=cc, p=par: eng.transpose(
                            po[:, p * 2048 + c * 128: p * 2048 + (c + 1) * 128],
                            ao32[:, c * 128:(c + 1) * 128], id32sb[:, :]),
                            waits=[(sV, vd)], inc=sT)
                    tb = K.c(sT)
                    for cc in range(NCH):
                        K.emit("vector", lambda eng, c=cc, t=tc, p=par: eng.tensor_add(
                            xres[:, c * TCORE + t * 128: c * TCORE + t * 128 + 128],
                            xres[:, c * TCORE + t * 128: c * TCORE + t * 128 + 128],
                            po[:, p * 2048 + c * 128: p * 2048 + (c + 1) * 128]),
                            waits=[(sT, tb)], inc=sV)
                    po_free[par] = K.c(sV)

                prev = None
                for tc in range(8):
                    par = tc % 2
                    for j in range(NCH):
                        bpar = j % 2
                        wp_n[bpar] = DMA("gpsimd", wpbuf[:, bpar * 2048:(bpar + 1) * 2048],
                                         wpq[j * 128:(j + 1) * 128, :],
                                         waits=[(sT, wp_free[bpar])], sem="p%d" % bpar)
                        for isl in range(4):
                            K.emit("tensor", lambda eng, jj=j, t=tc, p=par, q=isl, bp=bpar: eng.matmul(
                                po[:, p * 2048 + q * 512: p * 2048 + (q + 1) * 512],
                                qst[:, jj * TCORE + t * 128: jj * TCORE + t * 128 + 128],
                                wpbuf[:, bp * 2048 + q * 512: bp * 2048 + (q + 1) * 512],
                                start=(jj == 0), stop=(jj == NCH - 1)),
                                waits=([(sV, po_free[par]), ("p%d" % bpar, wp_n[bpar])]
                                       if isl == 0 else []), inc=sT)
                        wp_free[bpar] = K.c(sT)
                    tn_now = K.c(sT)
                    if prev is not None:
                        p6_quant(prev, tn_prev)
                    prev = tc
                    tn_prev = tn_now
                p6_quant(prev, tn_prev)
                K.barrier()
                K.flush(blk)

        p3_ln2_quant()
        p4_projections()
        qstack = contextlib.ExitStack()
        qst = qstack.enter_context(nc.sbuf_tensor(NM("qst"), [128, H * TCORE], F16))
        p4_qproj(qst)
        if DBG:
            with nc.Block() as blkd2:
                DMA("sync", dbg_q, qst[:, :], waits=[(sV, K.c(sV))])
                DMA("sync", dbg_sw, swb[:, :])
                DMA("sync", dbg_kg, cck_out.ap()[:, :], waits=[(sCC, 2)])
                K.flush(blkd2)
        p5_attention(qst)
        if DBG:
            with nc.Block() as blkd3:
                DMA("sync", dbg_y, qst[:, :], waits=[(sV, K.c(sV))])
                DMA("sync", dbg_z, zbuf[:, :])
                K.flush(blkd3)
        p6_outproj(qst)
        qstack.close()

    if stage >= 3:
        ln_phase(0)
        ffn_phase(with_absums=False)

    with nc.Block() as blk8:
        DMA("sync", yt.rearrange("(cc p) t -> p cc t", p=128),
            xres[:, :].rearrange("p (cc t) -> p cc t", t=TCORE),
            waits=[(sV, K.c(sV)), (sA, K.c(sA)), (sG, K.c(sG)), (sT, K.c(sT))])
        fin = K.c(sL)
        K.emit("sync", lambda eng: eng.nop(), waits=[(sL, fin)])
        K.flush(blk8)

    st.close()
    K._sem_stack.close()
    return nc


def build_attention(env):
    raise NotImplementedError


_BUILT = {}


def kernel(L0_attn_x, L0_attn_Wq, L0_attn_Wk, L0_attn_Wv, L0_attn_Wp,
           L0_ffn_W0, L0_ffn_bias0, L0_ffn_W1, L0_ffn_bias1):
    x = np.asarray(L0_attn_x, np.float32)
    consts = dict(
        w0t=np.ascontiguousarray(np.asarray(L0_ffn_W0, np.float32).T
                                 .reshape(16, 128, 32, 256).transpose(1, 2, 0, 3)
                                 .reshape(128, 131072)),
        w1t=np.ascontiguousarray(np.asarray(L0_ffn_W1, np.float32).T
                                 .reshape(2, 32, 128, 16, 128).transpose(2, 3, 0, 1, 4)
                                 .reshape(128, 131072)),
        b0m=np.ascontiguousarray(np.asarray(L0_ffn_bias0, np.float32).reshape(NNCH, 128).T),
        b1m=np.ascontiguousarray(np.asarray(L0_ffn_bias1, np.float32).reshape(NCH, 128).T),
        wqt=np.ascontiguousarray(np.asarray(L0_attn_Wq, np.float32).T),
        wkt=np.ascontiguousarray(np.asarray(L0_attn_Wk, np.float32).T),
        wvt=np.ascontiguousarray(np.asarray(L0_attn_Wv, np.float32).T),
        wpt=np.ascontiguousarray(np.asarray(L0_attn_Wp, np.float32).T),
        id32=np.eye(128, dtype=np.float32),
        id16=np.eye(128, dtype=np.float16),
        ones2=np.concatenate([np.ones((2, 128), np.float16),
                              np.zeros((126, 128), np.float16)], 0),
        ones1r=np.ones((1, 128), np.float16),
        onescol=np.ones((128, 1), np.float16),
        ones32r=np.ones((1, 128), np.float32),
        ones32c=np.ones((128, 1), np.float32),
        invccr=np.full((1, 128), 1.0 / (C * C), np.float32),
    )
    if STAGE not in _BUILT:
        _BUILT[STAGE] = build_nc(STAGE)
    ncp = _BUILT[STAGE]
    in_maps = []
    for c in range(8):
        b, t0 = c // 4, (c % 4) * TCORE
        m = dict(consts)
        m["xt"] = np.ascontiguousarray(x[b, t0:t0 + TCORE, :].T)
        in_maps.append(m)
    res = run_bass_kernel_spmd(ncp, in_maps, core_ids=list(range(8)))
    out = np.empty((2, 4096, C), np.float32)
    for c in range(8):
        b, t0 = c // 4, (c % 4) * TCORE
        out[b, t0:t0 + TCORE, :] = res.results[c]["yt"].T
    return out
